# version 02
# baseline (speedup 1.0000x reference)
import numpy as np
from contextlib import ExitStack

import concourse.bass as bass
import concourse.mybir as mybir
from concourse.bass_utils import run_bass_kernel_spmd

F32 = mybir.dt.float32
BF16 = mybir.dt.bfloat16
AF = mybir.ActivationFunctionType
ALU = mybir.AluOpType
AX = mybir.AxisListType

S = 2048
E = 2048
NCH = 16
HID = 8192
QSCALE = float(1.0 / np.sqrt(128.0))
LN_EPS = 1e-5


class Sem:
    __slots__ = ("h", "n")

    def __init__(self, nc, name=None):
        self.h = nc.alloc_semaphore(name=name)
        self.n = 0

    def inc(self, instr, k=1):
        instr.then_inc(self.h, k)
        self.n += k
        return self.n

    def wait(self, eng, tok=None):
        tok = self.n if tok is None else tok
        if tok > 0:
            eng.wait_ge(self.h, tok)


class Prog:
    def __init__(self):
        nc = bass.Bass("TRN2", target_bir_lowering=False)
        self.nc = nc
        self.t = nc.tensor
        self.v = nc.vector
        self.sc = nc.scalar
        self.g = nc.gpsimd
        self.sy = nc.sync
        self.ctx = ExitStack()
        self.sems = {}
        self.PS = self.sbuf_like = None
        self._arena = self.ctx.enter_context(nc.psum_tensor("arena", [128, 8 * 512], F32))

    def bank(self, i, cols=512):
        return self._arena[:, i * 512: i * 512 + cols]

    def bankb(self, i, cols=1024):
        return self._arena[:, i * 512:(i + 1) * 512].bitcast(BF16)[:, 0:cols]

    def bankrow(self, i, cols=512):
        return self._arena[0:1, i * 512: i * 512 + cols]

    def sem(self, name):
        if name not in self.sems:
            self.sems[name] = Sem(self.nc, name)
        return self.sems[name]

    def sbuf(self, name, shape, dtype, stack=None):
        return (stack or self.ctx).enter_context(self.nc.sbuf_tensor(name, shape, dtype))

    def barrier(self):
        self.nc.all_engine_barrier()

    def setup_consts(self):
        p = self
        self.identf = p.sbuf("identf", [128, 128], F32)
        self.identb = p.sbuf("identb", [128, 128], BF16)
        self.mask01 = p.sbuf("mask01", [128, 128], BF16)
        self.ones_col_b = p.sbuf("ones_col_b", [128, 1], BF16)
        self.ones_row_f = p.sbuf("ones_row_f", [1, 128], F32)
        sC = p.sem("sConst")
        g = p.g
        g.memset(self.identf[:], 0.0)
        g.affine_select(out=self.identf[:], in_=self.identf[:],
                        compare_op=ALU.not_equal, fill=1.0, base=0,
                        pattern=[[-1, 128]], channel_multiplier=1)
        g.memset(self.identb[:], 0.0)
        g.affine_select(out=self.identb[:], in_=self.identb[:],
                        compare_op=ALU.not_equal, fill=1.0, base=0,
                        pattern=[[-1, 128]], channel_multiplier=1)
        g.memset(self.mask01[:], 1.0)
        g.affine_select(out=self.mask01[:], in_=self.mask01[:],
                        compare_op=ALU.is_ge, fill=0.0, base=0,
                        pattern=[[-1, 128]], channel_multiplier=1)
        g.memset(self.ones_col_b[:], 1.0)
        sC.inc(g.memset(self.ones_row_f[:], 1.0))
        for eng in (p.t, p.v, p.sc):
            sC.wait(eng)

    def finish(self):
        self.ctx.close()
        return self.nc



def emit_load_x(p, x_dram, x_fm):
    st_ = ExitStack()
    stage = p.sbuf("ld_stage", [128, E], F32, stack=st_)
    sDMA = p.sem("sLdX")
    sPE = p.sem("sLdPE")
    sDVE = p.sem("sLdDVE")
    ctok = [0, 0]
    for tc in range(S // 128):
        tok = sDMA.inc(p.sy.dma_start(out=stage[:], in_=x_dram[tc * 128:(tc + 1) * 128, :]), 16)
        sDMA.wait(p.t, tok)
        for c in range(NCH):
            par = c % 2
            sDVE.wait(p.t, ctok[par])
            tp = sPE.inc(p.t.transpose(p.bank(par, 128), stage[:, c * 128:(c + 1) * 128], p.identf[:]))
            sPE.wait(p.v, tp)
            ctok[par] = sDVE.inc(p.v.tensor_copy(
                x_fm[:, c * S + tc * 128: c * S + tc * 128 + 128], p.bank(par, 128)))
        sPE.wait(p.sy)
    p.barrier()
    st_.close()


def emit_store_x(p, x_fm, y_dram):
    stage = p.sbuf("st_stage", [128, E], F32)
    sPE = p.sem("sStPE")
    sDVE = p.sem("sStDVE")
    sST = p.sem("sStDMA")
    ctok = [0, 0]
    for tc in range(S // 128):
        sST.wait(p.v)
        for c in range(NCH):
            par = c % 2
            sDVE.wait(p.t, ctok[par])
            tp = sPE.inc(p.t.transpose(p.bank(par, 128),
                                       x_fm[:, c * S + tc * 128: c * S + tc * 128 + 128],
                                       p.identf[:]))
            sPE.wait(p.v, tp)
            ctok[par] = sDVE.inc(p.v.tensor_copy(stage[:, c * 128:(c + 1) * 128], p.bank(par, 128)))
        sDVE.wait(p.sy)
        sST.inc(p.sy.dma_start(out=y_dram[tc * 128:(tc + 1) * 128, :], in_=stage[:]), 16)
    return sST


def emit_bias_cols(p, dram_2d, out_tile, rows, scale=None):
    st_ = ExitStack()
    raw = p.sbuf(f"braw_{out_tile.name}", [128, 128], F32, stack=st_)
    sB = p.sem("sBias")
    tokz = sB.inc(p.g.memset(raw[:], 0.0))
    sB.wait(p.sy, tokz)
    tok = sB.inc(p.sy.dma_start(out=raw[0:rows, :], in_=dram_2d), 16)
    sB.wait(p.t, tok)
    tp = sB.inc(p.t.transpose(p.bank(7, 128), raw[:], p.identf[:]))
    sB.wait(p.v, tp)
    if scale is None:
        sB.inc(p.v.tensor_copy(out_tile[:], p.bank(7, 128)[:, 0:rows]))
    else:
        sB.inc(p.v.tensor_scalar_mul(out_tile[:], p.bank(7, 128)[:, 0:rows], scale))
    sB.wait(p.t)
    sB.wait(p.v)
    sB.wait(p.g)
    sB.wait(p.sy)
    st_.close()


def emit_ln(p, x_fm, tag):
    st_ = ExitStack()
    rbf = p.sbuf(f"{tag}_rbf", [128, 512], BF16, stack=st_)
    rsq = p.sbuf(f"{tag}_rsq", [128, 512], BF16, stack=st_)
    rowA = p.sbuf(f"{tag}_rowA", [1, S], F32, stack=st_)
    rowB = p.sbuf(f"{tag}_rowB", [1, S], F32, stack=st_)
    rowC = p.sbuf(f"{tag}_rowC", [1, S], F32, stack=st_)
    sA = p.sem(f"{tag}A")
    sP = p.sem(f"{tag}P")
    sV = p.sem(f"{tag}V")
    ps0 = p.bankrow(0)
    ps1 = p.bankrow(1)
    for tt in range(4):
        for c in range(NCH):
            sP.wait(p.sc)
            p.sc.activation(out=rbf[:], in_=x_fm[:, c * S + tt * 512: c * S + tt * 512 + 512], func=AF.Copy)
            ta = sA.inc(p.sc.activation(out=rsq[:], in_=x_fm[:, c * S + tt * 512: c * S + tt * 512 + 512], func=AF.Square))
            sA.wait(p.t, ta)
            p.t.matmul(ps0, p.ones_col_b[:], rbf[:], start=(c == 0), stop=(c == NCH - 1))
            sP.inc(p.t.matmul(ps1, p.ones_col_b[:], rsq[:], start=(c == 0), stop=(c == NCH - 1)))
        sP.wait(p.v)
        p.v.tensor_copy(rowA[0:1, tt * 512: tt * 512 + 512], ps0)
        tv = sV.inc(p.v.tensor_copy(rowB[0:1, tt * 512: tt * 512 + 512], ps1))
        sV.wait(p.t, tv)
    def step(instr):
        tok = sV.inc(instr)
        sV.wait(p.v, tok)
        return tok
    step(p.v.tensor_scalar_mul(rowA[:], rowA[:], 1.0 / E))
    step(p.v.tensor_mul(rowC[:], rowA[:], rowA[:]))
    tokA = sV.inc(p.v.tensor_scalar_mul(rowA[:], rowA[:], -1.0))
    sV.wait(p.v, tokA)
    tokB = sV.inc(p.v.scalar_tensor_tensor(rowB[:], rowB[:], 1.0 / E, rowC[:], ALU.mult, ALU.subtract))
    sV.wait(p.v, tokB)
    tokB = sV.inc(p.v.tensor_single_scalar(rowB[:], rowB[:], LN_EPS, ALU.add))
    sV.wait(p.sc, tokB)
    ta = sA.inc(p.sc.activation(out=rowB[:], in_=rowB[:], func=AF.Sqrt))
    sA.wait(p.v, ta)
    step(p.v.reciprocal(rowB[:], rowB[:]))
    sV.wait(p.t)
    for tt in range(4):
        sV.wait(p.t)
        p.t.matmul(p.bank(2), p.ones_row_f[:], rowA[0:1, tt * 512: tt * 512 + 512], start=True, stop=True)
        tp = sP.inc(p.t.matmul(p.bank(3), p.ones_row_f[:], rowB[0:1, tt * 512: tt * 512 + 512], start=True, stop=True))
        sP.wait(p.v, tp)
        for c in range(NCH):
            sl = x_fm[:, c * S + tt * 512: c * S + tt * 512 + 512]
            p.v.tensor_tensor(sl, sl, p.bank(2), ALU.add)
            tv = sV.inc(p.v.tensor_tensor(sl, sl, p.bank(3), ALU.mult))
            sV.wait(p.v, tv)
    p.barrier()
    st_.close()



def emit_l0_attn(p, x_fm, Wq, bq, Wk, bk, Wv, bv, O_dram=None, parts="AB", nheads=16, dbg=None):
    stb = ExitStack()
    Kt = p.sbuf("Kt", [128, 4 * S], BF16, stack=stb)
    Vt = p.sbuf("Vt", [128, 4 * S], BF16, stack=stb)
    xb = [p.sbuf(f"at_xb{i}", [128, 512], BF16, stack=stb) for i in range(2)]
    expb = p.sbuf("expb", [128, S], BF16, stack=stb)
    expT = [p.sbuf(f"expT{i}", [128, 512], BF16, stack=stb) for i in range(2)]
    pmax = p.sbuf("pmax", [128, 4], F32, stack=stb)
    negmax = p.sbuf("negmax", [128, 1], F32, stack=stb)
    zcol = p.sbuf("zcol", [128, 1], F32, stack=stb)
    rz = p.sbuf("rz", [128, 1], F32, stack=stb)
    zrow = p.sbuf("zrow", [1, 128], F32, stack=stb)
    rzbc = p.sbuf("rzbc", [128, 128], BF16, stack=stb)
    bqs = p.sbuf("bqs", [128, 16], F32, stack=stb)
    bks = p.sbuf("bks", [128, 4], F32, stack=stb)
    bvs = p.sbuf("bvs", [128, 4], F32, stack=stb)

    sW = p.sem("atW")
    sA = p.sem("atA")
    sP = p.sem("atP")
    sV = p.sem("atV")

    emit_bias_cols(p, bq, bqs, 16, scale=QSCALE)
    emit_bias_cols(p, bk, bks, 4)
    emit_bias_cols(p, bv, bvs, 4)
    p.barrier()

    with ExitStack() as st_:
        wkv = [p.sbuf(f"wkv{i}", [128, E], BF16, stack=st_) for i in range(4)]
        Vf = p.sbuf("Vf", [128, 2 * S], BF16, stack=st_)
        wkv_lastuse = 0
        for sweep in range(2):
            g0 = sweep * 2
            sP.wait(p.g, wkv_lastuse)
            for i, (W, g) in enumerate(((Wk, g0), (Wk, g0 + 1), (Wv, g0), (Wv, g0 + 1))):
                sW.inc(p.g.dma_start(out=wkv[i][:].rearrange("p (c d) -> p c d", c=NCH),
                                     in_=W[g].rearrange("(c p) d -> p c d", p=128)), 16)
            sW.wait(p.t)
            casttok = [0, 0]
            for tt in range(4):
                for c in range(NCH):
                    par = c % 2
                    sP.wait(p.sc, casttok[par])
                    ta = sA.inc(p.sc.activation(out=xb[par][:],
                                                in_=x_fm[:, c * S + tt * 512: c * S + tt * 512 + 512],
                                                func=AF.Copy))
                    sA.wait(p.t, ta)
                    for o in range(4):
                        mm = p.t.matmul(p.bank(o), wkv[o][:, c * 128:(c + 1) * 128], xb[par][:],
                                        start=(c == 0), stop=(c == NCH - 1))
                    casttok[par] = sP.inc(mm)
                wkv_lastuse = sP.n
                sP.wait(p.sc)
                for o in range(4):
                    g = g0 + (o % 2)
                    if o < 2:
                        dst = Kt[:, g * S + tt * 512: g * S + tt * 512 + 512]
                        bias = bks[:, g:g + 1]
                    else:
                        dst = Vf[:, (o - 2) * S + tt * 512: (o - 2) * S + tt * 512 + 512]
                        bias = bvs[:, g:g + 1]
                    ta = sA.inc(p.sc.activation(out=dst, in_=p.bank(o), func=AF.Identity, bias=bias))
                sA.wait(p.t, ta)
            ctokT = [0, 0]
            for gg in range(2):
                g = g0 + gg
                for kt in range(NCH):
                    par = kt % 2
                    sV.wait(p.t, ctokT[par])
                    tp = sP.inc(p.t.transpose(p.bankb(4 + par)[:, 0:128],
                                              Vf[:, gg * S + kt * 128: gg * S + kt * 128 + 128],
                                              p.identb[:]))
                    sP.wait(p.v, tp)
                    ctokT[par] = sV.inc(p.v.tensor_copy(
                        Vt[:, g * S + kt * 128: g * S + kt * 128 + 128], p.bankb(4 + par)[:, 0:128]))
            sV.wait(p.t)
    p.barrier()

    Qh = p.sbuf("Qh", [128, 2 * S], BF16, stack=stb)
    wq = [p.sbuf(f"wq{i}", [128, E], BF16, stack=stb) for i in range(2)]
    ostg = [p.sbuf(f"ostg{i}", [128, 128], F32, stack=stb) for i in range(2)]
    sO = p.sem("atO")
    otok = [0, 0]
    onum = 0
    if "B" not in parts:
        p.barrier()
        stb.close()
        return
    wq_lastuse = 0
    for hp in range(max(0, nheads) // 2):
        heads = (2 * hp, 2 * hp + 1)
        sV.wait(p.t)
        sP.wait(p.g, wq_lastuse)
        for i, h in enumerate(heads):
            sW.inc(p.g.dma_start(out=wq[i][:].rearrange("p (c d) -> p c d", c=NCH),
                                 in_=Wq[h].rearrange("(c p) d -> p c d", p=128)), 16)
        sW.wait(p.t)
        casttok = [0, 0]
        for tt in range(4):
            for c in range(NCH):
                par = c % 2
                sP.wait(p.sc, casttok[par])
                ta = sA.inc(p.sc.activation(out=xb[par][:],
                                            in_=x_fm[:, c * S + tt * 512: c * S + tt * 512 + 512],
                                            func=AF.Copy))
                sA.wait(p.t, ta)
                for i in range(2):
                    mm = p.t.matmul(p.bank(6 + i), wq[i][:, c * 128:(c + 1) * 128], xb[par][:],
                                    start=(c == 0), stop=(c == NCH - 1))
                casttok[par] = sP.inc(mm)
            wq_lastuse = sP.n
            sP.wait(p.sc)
            for i, h in enumerate(heads):
                ta = sA.inc(p.sc.activation(out=Qh[:, i * S + tt * 512: i * S + tt * 512 + 512],
                                            in_=p.bank(6 + i), func=AF.Identity,
                                            scale=QSCALE, bias=bqs[:, h:h + 1]))
            sA.wait(p.t, ta)
        for i, h in enumerate(heads):
            g = h // 4
            exptok = 0
            evactok = 0
            for qi in range(16):
                nk = qi + 1
                ns = (nk + 3) // 4
                sA.wait(p.t)
                sV.wait(p.t, evactok)
                sctok = []
                for s in range(ns):
                    cols = min(512, nk * 128 - s * 512)
                    mm = p.t.matmul(p.bank(s, cols), Qh[:, i * S + qi * 128: i * S + qi * 128 + 128],
                                    Kt[:, g * S + s * 512: g * S + s * 512 + cols],
                                    start=True, stop=True)
                    sctok.append(sP.inc(mm))
                for s in range(ns):
                    cols = min(512, nk * 128 - s * 512)
                    sP.wait(p.v, sctok[s])
                    mr = p.v.reduce_max(pmax[:, s:s + 1], p.bank(s, cols), axis=AX.X)
                tv = sV.inc(mr)
                sV.wait(p.v, tv)
                tv = sV.inc(p.v.reduce_max(negmax[:], pmax[:, 0:ns], axis=AX.X, negate=True))
                if dbg is not None and "s0" in dbg and h == dbg["_h"] and qi == dbg["_qi"]:
                    p.v.tensor_copy(dbg["_sb"][:], p.bank(0, 128))
                    sV.inc(p.v.tensor_copy(dbg["_qb"][:], Qh[:, i * S + qi * 128: i * S + qi * 128 + 128]))
                sV.wait(p.sc, tv)
                sP.wait(p.sc, exptok)
                for s in range(ns):
                    cols = min(512, nk * 128 - s * 512)
                    ta = sA.inc(p.sc.activation(out=expb[:, s * 512: s * 512 + cols],
                                                in_=p.bank(s, cols), func=AF.Exp,
                                                bias=negmax[:, 0:1]))
                sA.wait(p.v, ta)
                dsl = expb[:, qi * 128: qi * 128 + 128]
                tv = sV.inc(p.v.tensor_tensor(dsl, dsl, p.mask01[:], ALU.mult))
                sV.wait(p.v, tv)
                tv = sV.inc(p.v.reduce_sum(zcol[:], expb[:, 0:nk * 128], axis=AX.X))
                sV.wait(p.v, tv)
                tv = sV.inc(p.v.reciprocal(rz[:], zcol[:]))
                sV.wait(p.t, tv)
                tp = sP.inc(p.t.transpose(p.bankrow(4, 128), rz[:], p.identf[:]))
                sP.wait(p.v, tp)
                tv = sV.inc(p.v.tensor_copy(zrow[:], p.bankrow(4, 128)))
                sV.wait(p.t, tv)
                tp = sP.inc(p.t.matmul(p.bank(5, 128), p.ones_row_f[:], zrow[:], start=True, stop=True))
                sP.wait(p.v, tp)
                tv = sV.inc(p.v.tensor_copy(rzbc[:], p.bank(5, 128)))
                sV.wait(p.t, tv)
                ngrp = ns
                Ttok = []
                Mtok = []
                mults_tok = []

                def emit_T(grp):
                    gpar = grp % 2
                    base = grp * 4
                    cnt = min(4, nk - base)
                    if grp >= 2:
                        sV.wait(p.t, mults_tok[grp - 2])
                    for j in range(cnt):
                        tpj = p.t.transpose(p.bankb(4 + gpar)[:, j * 128:(j + 1) * 128],
                                            expb[:, (base + j) * 128:(base + j + 1) * 128],
                                            p.identb[:])
                    Ttok.append(sP.inc(tpj))

                emit_T(0)
                if ngrp > 1:
                    emit_T(1)
                for grp in range(ngrp):
                    gpar = grp % 2
                    base = grp * 4
                    cnt = min(4, nk - base)
                    sP.wait(p.v, Ttok[grp])
                    for j in range(cnt):
                        mv = p.v.tensor_tensor(expT[gpar][:, j * 128:(j + 1) * 128],
                                               p.bankb(4 + gpar)[:, j * 128:(j + 1) * 128],
                                               rzbc[:], ALU.mult)
                    mults_tok.append(sV.inc(mv))
                    sV.wait(p.t, mults_tok[grp])
                    for j in range(cnt):
                        kt = base + j
                        mm = p.t.matmul(p.bank(6, 128), Vt[:, g * S + kt * 128: g * S + kt * 128 + 128],
                                        expT[gpar][:, j * 128:(j + 1) * 128],
                                        start=(kt == 0), stop=(kt == nk - 1))
                    Mtok.append(sP.inc(mm))
                    if grp + 2 < ngrp:
                        emit_T(grp + 2)
                exptok = sP.n
                sP.wait(p.v, Mtok[-1])
                if dbg is not None and "e0" in dbg and h == dbg["_h"] and qi == dbg["_qi"]:
                    p.v.tensor_copy(dbg["_eb"][:], expb[:])
                    p.v.tensor_copy(dbg["_zb"][:], zcol[:])
                    p.v.tensor_copy(dbg["_rb"][:], rzbc[:])
                    sV.inc(p.v.tensor_copy(dbg["_ob"][:], p.bank(6, 128)))
                par = onum % 2
                sO.wait(p.v, otok[par])
                evactok = sV.inc(p.v.tensor_copy(ostg[par][:], p.bank(6, 128)))
                sV.wait(p.sy, evactok)
                otok[par] = sO.inc(p.sy.dma_start(
                    out=O_dram[:, h * S + qi * 128: h * S + qi * 128 + 128], in_=ostg[par][:]), 16)
                onum += 1
    obig = [p.sbuf(f"obig{i}", [128, E], F32, stack=stb) for i in range(2)]
    sO.wait(p.sy)
    btok = [0, 0]
    for c in range(NCH):
        par = c % 2
        sV.wait(p.sy, btok[par])
        tokl = sO.inc(p.sy.dma_start(out=obig[par][:], in_=O_dram[:, c * S:(c + 1) * S]), 16)
        sO.wait(p.v, tokl)
        xsl = x_fm[:, c * S:(c + 1) * S]
        btok[par] = sV.inc(p.v.tensor_tensor(xsl, xsl, obig[par][:], ALU.add))
    sV.wait(p.t)
    p.barrier()
    if dbg is not None and "e0" in dbg:
        sD0 = p.sem("sDbg0")
        sV.wait(p.sy)
        sD0.inc(p.sy.dma_start(out=dbg["e0"], in_=dbg["_eb"][:]), 16)
        sD0.inc(p.sy.dma_start(out=dbg["z0"], in_=dbg["_zb"][:]), 16)
        sD0.inc(p.sy.dma_start(out=dbg["r0"], in_=dbg["_rb"][:]), 16)
        sD0.inc(p.sy.dma_start(out=dbg["o0"], in_=dbg["_ob"][:]), 16)
        if "s0" in dbg:
            sD0.inc(p.sy.dma_start(out=dbg["s0"], in_=dbg["_sb"][:]), 16)
            sD0.inc(p.sy.dma_start(out=dbg["q0"], in_=dbg["_qb"][:]), 16)
        sD0.wait(p.sy)
    if dbg is not None:
        sD = p.sem("sDbg")
        sD.inc(p.sy.dma_start(out=dbg["q"], in_=Qh[:]), 16)
        sD.inc(p.sy.dma_start(out=dbg["k"], in_=Kt[:]), 16)
        sD.inc(p.sy.dma_start(out=dbg["v"], in_=Vt[:]), 16)
        sD.inc(p.sy.dma_start(out=dbg["e"], in_=expb[:]), 16)
        sD.inc(p.sy.dma_start(out=dbg["z"], in_=zcol[:]), 16)
        sD.inc(p.sy.dma_start(out=dbg["rzbc"], in_=rzbc[:]), 16)
        sD.wait(p.sy)
        p.barrier()
    stb.close()



def emit_wT_prep(p, dram_rows_ap, wT, sW, sP, sV, tag, nch=NCH, staging=None,
                 out_off=0, out_stride=128):
    sP.wait(p.g)
    tokw = sW.inc(p.g.dma_start(out=staging[:, 0:nch * 128].rearrange("p (c d) -> p c d", c=nch),
                                in_=dram_rows_ap), 16)
    sW.wait(p.t, tokw)
    for c in range(nch):
        par = c % 2
        sV.wait(p.t, max(0, sV.n - 1))
        tp = sP.inc(p.t.transpose(p.bankb(4 + par)[:, 0:128],
                                  staging[:, c * 128:(c + 1) * 128], p.identb[:]))
        sP.wait(p.v, tp)
        o = out_off + c * out_stride
        tv = sV.inc(p.v.tensor_copy(wT[:, o: o + 128], p.bankb(4 + par)[:, 0:128]))
    return sV.n


def emit_l0_ffn(p, x_fm, w_gate, w_up, w_down, g_dram, parts="AB"):
    sW = p.sem("f0W")
    sA = p.sem("f0A")
    sP = p.sem("f0P")
    sV = p.sem("f0V")
    sG = p.sem("f0G")

    with ExitStack() as st_:
        xbh = p.sbuf("f0_xbh", [128, NCH * 1024], BF16, stack=st_)
        wgT = [p.sbuf(f"f0_wgT{i}", [128, NCH * 128], BF16, stack=st_) for i in range(4)]
        wuT = [p.sbuf(f"f0_wuT{i}", [128, NCH * 128], BF16, stack=st_) for i in range(4)]
        wstg = p.sbuf("f0_wstg", [128, E], BF16, stack=st_)
        sil = p.sbuf("f0_sil", [128, 512], F32, stack=st_)
        gt = p.sbuf("f0_gt", [128, 512], BF16, stack=st_)
        for half in range(2):
            t0 = half * 1024
            sP.wait(p.sc)
            ta = 0
            for c in range(NCH):
                ta = sA.inc(p.sc.activation(out=xbh[:, c * 1024:(c + 1) * 1024],
                                            in_=x_fm[:, c * S + t0: c * S + t0 + 1024],
                                            func=AF.Copy))
            sA.wait(p.t, ta)
            for Hq in range(16):
                sA.wait(p.t)
                sV.wait(p.t)
                for i in range(4):
                    h = Hq * 4 + i
                    emit_wT_prep(p, w_gate[h * 128:(h + 1) * 128, :].rearrange("r (c pp) -> r c pp", pp=128),
                                 wgT[i], sW, sP, sV, "wg", staging=wstg)
                    emit_wT_prep(p, w_up[h * 128:(h + 1) * 128, :].rearrange("r (c pp) -> r c pp", pp=128),
                                 wuT[i], sW, sP, sV, "wu", staging=wstg)
                sV.wait(p.t)
                for tsl in range(2):
                    tb = tsl * 512
                    for c in range(NCH):
                        for i in range(4):
                            p.t.matmul(p.bank(i), wgT[i][:, c * 128:(c + 1) * 128],
                                       xbh[:, c * 1024 + tb: c * 1024 + tb + 512],
                                       start=(c == 0), stop=(c == NCH - 1))
                            mm = p.t.matmul(p.bank(4 + i), wuT[i][:, c * 128:(c + 1) * 128],
                                            xbh[:, c * 1024 + tb: c * 1024 + tb + 512],
                                            start=(c == 0), stop=(c == NCH - 1))
                    tp = sP.inc(mm)
                    for i in range(4):
                        h = Hq * 4 + i
                        sP.wait(p.sc, tp)
                        sV.wait(p.sc)
                        ta = sA.inc(p.sc.activation(out=sil[:], in_=p.bank(i), func=AF.Silu))
                        sA.wait(p.v, ta)
                        tv = sV.inc(p.v.tensor_tensor(gt[:], p.bank(4 + i), sil[:], ALU.mult))
                        sV.wait(p.g, tv)
                        sG.inc(p.g.dma_start(out=g_dram[h, :, t0 + tb: t0 + tb + 512], in_=gt[:]), 16)
                        sG.wait(p.v)
                    sA.wait(p.t)
                    sV.wait(p.t)
    p.barrier()
    if "B" not in parts:
        return

    with ExitStack() as st_:
        wdT = [p.sbuf(f"f0_wdT{i}", [128, NCH * 128], BF16, stack=st_) for i in range(8)]
        gh = [p.sbuf(f"f0_gh{i}", [128, S], BF16, stack=st_) for i in range(8)]
        wdstg = p.sbuf("f0_wdstg", [128, NCH * 128], BF16, stack=st_)
        for grp in range(8):
            sP.wait(p.g)
            for i in range(8):
                h = grp * 8 + i
                emit_wT_prep(p, w_down[:, h * 128:(h + 1) * 128].rearrange("(c pp) h -> pp c h", pp=128),
                             wdT[i], sW, sP, sV, "wd", staging=wdstg)
                sG.inc(p.g.dma_start(out=gh[i][:], in_=g_dram[h, :, :]), 16)
            sV.wait(p.t)
            sG.wait(p.t)
            for win in range(8):
                dhalf, tsl = win // 4, win % 4
                tb = tsl * 512
                for i in range(8):
                    for dd in range(8):
                        D = dhalf * 8 + dd
                        mm = p.t.matmul(p.bank(dd), wdT[i][:, D * 128:(D + 1) * 128],
                                        gh[i][:, tb: tb + 512],
                                        start=(i == 0), stop=(i == 7))
                tp = sP.inc(mm)
                sP.wait(p.v, tp)
                for dd in range(8):
                    D = dhalf * 8 + dd
                    sl = x_fm[:, D * S + tb: D * S + tb + 512]
                    tv = sV.inc(p.v.tensor_tensor(sl, sl, p.bank(dd), ALU.add))
                sV.wait(p.t, tv)
    p.barrier()


def build_l0_ffn(parts="ABLN"):
    import os
    p = Prog()
    nc = p.nc
    x = nc.dram_tensor("x", [S, E], F32, kind="ExternalInput").ap()
    w_gate = nc.dram_tensor("w_gate", [HID, E], F32, kind="ExternalInput").ap()
    w_up = nc.dram_tensor("w_up", [HID, E], F32, kind="ExternalInput").ap()
    w_down = nc.dram_tensor("w_down", [E, HID], F32, kind="ExternalInput").ap()
    y = nc.dram_tensor("y", [S, E], F32, kind="ExternalOutput").ap()
    gkind = "ExternalOutput" if os.environ.get("DBG") else "Internal"
    g_dram = nc.dram_tensor("g_scratch", [64, 128, S], BF16, kind=gkind).ap()

    x_fm = p.sbuf("x_fm", [128, NCH * S], F32)
    p.setup_consts()
    emit_load_x(p, x, x_fm)
    if "A" in parts:
        emit_l0_ffn(p, x_fm, w_gate, w_up, w_down, g_dram, parts=parts)
    if "LN" in parts:
        emit_ln(p, x_fm, "ln1")
    sST = emit_store_x(p, x_fm, y)
    sST.wait(p.sy)
    return p.finish()



def emit_l1_attn(p, x_fm, w_qkv, w_proj, b_proj, parts="12"):
    sW = p.sem("a1W")
    sA = p.sem("a1A")
    sP = p.sem("a1P")
    sV = p.sem("a1V")

    blockdiag = p.sbuf("a1_bd", [128, 16 * 128], BF16)
    ksums = p.sbuf("a1_ks", [128, 16 * 4], BF16)
    sel4 = p.sbuf("a1_sel4", [4, 128], F32)
    bps = p.sbuf("a1_bps", [128, 16], F32)
    sC = p.sem("a1C")
    g = p.g
    g.memset(blockdiag[:], 0.0)
    g.memset(ksums[:], 0.0)
    g.memset(sel4[:], 1.0)
    g.affine_select(out=sel4[:], in_=sel4[:], compare_op=ALU.is_ge, fill=0.0,
                    base=0, pattern=[[1, 128]], channel_multiplier=-32)
    sC.inc(g.affine_select(out=sel4[:], in_=sel4[:], compare_op=ALU.is_ge, fill=0.0,
                           base=31, pattern=[[-1, 128]], channel_multiplier=32))
    for eng in (p.t, p.v, p.sc):
        sC.wait(eng)
    emit_bias_cols(p, b_proj.rearrange("(c pp) -> c pp", pp=128), bps, 16)
    p.barrier()

    if "1" in parts:
      with ExitStack() as st_:
        wkT = p.sbuf("a1_wkT", [128, NCH * 512], BF16, stack=st_)
        wvT = p.sbuf("a1_wvT", [128, NCH * 512], BF16, stack=st_)
        wstg = p.sbuf("a1_wstg", [128, E], BF16, stack=st_)
        xb = p.sbuf("a1_xb", [128, NCH * 128], BF16, stack=st_)
        k_tm = p.sbuf("a1_ktm", [128, 512], BF16, stack=st_)
        vext = p.sbuf("a1_vext", [128, 4 * 132], BF16, stack=st_)
        sC2 = p.sem("a1C2")
        sC2.inc(p.g.memset(vext[:], 1.0))
        sC2.wait(p.sc)
        for si in range(4):
            for rr in range(4):
                krow = 2048 + si * 512 + rr * 128
                vrow = 4096 + si * 512 + rr * 128
                emit_wT_prep(p, w_qkv[krow:krow + 128, :].rearrange("r (c pp) -> r c pp", pp=128),
                             wkT, sW, sP, sV, "wk", staging=wstg,
                             out_off=rr * 128, out_stride=512)
                emit_wT_prep(p, w_qkv[vrow:vrow + 128, :].rearrange("r (c pp) -> r c pp", pp=128),
                             wvT, sW, sP, sV, "wv", staging=wstg,
                             out_off=rr * 128, out_stride=512)
            sV.wait(p.t)
            for t128 in range(16):
                tb = t128 * 128
                sP.wait(p.sc)
                ta = 0
                for c in range(NCH):
                    ta = sA.inc(p.sc.activation(out=xb[:, c * 128:(c + 1) * 128],
                                                in_=x_fm[:, c * S + tb: c * S + tb + 128],
                                                func=AF.Copy))
                sA.wait(p.t, ta)
                for c in range(NCH):
                    p.t.matmul(p.bank(0), xb[:, c * 128:(c + 1) * 128], wkT[:, c * 512:(c + 1) * 512],
                               start=(c == 0), stop=(c == NCH - 1))
                    mm = p.t.matmul(p.bank(1), xb[:, c * 128:(c + 1) * 128], wvT[:, c * 512:(c + 1) * 512],
                                    start=(c == 0), stop=(c == NCH - 1))
                tp = sP.inc(mm)
                sP.wait(p.sc, tp)
                ta = sA.inc(p.sc.activation(out=k_tm[:], in_=p.bank(0), func=AF.Relu))
                for q in range(4):
                    ta = sA.inc(p.sc.activation(out=vext[:, q * 132: q * 132 + 128],
                                                in_=p.bank(1, 512)[:, q * 128:(q + 1) * 128],
                                                func=AF.Copy))
                sA.wait(p.t, ta)
                for q in range(4):
                    mm = p.t.matmul(p.bank(2 + q, 129), k_tm[:, q * 128:(q + 1) * 128],
                                    vext[:, q * 132: q * 132 + 129],
                                    start=(t128 == 0), stop=(t128 == 15))
                sP.inc(mm)
            sP.wait(p.v)
            for q in range(4):
                Q = si * 4 + q
                for qp in range(4):
                    b0 = 32 * qp
                    p.v.tensor_copy(blockdiag[b0:b0 + 32, Q * 128 + b0: Q * 128 + b0 + 32],
                                    p.bank(2 + q, 129)[b0:b0 + 32, b0:b0 + 32])
                    tv = sV.inc(p.v.tensor_copy(ksums[b0:b0 + 32, Q * 4 + qp: Q * 4 + qp + 1],
                                                p.bank(2 + q, 129)[b0:b0 + 32, 128:129]))
            sV.wait(p.t, tv)
      p.barrier()

    if "2" in parts:
      with ExitStack() as st_:
        xb5 = p.sbuf("a1_xb5", [128, NCH * 512], BF16, stack=st_)
        wT = p.sbuf("a1_wT", [128, NCH * 128], BF16, stack=st_)
        wstg = p.sbuf("a1_wstg2", [128, E], BF16, stack=st_)
        q4 = p.sbuf("a1_q4", [128, 512], BF16, stack=st_)
        den_r = p.sbuf("a1_denr", [4, 512], F32, stack=st_)
        denb_s = p.sbuf("a1_denb", [128, 512], F32, stack=st_)
        o_fm = p.sbuf("a1_ofm", [128, NCH * 512], BF16, stack=st_)
        for tt in range(4):
            tb = tt * 512
            sP.wait(p.sc)
            ta = 0
            for c in range(NCH):
                ta = sA.inc(p.sc.activation(out=xb5[:, c * 512:(c + 1) * 512],
                                            in_=x_fm[:, c * S + tb: c * S + tb + 512],
                                            func=AF.Copy))
            for Q in range(16):
                emit_wT_prep(p, w_qkv[Q * 128:(Q + 1) * 128, :].rearrange("r (c pp) -> r c pp", pp=128),
                             wT, sW, sP, sV, "wq", staging=wstg)
                sV.wait(p.t)
                sA.wait(p.t, ta)
                for c in range(NCH):
                    mm = p.t.matmul(p.bank(0), wT[:, c * 128:(c + 1) * 128],
                                    xb5[:, c * 512:(c + 1) * 512],
                                    start=(c == 0), stop=(c == NCH - 1))
                tp = sP.inc(mm)
                sP.wait(p.sc, tp)
                tq = sA.inc(p.sc.activation(out=q4[:], in_=p.bank(0), func=AF.Relu))
                sA.wait(p.t, tq)
                p.t.matmul(p.bank(1), blockdiag[:, Q * 128:(Q + 1) * 128], q4[:],
                           start=True, stop=True)
                tp = sP.inc(p.t.matmul(p.bank(2, 512)[0:4, :], ksums[:, Q * 4:(Q + 1) * 4], q4[:],
                                       start=True, stop=True))
                sP.wait(p.v, tp)
                tv = sV.inc(p.v.tensor_single_scalar(den_r[:], p.bank(2, 512)[0:4, :], 1e-15, ALU.add))
                sV.wait(p.v, tv)
                tv = sV.inc(p.v.reciprocal(den_r[:], den_r[:]))
                sV.wait(p.t, tv)
                tp = sP.inc(p.t.matmul(p.bank(3), sel4[:], den_r[:], start=True, stop=True))
                sP.wait(p.v, tp)
                tv = sV.inc(p.v.tensor_copy(denb_s[:], p.bank(3)))
                sV.wait(p.v, tv)
                tv = sV.inc(p.v.tensor_tensor(o_fm[:, Q * 512:(Q + 1) * 512], p.bank(1), denb_s[:], ALU.mult))
                sV.wait(p.t, tv)
            for D in range(NCH):
                emit_wT_prep(p, w_proj[D * 128:(D + 1) * 128, :].rearrange("r (c pp) -> r c pp", pp=128),
                             wT, sW, sP, sV, "wp", staging=wstg)
                sV.wait(p.t)
                for c in range(NCH):
                    mm = p.t.matmul(p.bank(0), wT[:, c * 128:(c + 1) * 128],
                                    o_fm[:, c * 512:(c + 1) * 512],
                                    start=(c == 0), stop=(c == NCH - 1))
                tp = sP.inc(mm)
                sP.wait(p.v, tp)
                sl = x_fm[:, D * S + tb: D * S + tb + 512]
                tv = sV.inc(p.v.scalar_tensor_tensor(sl, p.bank(0), bps[:, D:D + 1], sl,
                                                     ALU.add, ALU.add))
                sV.wait(p.t, tv)
      p.barrier()


def build_l1_attn(parts="12LN"):
    p = Prog()
    nc = p.nc
    x = nc.dram_tensor("x", [S, E], F32, kind="ExternalInput").ap()
    w_qkv = nc.dram_tensor("w_qkv", [3 * E, E], F32, kind="ExternalInput").ap()
    w_proj = nc.dram_tensor("w_proj", [E, E], F32, kind="ExternalInput").ap()
    b_proj = nc.dram_tensor("b_proj", [E], F32, kind="ExternalInput").ap()
    y = nc.dram_tensor("y", [S, E], F32, kind="ExternalOutput").ap()

    x_fm = p.sbuf("x_fm", [128, NCH * S], F32)
    p.setup_consts()
    emit_load_x(p, x, x_fm)
    emit_l1_attn(p, x_fm, w_qkv, w_proj, b_proj, parts=parts)
    if "LN" in parts:
        emit_ln(p, x_fm, "ln2")
    sST = emit_store_x(p, x_fm, y)
    sST.wait(p.sy)
    return p.finish()



def emit_l1_ffn(p, x_fm, w_gate, w_up, w_down, hT_dram, parts="AB"):
    sW = p.sem("f1W")
    sA = p.sem("f1A")
    sP = p.sem("f1P")
    sV = p.sem("f1V")
    sG = p.sem("f1G")

    if "A" in parts:
      with ExitStack() as st_:
        wg_sl = p.sbuf("f1_wg", [128, NCH * 512], BF16, stack=st_)
        wu_sl = p.sbuf("f1_wu", [128, NCH * 512], BF16, stack=st_)
        xb = p.sbuf("f1_xb", [128, NCH * 128], BF16, stack=st_)
        sil = p.sbuf("f1_sil", [128, 512], F32, stack=st_)
        h = p.sbuf("f1_h", [128, 512], F32, stack=st_)
        a4 = p.sbuf("f1_a4", [128, 512], F32, stack=st_)
        cmp6 = p.sbuf("f1_cmp", [128, 6 * 128], F32, stack=st_)
        sacc = p.sbuf("f1_sacc", [128, 128], F32, stack=st_)
        keep = p.sbuf("f1_keep", [128, 4 * 128], F32, stack=st_)
        hm = p.sbuf("f1_hm", [128, 512], BF16, stack=st_)
        hTt = p.sbuf("f1_hTt", [128, 512], BF16, stack=st_)

        def hs_(i):
            return h[:].rearrange("p (g f) -> p g f", f=4)[:, :, i:i + 1].rearrange("p g o -> p (g o)")

        def hms_(i):
            return hm[:].rearrange("p (g f) -> p g f", f=4)[:, :, i:i + 1].rearrange("p g o -> p (g o)")

        for hs in range(16):
            sP.wait(p.g)
            tokw = sW.inc(p.g.dma_start(
                out=wg_sl[:].rearrange("p (c i) -> p c i", c=NCH),
                in_=w_gate.rearrange("(c p) i -> p c i", p=128)[:, :, hs * 512:(hs + 1) * 512]), 16)
            tokw = sW.inc(p.g.dma_start(
                out=wu_sl[:].rearrange("p (c i) -> p c i", c=NCH),
                in_=w_up.rearrange("(c p) i -> p c i", p=128)[:, :, hs * 512:(hs + 1) * 512]), 16)
            sW.wait(p.t, tokw)
            for t128 in range(16):
                tb = t128 * 128
                sP.wait(p.sc)
                ta = 0
                for c in range(NCH):
                    ta = sA.inc(p.sc.activation(out=xb[:, c * 128:(c + 1) * 128],
                                                in_=x_fm[:, c * S + tb: c * S + tb + 128],
                                                func=AF.Copy))
                sA.wait(p.t, ta)
                for c in range(NCH):
                    p.t.matmul(p.bank(0), xb[:, c * 128:(c + 1) * 128], wg_sl[:, c * 512:(c + 1) * 512],
                               start=(c == 0), stop=(c == NCH - 1))
                    mm = p.t.matmul(p.bank(1), xb[:, c * 128:(c + 1) * 128], wu_sl[:, c * 512:(c + 1) * 512],
                                    start=(c == 0), stop=(c == NCH - 1))
                tp = sP.inc(mm)
                sP.wait(p.sc, tp)
                sV.wait(p.sc)
                ta = sA.inc(p.sc.activation(out=sil[:], in_=p.bank(0), func=AF.Silu))
                sA.wait(p.v, ta)
                tv = sV.inc(p.v.tensor_tensor(h[:], p.bank(1), sil[:], ALU.mult))
                sV.wait(p.t, tv)
                v = p.v
                sV.wait(p.sc)
                ta = sA.inc(p.sc.activation(out=a4[:], in_=h[:], func=AF.Abs))
                sA.wait(p.v, ta)
                def a_(i):
                    return a4[:].rearrange("p (g f) -> p g f", f=4)[:, :, i:i + 1].rearrange("p g o -> p (g o)")
                pair = [(0, 1), (0, 2), (0, 3), (1, 2), (1, 3), (2, 3)]
                for n, (i, j) in enumerate(pair):
                    v.tensor_tensor(cmp6[:, n * 128:(n + 1) * 128], a_(i), a_(j), ALU.is_ge)
                c01, c02, c03, c12, c13, c23 = [cmp6[:, n * 128:(n + 1) * 128] for n in range(6)]
                v.tensor_tensor(sacc[:], c01, c02, ALU.add)
                v.tensor_tensor(sacc[:], sacc[:], c03, ALU.add)
                v.tensor_single_scalar(keep[:, 0:128], sacc[:], 1.5, ALU.is_ge)
                v.tensor_tensor(sacc[:], c12, c13, ALU.add)
                v.tensor_tensor(sacc[:], sacc[:], c01, ALU.subtract)
                v.tensor_single_scalar(keep[:, 128:256], sacc[:], 0.5, ALU.is_ge)
                v.tensor_tensor(sacc[:], c02, c12, ALU.add)
                v.tensor_tensor(sacc[:], c23, sacc[:], ALU.subtract)
                v.tensor_single_scalar(keep[:, 256:384], sacc[:], -0.5, ALU.is_ge)
                v.tensor_tensor(sacc[:], c03, c13, ALU.add)
                v.tensor_tensor(sacc[:], sacc[:], c23, ALU.add)
                v.tensor_single_scalar(sacc[:], sacc[:], 1.5, ALU.is_ge)
                v.tensor_scalar(keep[:, 384:512], sacc[:], -1.0, 1.0, ALU.mult, ALU.add)
                for i in range(4):
                    mmv = v.tensor_tensor(hms_(i), hs_(i), keep[:, i * 128:(i + 1) * 128], ALU.mult)
                tv = sV.inc(mmv)
                sV.wait(p.t, tv)
                sG.wait(p.v)
                for j in range(4):
                    par = j % 2
                    sV.wait(p.t, max(0, sV.n - 1))
                    tp = sP.inc(p.t.transpose(p.bankb(6 + par)[:, 0:128],
                                              hm[:, j * 128:(j + 1) * 128], p.identb[:]))
                    sP.wait(p.v, tp)
                    tv = sV.inc(p.v.tensor_copy(hTt[:, j * 128:(j + 1) * 128], p.bankb(6 + par)[:, 0:128]))
                sV.wait(p.g, tv)
                sG.inc(p.g.dma_start(out=hT_dram[4 * hs: 4 * hs + 4, :, tb: tb + 128].rearrange("j p t -> p j t"),
                                     in_=hTt[:].rearrange("p (j t) -> p j t", j=4)), 16)
      p.barrier()

    if "B" in parts:
      with ExitStack() as st_:
        wds = [p.sbuf(f"f1_wds{i}", [128, E], BF16, stack=st_) for i in range(8)]
        gh = [p.sbuf(f"f1_gh{i}", [128, S], BF16, stack=st_) for i in range(8)]
        for grp in range(8):
            sP.wait(p.g)
            for i in range(8):
                hh = grp * 8 + i
                sW.inc(p.g.dma_start(out=wds[i][:].rearrange("p (c d) -> p c d", c=NCH),
                                     in_=w_down[hh * 128:(hh + 1) * 128, :].rearrange("r (c d) -> r c d", d=128)), 16)
                sG.inc(p.g.dma_start(out=gh[i][:], in_=hT_dram[hh, :, :]), 16)
            sW.wait(p.t)
            sG.wait(p.t)
            for win in range(8):
                dhalf, tsl = win // 4, win % 4
                tb = tsl * 512
                for i in range(8):
                    for dd in range(8):
                        D = dhalf * 8 + dd
                        mm = p.t.matmul(p.bank(dd), wds[i][:, D * 128:(D + 1) * 128],
                                        gh[i][:, tb: tb + 512],
                                        start=(i == 0), stop=(i == 7))
                tp = sP.inc(mm)
                sP.wait(p.v, tp)
                for dd in range(8):
                    D = dhalf * 8 + dd
                    sl = x_fm[:, D * S + tb: D * S + tb + 512]
                    tv = sV.inc(p.v.tensor_tensor(sl, sl, p.bank(dd), ALU.add))
                sV.wait(p.t, tv)
      p.barrier()


def build_l1_ffn(parts="ABLN"):
    import os
    p = Prog()
    nc = p.nc
    x = nc.dram_tensor("x", [S, E], F32, kind="ExternalInput").ap()
    w_gate = nc.dram_tensor("w_gate", [E, HID], F32, kind="ExternalInput").ap()
    w_up = nc.dram_tensor("w_up", [E, HID], F32, kind="ExternalInput").ap()
    w_down = nc.dram_tensor("w_down", [HID, E], F32, kind="ExternalInput").ap()
    y = nc.dram_tensor("y", [S, E], F32, kind="ExternalOutput").ap()
    hkind = "ExternalOutput" if os.environ.get("DBG") else "Internal"
    hT_dram = nc.dram_tensor("hT_scratch", [64, 128, S], BF16, kind=hkind).ap()

    x_fm = p.sbuf("x_fm", [128, NCH * S], F32)
    p.setup_consts()
    emit_load_x(p, x, x_fm)
    emit_l1_ffn(p, x_fm, w_gate, w_up, w_down, hT_dram, parts=parts)
    if "LN" in parts:
        emit_ln(p, x_fm, "ln3")
    sST = emit_store_x(p, x_fm, y)
    sST.wait(p.sy)
    return p.finish()


def build_l0_attn(parts="ABLN"):
    p = Prog()
    nc = p.nc
    x = nc.dram_tensor("x", [S, E], F32, kind="ExternalInput").ap()
    Wq = nc.dram_tensor("Wq", [16, E, 128], F32, kind="ExternalInput").ap()
    bq = nc.dram_tensor("bq", [16, 128], F32, kind="ExternalInput").ap()
    Wk = nc.dram_tensor("Wk", [4, E, 128], F32, kind="ExternalInput").ap()
    bk = nc.dram_tensor("bk", [4, 128], F32, kind="ExternalInput").ap()
    Wv = nc.dram_tensor("Wv", [4, E, 128], F32, kind="ExternalInput").ap()
    bv = nc.dram_tensor("bv", [4, 128], F32, kind="ExternalInput").ap()
    y = nc.dram_tensor("y", [S, E], F32, kind="ExternalOutput").ap()

    x_fm = p.sbuf("x_fm", [128, NCH * S], F32)
    p.setup_consts()
    emit_load_x(p, x, x_fm)
    if "A" in parts:
        import os
        nh = int(os.environ.get("NHEADS", "16"))
        dbg = None
        if os.environ.get("DBG"):
            dbg = {
                "q": nc.dram_tensor("dbg_q", [128, 2 * S], BF16, kind="ExternalOutput").ap(),
                "k": nc.dram_tensor("dbg_k", [128, 4 * S], BF16, kind="ExternalOutput").ap(),
                "v": nc.dram_tensor("dbg_v", [128, 4 * S], BF16, kind="ExternalOutput").ap(),
                "e": nc.dram_tensor("dbg_e", [128, S], BF16, kind="ExternalOutput").ap(),
                "z": nc.dram_tensor("dbg_z", [128, 1], F32, kind="ExternalOutput").ap(),
                "rzbc": nc.dram_tensor("dbg_rzbc", [128, 128], BF16, kind="ExternalOutput").ap(),
            }
            if os.environ.get("DBG_QI") is not None:
                dbg["_h"] = int(os.environ.get("DBG_H", str(nh - 1)))
                dbg["_qi"] = int(os.environ["DBG_QI"])
                dbg["e0"] = nc.dram_tensor("dbg_e0", [128, S], BF16, kind="ExternalOutput").ap()
                dbg["z0"] = nc.dram_tensor("dbg_z0", [128, 1], F32, kind="ExternalOutput").ap()
                dbg["r0"] = nc.dram_tensor("dbg_r0", [128, 128], BF16, kind="ExternalOutput").ap()
                dbg["o0"] = nc.dram_tensor("dbg_o0", [128, 128], F32, kind="ExternalOutput").ap()
                dbg["_eb"] = p.sbuf("dbg_eb", [128, S], BF16)
                dbg["_zb"] = p.sbuf("dbg_zb", [128, 1], F32)
                dbg["_rb"] = p.sbuf("dbg_rb", [128, 128], BF16)
                dbg["_ob"] = p.sbuf("dbg_ob", [128, 128], F32)
                dbg["s0"] = nc.dram_tensor("dbg_s0", [128, 128], F32, kind="ExternalOutput").ap()
                dbg["q0"] = nc.dram_tensor("dbg_q0", [128, 128], BF16, kind="ExternalOutput").ap()
                dbg["_sb"] = p.sbuf("dbg_sb", [128, 128], F32)
                dbg["_qb"] = p.sbuf("dbg_qb", [128, 128], BF16)
        O_dram = nc.dram_tensor("O_scratch", [128, NCH * S], F32, kind="Internal").ap()
        emit_l0_attn(p, x_fm, Wq, bq, Wk, bk, Wv, bv, O_dram=O_dram, parts=parts, nheads=nh, dbg=dbg)
    if "LN" in parts:
        emit_ln(p, x_fm, "ln0")
    sST = emit_store_x(p, x_fm, y)
    sST.wait(p.sy)
    return p.finish()


def kernel(**inputs):
    g = lambda k: np.ascontiguousarray(inputs[k])
    x = g("L0_attn_x")
    n = 8
    cores = list(range(n))

    nc1 = build_l0_attn()
    maps1 = [{"x": np.ascontiguousarray(x[b]), "Wq": g("L0_attn_Wq"), "bq": g("L0_attn_bq"),
              "Wk": g("L0_attn_Wk"), "bk": g("L0_attn_bk"),
              "Wv": g("L0_attn_Wv"), "bv": g("L0_attn_bv")} for b in range(n)]
    r1 = run_bass_kernel_spmd(nc1, maps1, core_ids=cores)
    x1 = [r1.results[b]["y"] for b in range(n)]

    nc2 = build_l0_ffn()
    maps2 = [{"x": x1[b], "w_gate": g("L0_ffn_w_gate"), "w_up": g("L0_ffn_w_up"),
              "w_down": g("L0_ffn_w_down")} for b in range(n)]
    r2 = run_bass_kernel_spmd(nc2, maps2, core_ids=cores)
    x2 = [r2.results[b]["y"] for b in range(n)]

    nc3 = build_l1_attn()
    maps3 = [{"x": x2[b], "w_qkv": g("L1_attn_w_qkv"), "w_proj": g("L1_attn_w_proj"),
              "b_proj": g("L1_attn_b_proj")} for b in range(n)]
    r3 = run_bass_kernel_spmd(nc3, maps3, core_ids=cores)
    x3 = [r3.results[b]["y"] for b in range(n)]

    nc4 = build_l1_ffn()
    maps4 = [{"x": x3[b], "w_gate": g("L1_ffn_w_gate"), "w_up": g("L1_ffn_w_up"),
              "w_down": g("L1_ffn_w_down")} for b in range(n)]
    r4 = run_bass_kernel_spmd(nc4, maps4, core_ids=cores)
    return np.stack([np.asarray(r4.results[b]["y"]) for b in range(n)], axis=0)


# version 03
# speedup vs baseline: 1.0508x; 1.0508x over previous
import numpy as np
from contextlib import ExitStack

import concourse.bass as bass
import concourse.mybir as mybir
from concourse.bass_utils import run_bass_kernel_spmd

F32 = mybir.dt.float32
BF16 = mybir.dt.bfloat16
AF = mybir.ActivationFunctionType
ALU = mybir.AluOpType
AX = mybir.AxisListType

S = 2048
E = 2048
NCH = 16
HID = 8192
QSCALE = float(1.0 / np.sqrt(128.0))
LN_EPS = 1e-5


class Sem:
    __slots__ = ("h", "n")

    def __init__(self, nc, name=None):
        self.h = nc.alloc_semaphore(name=name)
        self.n = 0

    def inc(self, instr, k=1):
        instr.then_inc(self.h, k)
        self.n += k
        return self.n

    def wait(self, eng, tok=None):
        tok = self.n if tok is None else tok
        if tok > 0:
            eng.wait_ge(self.h, tok)


class Prog:
    def __init__(self):
        nc = bass.Bass("TRN2", target_bir_lowering=False)
        self.nc = nc
        self.t = nc.tensor
        self.v = nc.vector
        self.sc = nc.scalar
        self.g = nc.gpsimd
        self.sy = nc.sync
        self.ctx = ExitStack()
        self.sems = {}
        self.PS = self.sbuf_like = None
        self._arena = self.ctx.enter_context(nc.psum_tensor("arena", [128, 8 * 512], F32))

    def bank(self, i, cols=512):
        return self._arena[:, i * 512: i * 512 + cols]

    def bankb(self, i, cols=1024):
        return self._arena[:, i * 512:(i + 1) * 512].bitcast(BF16)[:, 0:cols]

    def bankrow(self, i, cols=512):
        return self._arena[0:1, i * 512: i * 512 + cols]

    def sem(self, name):
        if name not in self.sems:
            self.sems[name] = Sem(self.nc, name)
        return self.sems[name]

    def sbuf(self, name, shape, dtype, stack=None):
        return (stack or self.ctx).enter_context(self.nc.sbuf_tensor(name, shape, dtype))

    def barrier(self):
        self.nc.all_engine_barrier()

    def setup_consts(self):
        p = self
        self.identf = p.sbuf("identf", [128, 128], F32)
        self.identb = p.sbuf("identb", [128, 128], BF16)
        self.mask01 = p.sbuf("mask01", [128, 128], BF16)
        self.ones_col_b = p.sbuf("ones_col_b", [128, 1], BF16)
        self.ones_row_f = p.sbuf("ones_row_f", [1, 128], F32)
        sC = p.sem("sConst")
        g = p.g
        g.memset(self.identf[:], 0.0)
        g.affine_select(out=self.identf[:], in_=self.identf[:],
                        compare_op=ALU.not_equal, fill=1.0, base=0,
                        pattern=[[-1, 128]], channel_multiplier=1)
        g.memset(self.identb[:], 0.0)
        g.affine_select(out=self.identb[:], in_=self.identb[:],
                        compare_op=ALU.not_equal, fill=1.0, base=0,
                        pattern=[[-1, 128]], channel_multiplier=1)
        g.memset(self.mask01[:], 1.0)
        g.affine_select(out=self.mask01[:], in_=self.mask01[:],
                        compare_op=ALU.is_ge, fill=0.0, base=0,
                        pattern=[[-1, 128]], channel_multiplier=1)
        g.memset(self.ones_col_b[:], 1.0)
        sC.inc(g.memset(self.ones_row_f[:], 1.0))
        for eng in (p.t, p.v, p.sc):
            sC.wait(eng)

    def finish(self):
        self.ctx.close()
        return self.nc



def emit_load_x(p, x_dram, x_fm):
    st_ = ExitStack()
    stage = p.sbuf("ld_stage", [128, E], F32, stack=st_)
    sDMA = p.sem("sLdX")
    sPE = p.sem("sLdPE")
    sDVE = p.sem("sLdDVE")
    ctok = [0, 0]
    for tc in range(S // 128):
        tok = sDMA.inc(p.sy.dma_start(out=stage[:], in_=x_dram[tc * 128:(tc + 1) * 128, :]), 16)
        sDMA.wait(p.t, tok)
        for c in range(NCH):
            par = c % 2
            sDVE.wait(p.t, ctok[par])
            tp = sPE.inc(p.t.transpose(p.bank(par, 128), stage[:, c * 128:(c + 1) * 128], p.identf[:]))
            sPE.wait(p.v, tp)
            ctok[par] = sDVE.inc(p.v.tensor_copy(
                x_fm[:, c * S + tc * 128: c * S + tc * 128 + 128], p.bank(par, 128)))
        sPE.wait(p.sy)
    p.barrier()
    st_.close()


def emit_store_x(p, x_fm, y_dram):
    stage = p.sbuf("st_stage", [128, E], F32)
    sPE = p.sem("sStPE")
    sDVE = p.sem("sStDVE")
    sST = p.sem("sStDMA")
    ctok = [0, 0]
    for tc in range(S // 128):
        sST.wait(p.v)
        for c in range(NCH):
            par = c % 2
            sDVE.wait(p.t, ctok[par])
            tp = sPE.inc(p.t.transpose(p.bank(par, 128),
                                       x_fm[:, c * S + tc * 128: c * S + tc * 128 + 128],
                                       p.identf[:]))
            sPE.wait(p.v, tp)
            ctok[par] = sDVE.inc(p.v.tensor_copy(stage[:, c * 128:(c + 1) * 128], p.bank(par, 128)))
        sDVE.wait(p.sy)
        sST.inc(p.sy.dma_start(out=y_dram[tc * 128:(tc + 1) * 128, :], in_=stage[:]), 16)
    return sST


def emit_bias_cols(p, dram_2d, out_tile, rows, scale=None):
    st_ = ExitStack()
    raw = p.sbuf(f"braw_{out_tile.name}", [128, 128], F32, stack=st_)
    sB = p.sem("sBias")
    tokz = sB.inc(p.g.memset(raw[:], 0.0))
    sB.wait(p.sy, tokz)
    tok = sB.inc(p.sy.dma_start(out=raw[0:rows, :], in_=dram_2d), 16)
    sB.wait(p.t, tok)
    tp = sB.inc(p.t.transpose(p.bank(7, 128), raw[:], p.identf[:]))
    sB.wait(p.v, tp)
    if scale is None:
        sB.inc(p.v.tensor_copy(out_tile[:], p.bank(7, 128)[:, 0:rows]))
    else:
        sB.inc(p.v.tensor_scalar_mul(out_tile[:], p.bank(7, 128)[:, 0:rows], scale))
    sB.wait(p.t)
    sB.wait(p.v)
    sB.wait(p.g)
    sB.wait(p.sy)
    st_.close()


def emit_ln(p, x_fm, tag):
    st_ = ExitStack()
    rbf = p.sbuf(f"{tag}_rbf", [128, 512], BF16, stack=st_)
    rsq = p.sbuf(f"{tag}_rsq", [128, 512], BF16, stack=st_)
    rowA = p.sbuf(f"{tag}_rowA", [1, S], F32, stack=st_)
    rowB = p.sbuf(f"{tag}_rowB", [1, S], F32, stack=st_)
    rowC = p.sbuf(f"{tag}_rowC", [1, S], F32, stack=st_)
    sA = p.sem(f"{tag}A")
    sP = p.sem(f"{tag}P")
    sV = p.sem(f"{tag}V")
    ps0 = p.bankrow(0)
    ps1 = p.bankrow(1)
    for tt in range(4):
        for c in range(NCH):
            sP.wait(p.sc)
            p.sc.activation(out=rbf[:], in_=x_fm[:, c * S + tt * 512: c * S + tt * 512 + 512], func=AF.Copy)
            ta = sA.inc(p.sc.activation(out=rsq[:], in_=x_fm[:, c * S + tt * 512: c * S + tt * 512 + 512], func=AF.Square))
            sA.wait(p.t, ta)
            p.t.matmul(ps0, p.ones_col_b[:], rbf[:], start=(c == 0), stop=(c == NCH - 1))
            sP.inc(p.t.matmul(ps1, p.ones_col_b[:], rsq[:], start=(c == 0), stop=(c == NCH - 1)))
        sP.wait(p.v)
        p.v.tensor_copy(rowA[0:1, tt * 512: tt * 512 + 512], ps0)
        tv = sV.inc(p.v.tensor_copy(rowB[0:1, tt * 512: tt * 512 + 512], ps1))
        sV.wait(p.t, tv)
    def step(instr):
        tok = sV.inc(instr)
        sV.wait(p.v, tok)
        return tok
    step(p.v.tensor_scalar_mul(rowA[:], rowA[:], 1.0 / E))
    step(p.v.tensor_mul(rowC[:], rowA[:], rowA[:]))
    tokA = sV.inc(p.v.tensor_scalar_mul(rowA[:], rowA[:], -1.0))
    sV.wait(p.v, tokA)
    tokB = sV.inc(p.v.scalar_tensor_tensor(rowB[:], rowB[:], 1.0 / E, rowC[:], ALU.mult, ALU.subtract))
    sV.wait(p.v, tokB)
    tokB = sV.inc(p.v.tensor_single_scalar(rowB[:], rowB[:], LN_EPS, ALU.add))
    sV.wait(p.sc, tokB)
    ta = sA.inc(p.sc.activation(out=rowB[:], in_=rowB[:], func=AF.Sqrt))
    sA.wait(p.v, ta)
    step(p.v.reciprocal(rowB[:], rowB[:]))
    sV.wait(p.t)
    for tt in range(4):
        sV.wait(p.t)
        p.t.matmul(p.bank(2), p.ones_row_f[:], rowA[0:1, tt * 512: tt * 512 + 512], start=True, stop=True)
        tp = sP.inc(p.t.matmul(p.bank(3), p.ones_row_f[:], rowB[0:1, tt * 512: tt * 512 + 512], start=True, stop=True))
        sP.wait(p.v, tp)
        for c in range(NCH):
            sl = x_fm[:, c * S + tt * 512: c * S + tt * 512 + 512]
            p.v.tensor_tensor(sl, sl, p.bank(2), ALU.add)
            tv = sV.inc(p.v.tensor_tensor(sl, sl, p.bank(3), ALU.mult))
            sV.wait(p.v, tv)
    p.barrier()
    st_.close()



def emit_l0_attn(p, x_fm, Wq, bq, Wk, bk, Wv, bv, O_dram=None, parts="AB", nheads=16, dbg=None):
    stb = ExitStack()
    Kt = p.sbuf("Kt", [128, 4 * S], BF16, stack=stb)
    Vt = p.sbuf("Vt", [128, 4 * S], BF16, stack=stb)
    xb = [p.sbuf(f"at_xb{i}", [128, 512], BF16, stack=stb) for i in range(2)]
    expb = p.sbuf("expb", [128, S], BF16, stack=stb)
    expT = [p.sbuf(f"expT{i}", [128, 512], BF16, stack=stb) for i in range(2)]
    pmax = p.sbuf("pmax", [128, 4], F32, stack=stb)
    negmax = p.sbuf("negmax", [128, 1], F32, stack=stb)
    zcol = p.sbuf("zcol", [128, 1], F32, stack=stb)
    rz = p.sbuf("rz", [128, 1], F32, stack=stb)
    zrow = p.sbuf("zrow", [1, 128], F32, stack=stb)
    rzbc = p.sbuf("rzbc", [128, 128], BF16, stack=stb)
    bqs = p.sbuf("bqs", [128, 16], F32, stack=stb)
    bks = p.sbuf("bks", [128, 4], F32, stack=stb)
    bvs = p.sbuf("bvs", [128, 4], F32, stack=stb)

    sW = p.sem("atW")
    sA = p.sem("atA")
    sP = p.sem("atP")
    sV = p.sem("atV")

    emit_bias_cols(p, bq, bqs, 16, scale=QSCALE)
    emit_bias_cols(p, bk, bks, 4)
    emit_bias_cols(p, bv, bvs, 4)
    p.barrier()

    with ExitStack() as st_:
        wkv = [p.sbuf(f"wkv{i}", [128, E], BF16, stack=st_) for i in range(4)]
        Vf = p.sbuf("Vf", [128, 2 * S], BF16, stack=st_)
        wkv_lastuse = 0
        for sweep in range(2):
            g0 = sweep * 2
            sP.wait(p.g, wkv_lastuse)
            for i, (W, g) in enumerate(((Wk, g0), (Wk, g0 + 1), (Wv, g0), (Wv, g0 + 1))):
                sW.inc(p.g.dma_start(out=wkv[i][:].rearrange("p (c d) -> p c d", c=NCH),
                                     in_=W[g].rearrange("(c p) d -> p c d", p=128)), 16)
            sW.wait(p.t)
            casttok = [0, 0]
            for tt in range(4):
                for c in range(NCH):
                    par = c % 2
                    sP.wait(p.sc, casttok[par])
                    ta = sA.inc(p.sc.activation(out=xb[par][:],
                                                in_=x_fm[:, c * S + tt * 512: c * S + tt * 512 + 512],
                                                func=AF.Copy))
                    sA.wait(p.t, ta)
                    for o in range(4):
                        mm = p.t.matmul(p.bank(o), wkv[o][:, c * 128:(c + 1) * 128], xb[par][:],
                                        start=(c == 0), stop=(c == NCH - 1))
                    casttok[par] = sP.inc(mm)
                wkv_lastuse = sP.n
                sP.wait(p.sc)
                for o in range(4):
                    g = g0 + (o % 2)
                    if o < 2:
                        dst = Kt[:, g * S + tt * 512: g * S + tt * 512 + 512]
                        bias = bks[:, g:g + 1]
                    else:
                        dst = Vf[:, (o - 2) * S + tt * 512: (o - 2) * S + tt * 512 + 512]
                        bias = bvs[:, g:g + 1]
                    ta = sA.inc(p.sc.activation(out=dst, in_=p.bank(o), func=AF.Identity, bias=bias))
                sA.wait(p.t, ta)
            ctokT = [0, 0]
            for gg in range(2):
                g = g0 + gg
                for kt in range(NCH):
                    par = kt % 2
                    sV.wait(p.t, ctokT[par])
                    tp = sP.inc(p.t.transpose(p.bankb(4 + par)[:, 0:128],
                                              Vf[:, gg * S + kt * 128: gg * S + kt * 128 + 128],
                                              p.identb[:]))
                    sP.wait(p.v, tp)
                    ctokT[par] = sV.inc(p.v.tensor_copy(
                        Vt[:, g * S + kt * 128: g * S + kt * 128 + 128], p.bankb(4 + par)[:, 0:128]))
            sV.wait(p.t)
    p.barrier()

    Qh = p.sbuf("Qh", [128, 2 * S], BF16, stack=stb)
    wq = [p.sbuf(f"wq{i}", [128, E], BF16, stack=stb) for i in range(2)]
    ostg = [p.sbuf(f"ostg{i}", [128, 128], F32, stack=stb) for i in range(2)]
    sO = p.sem("atO")
    otok = [0, 0]
    onum = 0
    if "B" not in parts:
        p.barrier()
        stb.close()
        return
    wq_lastuse = 0
    for hp in range(max(0, nheads) // 2):
        heads = (2 * hp, 2 * hp + 1)
        sV.wait(p.t)
        sP.wait(p.g, wq_lastuse)
        for i, h in enumerate(heads):
            sW.inc(p.g.dma_start(out=wq[i][:].rearrange("p (c d) -> p c d", c=NCH),
                                 in_=Wq[h].rearrange("(c p) d -> p c d", p=128)), 16)
        sW.wait(p.t)
        casttok = [0, 0]
        for tt in range(4):
            for c in range(NCH):
                par = c % 2
                sP.wait(p.sc, casttok[par])
                ta = sA.inc(p.sc.activation(out=xb[par][:],
                                            in_=x_fm[:, c * S + tt * 512: c * S + tt * 512 + 512],
                                            func=AF.Copy))
                sA.wait(p.t, ta)
                for i in range(2):
                    mm = p.t.matmul(p.bank(6 + i), wq[i][:, c * 128:(c + 1) * 128], xb[par][:],
                                    start=(c == 0), stop=(c == NCH - 1))
                casttok[par] = sP.inc(mm)
            wq_lastuse = sP.n
            sP.wait(p.sc)
            for i, h in enumerate(heads):
                ta = sA.inc(p.sc.activation(out=Qh[:, i * S + tt * 512: i * S + tt * 512 + 512],
                                            in_=p.bank(6 + i), func=AF.Identity,
                                            scale=QSCALE, bias=bqs[:, h:h + 1]))
            sA.wait(p.t, ta)
        for i, h in enumerate(heads):
            g = h // 4
            exptok = 0
            evactok = 0
            for qi in range(16):
                nk = qi + 1
                ns = (nk + 3) // 4
                sA.wait(p.t)
                sV.wait(p.t, evactok)
                sctok = []
                for s in range(ns):
                    cols = min(512, nk * 128 - s * 512)
                    mm = p.t.matmul(p.bank(s, cols), Qh[:, i * S + qi * 128: i * S + qi * 128 + 128],
                                    Kt[:, g * S + s * 512: g * S + s * 512 + cols],
                                    start=True, stop=True)
                    sctok.append(sP.inc(mm))
                for s in range(ns):
                    cols = min(512, nk * 128 - s * 512)
                    sP.wait(p.v, sctok[s])
                    mr = p.v.reduce_max(pmax[:, s:s + 1], p.bank(s, cols), axis=AX.X)
                tv = sV.inc(mr)
                sV.wait(p.v, tv)
                tv = sV.inc(p.v.reduce_max(negmax[:], pmax[:, 0:ns], axis=AX.X, negate=True))
                if dbg is not None and "s0" in dbg and h == dbg["_h"] and qi == dbg["_qi"]:
                    p.v.tensor_copy(dbg["_sb"][:], p.bank(0, 128))
                    sV.inc(p.v.tensor_copy(dbg["_qb"][:], Qh[:, i * S + qi * 128: i * S + qi * 128 + 128]))
                sV.wait(p.sc, tv)
                sP.wait(p.sc, exptok)
                for s in range(ns):
                    cols = min(512, nk * 128 - s * 512)
                    ta = sA.inc(p.sc.activation(out=expb[:, s * 512: s * 512 + cols],
                                                in_=p.bank(s, cols), func=AF.Exp,
                                                bias=negmax[:, 0:1]))
                sA.wait(p.v, ta)
                dsl = expb[:, qi * 128: qi * 128 + 128]
                tv = sV.inc(p.v.tensor_tensor(dsl, dsl, p.mask01[:], ALU.mult))
                sV.wait(p.v, tv)
                tv = sV.inc(p.v.reduce_sum(zcol[:], expb[:, 0:nk * 128], axis=AX.X))
                sV.wait(p.v, tv)
                tv = sV.inc(p.v.reciprocal(rz[:], zcol[:]))
                sV.wait(p.t, tv)
                tp = sP.inc(p.t.transpose(p.bankrow(4, 128), rz[:], p.identf[:]))
                sP.wait(p.v, tp)
                tv = sV.inc(p.v.tensor_copy(zrow[:], p.bankrow(4, 128)))
                sV.wait(p.t, tv)
                tp = sP.inc(p.t.matmul(p.bank(5, 128), p.ones_row_f[:], zrow[:], start=True, stop=True))
                sP.wait(p.v, tp)
                tv = sV.inc(p.v.tensor_copy(rzbc[:], p.bank(5, 128)))
                sV.wait(p.t, tv)
                ngrp = ns
                Ttok = []
                Mtok = []
                mults_tok = []

                def emit_T(grp):
                    gpar = grp % 2
                    base = grp * 4
                    cnt = min(4, nk - base)
                    if grp >= 2:
                        sV.wait(p.t, mults_tok[grp - 2])
                    for j in range(cnt):
                        tpj = p.t.transpose(p.bankb(4 + gpar)[:, j * 128:(j + 1) * 128],
                                            expb[:, (base + j) * 128:(base + j + 1) * 128],
                                            p.identb[:])
                    Ttok.append(sP.inc(tpj))

                emit_T(0)
                if ngrp > 1:
                    emit_T(1)
                for grp in range(ngrp):
                    gpar = grp % 2
                    base = grp * 4
                    cnt = min(4, nk - base)
                    sP.wait(p.v, Ttok[grp])
                    for j in range(cnt):
                        mv = p.v.tensor_tensor(expT[gpar][:, j * 128:(j + 1) * 128],
                                               p.bankb(4 + gpar)[:, j * 128:(j + 1) * 128],
                                               rzbc[:], ALU.mult)
                    mults_tok.append(sV.inc(mv))
                    sV.wait(p.t, mults_tok[grp])
                    for j in range(cnt):
                        kt = base + j
                        mm = p.t.matmul(p.bank(6, 128), Vt[:, g * S + kt * 128: g * S + kt * 128 + 128],
                                        expT[gpar][:, j * 128:(j + 1) * 128],
                                        start=(kt == 0), stop=(kt == nk - 1))
                    Mtok.append(sP.inc(mm))
                    if grp + 2 < ngrp:
                        emit_T(grp + 2)
                exptok = sP.n
                sP.wait(p.v, Mtok[-1])
                if dbg is not None and "e0" in dbg and h == dbg["_h"] and qi == dbg["_qi"]:
                    p.v.tensor_copy(dbg["_eb"][:], expb[:])
                    p.v.tensor_copy(dbg["_zb"][:], zcol[:])
                    p.v.tensor_copy(dbg["_rb"][:], rzbc[:])
                    sV.inc(p.v.tensor_copy(dbg["_ob"][:], p.bank(6, 128)))
                par = onum % 2
                sO.wait(p.v, otok[par])
                evactok = sV.inc(p.v.tensor_copy(ostg[par][:], p.bank(6, 128)))
                sV.wait(p.sy, evactok)
                otok[par] = sO.inc(p.sy.dma_start(
                    out=O_dram[:, h * S + qi * 128: h * S + qi * 128 + 128], in_=ostg[par][:]), 16)
                onum += 1
    obig = [p.sbuf(f"obig{i}", [128, E], F32, stack=stb) for i in range(2)]
    sO.wait(p.sy)
    btok = [0, 0]
    for c in range(NCH):
        par = c % 2
        sV.wait(p.sy, btok[par])
        tokl = sO.inc(p.sy.dma_start(out=obig[par][:], in_=O_dram[:, c * S:(c + 1) * S]), 16)
        sO.wait(p.v, tokl)
        xsl = x_fm[:, c * S:(c + 1) * S]
        btok[par] = sV.inc(p.v.tensor_tensor(xsl, xsl, obig[par][:], ALU.add))
    sV.wait(p.t)
    p.barrier()
    if dbg is not None and "e0" in dbg:
        sD0 = p.sem("sDbg0")
        sV.wait(p.sy)
        sD0.inc(p.sy.dma_start(out=dbg["e0"], in_=dbg["_eb"][:]), 16)
        sD0.inc(p.sy.dma_start(out=dbg["z0"], in_=dbg["_zb"][:]), 16)
        sD0.inc(p.sy.dma_start(out=dbg["r0"], in_=dbg["_rb"][:]), 16)
        sD0.inc(p.sy.dma_start(out=dbg["o0"], in_=dbg["_ob"][:]), 16)
        if "s0" in dbg:
            sD0.inc(p.sy.dma_start(out=dbg["s0"], in_=dbg["_sb"][:]), 16)
            sD0.inc(p.sy.dma_start(out=dbg["q0"], in_=dbg["_qb"][:]), 16)
        sD0.wait(p.sy)
    if dbg is not None:
        sD = p.sem("sDbg")
        sD.inc(p.sy.dma_start(out=dbg["q"], in_=Qh[:]), 16)
        sD.inc(p.sy.dma_start(out=dbg["k"], in_=Kt[:]), 16)
        sD.inc(p.sy.dma_start(out=dbg["v"], in_=Vt[:]), 16)
        sD.inc(p.sy.dma_start(out=dbg["e"], in_=expb[:]), 16)
        sD.inc(p.sy.dma_start(out=dbg["z"], in_=zcol[:]), 16)
        sD.inc(p.sy.dma_start(out=dbg["rzbc"], in_=rzbc[:]), 16)
        sD.wait(p.sy)
        p.barrier()
    stb.close()



def emit_wT_prep(p, dram_rows_ap, wT, sW, sP, sV, tag, nch=NCH, staging=None,
                 out_off=0, out_stride=128):
    sP.wait(p.g)
    tokw = sW.inc(p.g.dma_start(out=staging[:, 0:nch * 128].rearrange("p (c d) -> p c d", c=nch),
                                in_=dram_rows_ap), 16)
    sW.wait(p.t, tokw)
    for c in range(nch):
        par = c % 2
        sV.wait(p.t, max(0, sV.n - 1))
        tp = sP.inc(p.t.transpose(p.bankb(4 + par)[:, 0:128],
                                  staging[:, c * 128:(c + 1) * 128], p.identb[:]))
        sP.wait(p.v, tp)
        o = out_off + c * out_stride
        tv = sV.inc(p.v.tensor_copy(wT[:, o: o + 128], p.bankb(4 + par)[:, 0:128]))
    return sV.n


def emit_l0_ffn(p, x_fm, w_gate, w_up, w_down, g_dram, parts="AB"):
    sW = p.sem("f0W")
    sA = p.sem("f0A")
    sP = p.sem("f0P")
    sV = p.sem("f0V")
    sG = p.sem("f0G")

    with ExitStack() as st_:
        xbh = p.sbuf("f0_xbh", [128, NCH * 1024], BF16, stack=st_)
        wgT = [p.sbuf(f"f0_wgT{i}", [128, NCH * 128], BF16, stack=st_) for i in range(4)]
        wuT = [p.sbuf(f"f0_wuT{i}", [128, NCH * 128], BF16, stack=st_) for i in range(4)]
        wstg = p.sbuf("f0_wstg", [128, E], BF16, stack=st_)
        sil = p.sbuf("f0_sil", [128, 512], F32, stack=st_)
        gt = p.sbuf("f0_gt", [128, 512], BF16, stack=st_)
        for half in range(2):
            t0 = half * 1024
            sP.wait(p.sc)
            ta = 0
            for c in range(NCH):
                ta = sA.inc(p.sc.activation(out=xbh[:, c * 1024:(c + 1) * 1024],
                                            in_=x_fm[:, c * S + t0: c * S + t0 + 1024],
                                            func=AF.Copy))
            sA.wait(p.t, ta)
            for Hq in range(16):
                sA.wait(p.t)
                sV.wait(p.t)
                for i in range(4):
                    h = Hq * 4 + i
                    emit_wT_prep(p, w_gate[h * 128:(h + 1) * 128, :].rearrange("r (c pp) -> r c pp", pp=128),
                                 wgT[i], sW, sP, sV, "wg", staging=wstg)
                    emit_wT_prep(p, w_up[h * 128:(h + 1) * 128, :].rearrange("r (c pp) -> r c pp", pp=128),
                                 wuT[i], sW, sP, sV, "wu", staging=wstg)
                sV.wait(p.t)
                for tsl in range(2):
                    tb = tsl * 512
                    for c in range(NCH):
                        for i in range(4):
                            p.t.matmul(p.bank(i), wgT[i][:, c * 128:(c + 1) * 128],
                                       xbh[:, c * 1024 + tb: c * 1024 + tb + 512],
                                       start=(c == 0), stop=(c == NCH - 1))
                            mm = p.t.matmul(p.bank(4 + i), wuT[i][:, c * 128:(c + 1) * 128],
                                            xbh[:, c * 1024 + tb: c * 1024 + tb + 512],
                                            start=(c == 0), stop=(c == NCH - 1))
                    tp = sP.inc(mm)
                    for i in range(4):
                        h = Hq * 4 + i
                        sP.wait(p.sc, tp)
                        sV.wait(p.sc)
                        ta = sA.inc(p.sc.activation(out=sil[:], in_=p.bank(i), func=AF.Silu))
                        sA.wait(p.v, ta)
                        tv = sV.inc(p.v.tensor_tensor(gt[:], p.bank(4 + i), sil[:], ALU.mult))
                        sV.wait(p.g, tv)
                        sG.inc(p.g.dma_start(out=g_dram[h, :, t0 + tb: t0 + tb + 512], in_=gt[:]), 16)
                        sG.wait(p.v)
                    sA.wait(p.t)
                    sV.wait(p.t)
    p.barrier()
    if "B" not in parts:
        return

    with ExitStack() as st_:
        wdT = [p.sbuf(f"f0_wdT{i}", [128, NCH * 128], BF16, stack=st_) for i in range(8)]
        gh = [p.sbuf(f"f0_gh{i}", [128, S], BF16, stack=st_) for i in range(8)]
        wdstg = p.sbuf("f0_wdstg", [128, NCH * 128], BF16, stack=st_)
        for grp in range(8):
            sP.wait(p.g)
            for i in range(8):
                h = grp * 8 + i
                emit_wT_prep(p, w_down[:, h * 128:(h + 1) * 128].rearrange("(c pp) h -> pp c h", pp=128),
                             wdT[i], sW, sP, sV, "wd", staging=wdstg)
                sG.inc(p.g.dma_start(out=gh[i][:], in_=g_dram[h, :, :]), 16)
            sV.wait(p.t)
            sG.wait(p.t)
            for win in range(8):
                dhalf, tsl = win // 4, win % 4
                tb = tsl * 512
                for i in range(8):
                    for dd in range(8):
                        D = dhalf * 8 + dd
                        mm = p.t.matmul(p.bank(dd), wdT[i][:, D * 128:(D + 1) * 128],
                                        gh[i][:, tb: tb + 512],
                                        start=(i == 0), stop=(i == 7))
                tp = sP.inc(mm)
                sP.wait(p.v, tp)
                for dd in range(8):
                    D = dhalf * 8 + dd
                    sl = x_fm[:, D * S + tb: D * S + tb + 512]
                    tv = sV.inc(p.v.tensor_tensor(sl, sl, p.bank(dd), ALU.add))
                sV.wait(p.t, tv)
    p.barrier()


def build_l0_ffn(parts="ABLN"):
    import os
    p = Prog()
    nc = p.nc
    x = nc.dram_tensor("x", [S, E], F32, kind="ExternalInput").ap()
    w_gate = nc.dram_tensor("w_gate", [HID, E], F32, kind="ExternalInput").ap()
    w_up = nc.dram_tensor("w_up", [HID, E], F32, kind="ExternalInput").ap()
    w_down = nc.dram_tensor("w_down", [E, HID], F32, kind="ExternalInput").ap()
    y = nc.dram_tensor("y", [S, E], F32, kind="ExternalOutput").ap()
    gkind = "ExternalOutput" if os.environ.get("DBG") else "Internal"
    g_dram = nc.dram_tensor("g_scratch", [64, 128, S], BF16, kind=gkind).ap()

    x_fm = p.sbuf("x_fm", [128, NCH * S], F32)
    p.setup_consts()
    emit_load_x(p, x, x_fm)
    if "A" in parts:
        emit_l0_ffn(p, x_fm, w_gate, w_up, w_down, g_dram, parts=parts)
    if "LN" in parts:
        emit_ln(p, x_fm, "ln1")
    sST = emit_store_x(p, x_fm, y)
    sST.wait(p.sy)
    return p.finish()



def emit_l1_attn(p, x_fm, w_qkv, w_proj, b_proj, parts="12"):
    sW = p.sem("a1W")
    sA = p.sem("a1A")
    sP = p.sem("a1P")
    sV = p.sem("a1V")

    blockdiag = p.sbuf("a1_bd", [128, 16 * 128], BF16)
    ksums = p.sbuf("a1_ks", [128, 16 * 4], BF16)
    sel4 = p.sbuf("a1_sel4", [4, 128], F32)
    bps = p.sbuf("a1_bps", [128, 16], F32)
    sC = p.sem("a1C")
    g = p.g
    g.memset(blockdiag[:], 0.0)
    g.memset(ksums[:], 0.0)
    g.memset(sel4[:], 1.0)
    g.affine_select(out=sel4[:], in_=sel4[:], compare_op=ALU.is_ge, fill=0.0,
                    base=0, pattern=[[1, 128]], channel_multiplier=-32)
    sC.inc(g.affine_select(out=sel4[:], in_=sel4[:], compare_op=ALU.is_ge, fill=0.0,
                           base=31, pattern=[[-1, 128]], channel_multiplier=32))
    for eng in (p.t, p.v, p.sc):
        sC.wait(eng)
    emit_bias_cols(p, b_proj.rearrange("(c pp) -> c pp", pp=128), bps, 16)
    p.barrier()

    if "1" in parts:
      with ExitStack() as st_:
        wkT = p.sbuf("a1_wkT", [128, NCH * 512], BF16, stack=st_)
        wvT = p.sbuf("a1_wvT", [128, NCH * 512], BF16, stack=st_)
        wstg = p.sbuf("a1_wstg", [128, E], BF16, stack=st_)
        xb = p.sbuf("a1_xb", [128, NCH * 128], BF16, stack=st_)
        k_tm = p.sbuf("a1_ktm", [128, 512], BF16, stack=st_)
        vext = p.sbuf("a1_vext", [128, 4 * 132], BF16, stack=st_)
        sC2 = p.sem("a1C2")
        sC2.inc(p.g.memset(vext[:], 1.0))
        sC2.wait(p.sc)
        for si in range(4):
            for rr in range(4):
                krow = 2048 + si * 512 + rr * 128
                vrow = 4096 + si * 512 + rr * 128
                emit_wT_prep(p, w_qkv[krow:krow + 128, :].rearrange("r (c pp) -> r c pp", pp=128),
                             wkT, sW, sP, sV, "wk", staging=wstg,
                             out_off=rr * 128, out_stride=512)
                emit_wT_prep(p, w_qkv[vrow:vrow + 128, :].rearrange("r (c pp) -> r c pp", pp=128),
                             wvT, sW, sP, sV, "wv", staging=wstg,
                             out_off=rr * 128, out_stride=512)
            sV.wait(p.t)
            for t128 in range(16):
                tb = t128 * 128
                sP.wait(p.sc)
                ta = 0
                for c in range(NCH):
                    ta = sA.inc(p.sc.activation(out=xb[:, c * 128:(c + 1) * 128],
                                                in_=x_fm[:, c * S + tb: c * S + tb + 128],
                                                func=AF.Copy))
                sA.wait(p.t, ta)
                for c in range(NCH):
                    p.t.matmul(p.bank(0), xb[:, c * 128:(c + 1) * 128], wkT[:, c * 512:(c + 1) * 512],
                               start=(c == 0), stop=(c == NCH - 1))
                    mm = p.t.matmul(p.bank(1), xb[:, c * 128:(c + 1) * 128], wvT[:, c * 512:(c + 1) * 512],
                                    start=(c == 0), stop=(c == NCH - 1))
                tp = sP.inc(mm)
                sP.wait(p.sc, tp)
                ta = sA.inc(p.sc.activation(out=k_tm[:], in_=p.bank(0), func=AF.Relu))
                for q in range(4):
                    ta = sA.inc(p.sc.activation(out=vext[:, q * 132: q * 132 + 128],
                                                in_=p.bank(1, 512)[:, q * 128:(q + 1) * 128],
                                                func=AF.Copy))
                sA.wait(p.t, ta)
                for q in range(4):
                    mm = p.t.matmul(p.bank(2 + q, 129), k_tm[:, q * 128:(q + 1) * 128],
                                    vext[:, q * 132: q * 132 + 129],
                                    start=(t128 == 0), stop=(t128 == 15))
                sP.inc(mm)
            sP.wait(p.v)
            for q in range(4):
                Q = si * 4 + q
                for qp in range(4):
                    b0 = 32 * qp
                    p.v.tensor_copy(blockdiag[b0:b0 + 32, Q * 128 + b0: Q * 128 + b0 + 32],
                                    p.bank(2 + q, 129)[b0:b0 + 32, b0:b0 + 32])
                    tv = sV.inc(p.v.tensor_copy(ksums[b0:b0 + 32, Q * 4 + qp: Q * 4 + qp + 1],
                                                p.bank(2 + q, 129)[b0:b0 + 32, 128:129]))
            sV.wait(p.t, tv)
      p.barrier()

    if "2" in parts:
      with ExitStack() as st_:
        xb5 = p.sbuf("a1_xb5", [128, NCH * 512], BF16, stack=st_)
        wT = p.sbuf("a1_wT", [128, NCH * 128], BF16, stack=st_)
        wstg = p.sbuf("a1_wstg2", [128, E], BF16, stack=st_)
        q4 = p.sbuf("a1_q4", [128, 512], BF16, stack=st_)
        den_r = p.sbuf("a1_denr", [4, 512], F32, stack=st_)
        denb_s = p.sbuf("a1_denb", [128, 512], F32, stack=st_)
        o_fm = p.sbuf("a1_ofm", [128, NCH * 512], BF16, stack=st_)
        for tt in range(4):
            tb = tt * 512
            sP.wait(p.sc)
            ta = 0
            for c in range(NCH):
                ta = sA.inc(p.sc.activation(out=xb5[:, c * 512:(c + 1) * 512],
                                            in_=x_fm[:, c * S + tb: c * S + tb + 512],
                                            func=AF.Copy))
            for Q in range(16):
                emit_wT_prep(p, w_qkv[Q * 128:(Q + 1) * 128, :].rearrange("r (c pp) -> r c pp", pp=128),
                             wT, sW, sP, sV, "wq", staging=wstg)
                sV.wait(p.t)
                sA.wait(p.t, ta)
                for c in range(NCH):
                    mm = p.t.matmul(p.bank(0), wT[:, c * 128:(c + 1) * 128],
                                    xb5[:, c * 512:(c + 1) * 512],
                                    start=(c == 0), stop=(c == NCH - 1))
                tp = sP.inc(mm)
                sP.wait(p.sc, tp)
                tq = sA.inc(p.sc.activation(out=q4[:], in_=p.bank(0), func=AF.Relu))
                sA.wait(p.t, tq)
                p.t.matmul(p.bank(1), blockdiag[:, Q * 128:(Q + 1) * 128], q4[:],
                           start=True, stop=True)
                tp = sP.inc(p.t.matmul(p.bank(2, 512)[0:4, :], ksums[:, Q * 4:(Q + 1) * 4], q4[:],
                                       start=True, stop=True))
                sP.wait(p.v, tp)
                tv = sV.inc(p.v.tensor_single_scalar(den_r[:], p.bank(2, 512)[0:4, :], 1e-15, ALU.add))
                sV.wait(p.v, tv)
                tv = sV.inc(p.v.reciprocal(den_r[:], den_r[:]))
                sV.wait(p.t, tv)
                tp = sP.inc(p.t.matmul(p.bank(3), sel4[:], den_r[:], start=True, stop=True))
                sP.wait(p.v, tp)
                tv = sV.inc(p.v.tensor_copy(denb_s[:], p.bank(3)))
                sV.wait(p.v, tv)
                tv = sV.inc(p.v.tensor_tensor(o_fm[:, Q * 512:(Q + 1) * 512], p.bank(1), denb_s[:], ALU.mult))
                sV.wait(p.t, tv)
            for D in range(NCH):
                emit_wT_prep(p, w_proj[D * 128:(D + 1) * 128, :].rearrange("r (c pp) -> r c pp", pp=128),
                             wT, sW, sP, sV, "wp", staging=wstg)
                sV.wait(p.t)
                for c in range(NCH):
                    mm = p.t.matmul(p.bank(0), wT[:, c * 128:(c + 1) * 128],
                                    o_fm[:, c * 512:(c + 1) * 512],
                                    start=(c == 0), stop=(c == NCH - 1))
                tp = sP.inc(mm)
                sP.wait(p.v, tp)
                sl = x_fm[:, D * S + tb: D * S + tb + 512]
                tv = sV.inc(p.v.scalar_tensor_tensor(sl, p.bank(0), bps[:, D:D + 1], sl,
                                                     ALU.add, ALU.add))
                sV.wait(p.t, tv)
      p.barrier()


def build_l1_attn(parts="12LN"):
    p = Prog()
    nc = p.nc
    x = nc.dram_tensor("x", [S, E], F32, kind="ExternalInput").ap()
    w_qkv = nc.dram_tensor("w_qkv", [3 * E, E], F32, kind="ExternalInput").ap()
    w_proj = nc.dram_tensor("w_proj", [E, E], F32, kind="ExternalInput").ap()
    b_proj = nc.dram_tensor("b_proj", [E], F32, kind="ExternalInput").ap()
    y = nc.dram_tensor("y", [S, E], F32, kind="ExternalOutput").ap()

    x_fm = p.sbuf("x_fm", [128, NCH * S], F32)
    p.setup_consts()
    emit_load_x(p, x, x_fm)
    emit_l1_attn(p, x_fm, w_qkv, w_proj, b_proj, parts=parts)
    if "LN" in parts:
        emit_ln(p, x_fm, "ln2")
    sST = emit_store_x(p, x_fm, y)
    sST.wait(p.sy)
    return p.finish()



def emit_l1_ffn(p, x_fm, w_gate, w_up, w_down, hT_dram, parts="AB"):
    sW = p.sem("f1W")
    sA = p.sem("f1A")
    sP = p.sem("f1P")
    sV = p.sem("f1V")
    sG = p.sem("f1G")

    if "A" in parts:
      with ExitStack() as st_:
        wg_sl = p.sbuf("f1_wg", [128, NCH * 512], BF16, stack=st_)
        wu_sl = p.sbuf("f1_wu", [128, NCH * 512], BF16, stack=st_)
        xb = p.sbuf("f1_xb", [128, NCH * 128], BF16, stack=st_)
        sil = p.sbuf("f1_sil", [128, 512], F32, stack=st_)
        h = p.sbuf("f1_h", [128, 512], F32, stack=st_)
        a4 = p.sbuf("f1_a4", [128, 512], F32, stack=st_)
        cmp6 = p.sbuf("f1_cmp", [128, 6 * 128], F32, stack=st_)
        sacc = p.sbuf("f1_sacc", [128, 128], F32, stack=st_)
        keep = p.sbuf("f1_keep", [128, 4 * 128], F32, stack=st_)
        hm = p.sbuf("f1_hm", [128, 512], BF16, stack=st_)
        hTt = p.sbuf("f1_hTt", [128, 512], BF16, stack=st_)

        def hs_(i):
            return h[:].rearrange("p (g f) -> p g f", f=4)[:, :, i:i + 1].rearrange("p g o -> p (g o)")

        def hms_(i):
            return hm[:].rearrange("p (g f) -> p g f", f=4)[:, :, i:i + 1].rearrange("p g o -> p (g o)")

        for hs in range(16):
            sP.wait(p.g)
            tokw = sW.inc(p.g.dma_start(
                out=wg_sl[:].rearrange("p (c i) -> p c i", c=NCH),
                in_=w_gate.rearrange("(c p) i -> p c i", p=128)[:, :, hs * 512:(hs + 1) * 512]), 16)
            tokw = sW.inc(p.g.dma_start(
                out=wu_sl[:].rearrange("p (c i) -> p c i", c=NCH),
                in_=w_up.rearrange("(c p) i -> p c i", p=128)[:, :, hs * 512:(hs + 1) * 512]), 16)
            sW.wait(p.t, tokw)
            for t128 in range(16):
                tb = t128 * 128
                sP.wait(p.sc)
                ta = 0
                for c in range(NCH):
                    ta = sA.inc(p.sc.activation(out=xb[:, c * 128:(c + 1) * 128],
                                                in_=x_fm[:, c * S + tb: c * S + tb + 128],
                                                func=AF.Copy))
                sA.wait(p.t, ta)
                for c in range(NCH):
                    p.t.matmul(p.bank(0), xb[:, c * 128:(c + 1) * 128], wg_sl[:, c * 512:(c + 1) * 512],
                               start=(c == 0), stop=(c == NCH - 1))
                    mm = p.t.matmul(p.bank(1), xb[:, c * 128:(c + 1) * 128], wu_sl[:, c * 512:(c + 1) * 512],
                                    start=(c == 0), stop=(c == NCH - 1))
                tp = sP.inc(mm)
                sP.wait(p.sc, tp)
                sV.wait(p.sc)
                ta = sA.inc(p.sc.activation(out=sil[:], in_=p.bank(0), func=AF.Silu))
                sA.wait(p.v, ta)
                tv = sV.inc(p.v.tensor_tensor(h[:], p.bank(1), sil[:], ALU.mult))
                sV.wait(p.t, tv)
                v = p.v
                sV.wait(p.sc)
                ta = sA.inc(p.sc.activation(out=a4[:], in_=h[:], func=AF.Abs))
                sA.wait(p.v, ta)
                def a_(i):
                    return a4[:].rearrange("p (g f) -> p g f", f=4)[:, :, i:i + 1].rearrange("p g o -> p (g o)")
                pair = [(0, 1), (0, 2), (0, 3), (1, 2), (1, 3), (2, 3)]
                for n, (i, j) in enumerate(pair):
                    v.tensor_tensor(cmp6[:, n * 128:(n + 1) * 128], a_(i), a_(j), ALU.is_ge)
                c01, c02, c03, c12, c13, c23 = [cmp6[:, n * 128:(n + 1) * 128] for n in range(6)]
                v.tensor_tensor(sacc[:], c01, c02, ALU.add)
                v.tensor_tensor(sacc[:], sacc[:], c03, ALU.add)
                v.tensor_single_scalar(keep[:, 0:128], sacc[:], 1.5, ALU.is_ge)
                v.tensor_tensor(sacc[:], c12, c13, ALU.add)
                v.tensor_tensor(sacc[:], sacc[:], c01, ALU.subtract)
                v.tensor_single_scalar(keep[:, 128:256], sacc[:], 0.5, ALU.is_ge)
                v.tensor_tensor(sacc[:], c02, c12, ALU.add)
                v.tensor_tensor(sacc[:], c23, sacc[:], ALU.subtract)
                v.tensor_single_scalar(keep[:, 256:384], sacc[:], -0.5, ALU.is_ge)
                v.tensor_tensor(sacc[:], c03, c13, ALU.add)
                v.tensor_tensor(sacc[:], sacc[:], c23, ALU.add)
                v.tensor_single_scalar(sacc[:], sacc[:], 1.5, ALU.is_ge)
                v.tensor_scalar(keep[:, 384:512], sacc[:], -1.0, 1.0, ALU.mult, ALU.add)
                for i in range(4):
                    mmv = v.tensor_tensor(hms_(i), hs_(i), keep[:, i * 128:(i + 1) * 128], ALU.mult)
                tv = sV.inc(mmv)
                sV.wait(p.t, tv)
                sG.wait(p.v)
                for j in range(4):
                    par = j % 2
                    sV.wait(p.t, max(0, sV.n - 1))
                    tp = sP.inc(p.t.transpose(p.bankb(6 + par)[:, 0:128],
                                              hm[:, j * 128:(j + 1) * 128], p.identb[:]))
                    sP.wait(p.v, tp)
                    tv = sV.inc(p.v.tensor_copy(hTt[:, j * 128:(j + 1) * 128], p.bankb(6 + par)[:, 0:128]))
                sV.wait(p.g, tv)
                sG.inc(p.g.dma_start(out=hT_dram[4 * hs: 4 * hs + 4, :, tb: tb + 128].rearrange("j p t -> p j t"),
                                     in_=hTt[:].rearrange("p (j t) -> p j t", j=4)), 16)
      p.barrier()

    if "B" in parts:
      with ExitStack() as st_:
        wds = [p.sbuf(f"f1_wds{i}", [128, E], BF16, stack=st_) for i in range(8)]
        gh = [p.sbuf(f"f1_gh{i}", [128, S], BF16, stack=st_) for i in range(8)]
        for grp in range(8):
            sP.wait(p.g)
            for i in range(8):
                hh = grp * 8 + i
                sW.inc(p.g.dma_start(out=wds[i][:].rearrange("p (c d) -> p c d", c=NCH),
                                     in_=w_down[hh * 128:(hh + 1) * 128, :].rearrange("r (c d) -> r c d", d=128)), 16)
                sG.inc(p.g.dma_start(out=gh[i][:], in_=hT_dram[hh, :, :]), 16)
            sW.wait(p.t)
            sG.wait(p.t)
            for win in range(8):
                dhalf, tsl = win // 4, win % 4
                tb = tsl * 512
                for i in range(8):
                    for dd in range(8):
                        D = dhalf * 8 + dd
                        mm = p.t.matmul(p.bank(dd), wds[i][:, D * 128:(D + 1) * 128],
                                        gh[i][:, tb: tb + 512],
                                        start=(i == 0), stop=(i == 7))
                tp = sP.inc(mm)
                sP.wait(p.v, tp)
                for dd in range(8):
                    D = dhalf * 8 + dd
                    sl = x_fm[:, D * S + tb: D * S + tb + 512]
                    tv = sV.inc(p.v.tensor_tensor(sl, sl, p.bank(dd), ALU.add))
                sV.wait(p.t, tv)
      p.barrier()


def build_l1_ffn(parts="ABLN"):
    import os
    p = Prog()
    nc = p.nc
    x = nc.dram_tensor("x", [S, E], F32, kind="ExternalInput").ap()
    w_gate = nc.dram_tensor("w_gate", [E, HID], F32, kind="ExternalInput").ap()
    w_up = nc.dram_tensor("w_up", [E, HID], F32, kind="ExternalInput").ap()
    w_down = nc.dram_tensor("w_down", [HID, E], F32, kind="ExternalInput").ap()
    y = nc.dram_tensor("y", [S, E], F32, kind="ExternalOutput").ap()
    hkind = "ExternalOutput" if os.environ.get("DBG") else "Internal"
    hT_dram = nc.dram_tensor("hT_scratch", [64, 128, S], BF16, kind=hkind).ap()

    x_fm = p.sbuf("x_fm", [128, NCH * S], F32)
    p.setup_consts()
    emit_load_x(p, x, x_fm)
    emit_l1_ffn(p, x_fm, w_gate, w_up, w_down, hT_dram, parts=parts)
    if "LN" in parts:
        emit_ln(p, x_fm, "ln3")
    sST = emit_store_x(p, x_fm, y)
    sST.wait(p.sy)
    return p.finish()


def build_l0_attn(parts="ABLN"):
    p = Prog()
    nc = p.nc
    x = nc.dram_tensor("x", [S, E], F32, kind="ExternalInput").ap()
    Wq = nc.dram_tensor("Wq", [16, E, 128], F32, kind="ExternalInput").ap()
    bq = nc.dram_tensor("bq", [16, 128], F32, kind="ExternalInput").ap()
    Wk = nc.dram_tensor("Wk", [4, E, 128], F32, kind="ExternalInput").ap()
    bk = nc.dram_tensor("bk", [4, 128], F32, kind="ExternalInput").ap()
    Wv = nc.dram_tensor("Wv", [4, E, 128], F32, kind="ExternalInput").ap()
    bv = nc.dram_tensor("bv", [4, 128], F32, kind="ExternalInput").ap()
    y = nc.dram_tensor("y", [S, E], F32, kind="ExternalOutput").ap()

    x_fm = p.sbuf("x_fm", [128, NCH * S], F32)
    p.setup_consts()
    emit_load_x(p, x, x_fm)
    if "A" in parts:
        import os
        nh = int(os.environ.get("NHEADS", "16"))
        dbg = None
        if os.environ.get("DBG"):
            dbg = {
                "q": nc.dram_tensor("dbg_q", [128, 2 * S], BF16, kind="ExternalOutput").ap(),
                "k": nc.dram_tensor("dbg_k", [128, 4 * S], BF16, kind="ExternalOutput").ap(),
                "v": nc.dram_tensor("dbg_v", [128, 4 * S], BF16, kind="ExternalOutput").ap(),
                "e": nc.dram_tensor("dbg_e", [128, S], BF16, kind="ExternalOutput").ap(),
                "z": nc.dram_tensor("dbg_z", [128, 1], F32, kind="ExternalOutput").ap(),
                "rzbc": nc.dram_tensor("dbg_rzbc", [128, 128], BF16, kind="ExternalOutput").ap(),
            }
            if os.environ.get("DBG_QI") is not None:
                dbg["_h"] = int(os.environ.get("DBG_H", str(nh - 1)))
                dbg["_qi"] = int(os.environ["DBG_QI"])
                dbg["e0"] = nc.dram_tensor("dbg_e0", [128, S], BF16, kind="ExternalOutput").ap()
                dbg["z0"] = nc.dram_tensor("dbg_z0", [128, 1], F32, kind="ExternalOutput").ap()
                dbg["r0"] = nc.dram_tensor("dbg_r0", [128, 128], BF16, kind="ExternalOutput").ap()
                dbg["o0"] = nc.dram_tensor("dbg_o0", [128, 128], F32, kind="ExternalOutput").ap()
                dbg["_eb"] = p.sbuf("dbg_eb", [128, S], BF16)
                dbg["_zb"] = p.sbuf("dbg_zb", [128, 1], F32)
                dbg["_rb"] = p.sbuf("dbg_rb", [128, 128], BF16)
                dbg["_ob"] = p.sbuf("dbg_ob", [128, 128], F32)
                dbg["s0"] = nc.dram_tensor("dbg_s0", [128, 128], F32, kind="ExternalOutput").ap()
                dbg["q0"] = nc.dram_tensor("dbg_q0", [128, 128], BF16, kind="ExternalOutput").ap()
                dbg["_sb"] = p.sbuf("dbg_sb", [128, 128], F32)
                dbg["_qb"] = p.sbuf("dbg_qb", [128, 128], BF16)
        O_dram = nc.dram_tensor("O_scratch", [128, NCH * S], F32, kind="Internal").ap()
        emit_l0_attn(p, x_fm, Wq, bq, Wk, bk, Wv, bv, O_dram=O_dram, parts=parts, nheads=nh, dbg=dbg)
    if "LN" in parts:
        emit_ln(p, x_fm, "ln0")
    sST = emit_store_x(p, x_fm, y)
    sST.wait(p.sy)
    return p.finish()


def build_fused():
    p = Prog()
    nc = p.nc
    x = nc.dram_tensor("x", [S, E], F32, kind="ExternalInput").ap()
    Wq = nc.dram_tensor("Wq", [16, E, 128], F32, kind="ExternalInput").ap()
    bq = nc.dram_tensor("bq", [16, 128], F32, kind="ExternalInput").ap()
    Wk = nc.dram_tensor("Wk", [4, E, 128], F32, kind="ExternalInput").ap()
    bk = nc.dram_tensor("bk", [4, 128], F32, kind="ExternalInput").ap()
    Wv = nc.dram_tensor("Wv", [4, E, 128], F32, kind="ExternalInput").ap()
    bv = nc.dram_tensor("bv", [4, 128], F32, kind="ExternalInput").ap()
    w_gate0 = nc.dram_tensor("w_gate0", [HID, E], F32, kind="ExternalInput").ap()
    w_up0 = nc.dram_tensor("w_up0", [HID, E], F32, kind="ExternalInput").ap()
    w_down0 = nc.dram_tensor("w_down0", [E, HID], F32, kind="ExternalInput").ap()
    w_qkv = nc.dram_tensor("w_qkv", [3 * E, E], F32, kind="ExternalInput").ap()
    w_proj = nc.dram_tensor("w_proj", [E, E], F32, kind="ExternalInput").ap()
    b_proj = nc.dram_tensor("b_proj", [E], F32, kind="ExternalInput").ap()
    w_gate1 = nc.dram_tensor("w_gate1", [E, HID], F32, kind="ExternalInput").ap()
    w_up1 = nc.dram_tensor("w_up1", [E, HID], F32, kind="ExternalInput").ap()
    w_down1 = nc.dram_tensor("w_down1", [HID, E], F32, kind="ExternalInput").ap()
    y = nc.dram_tensor("y", [S, E], F32, kind="ExternalOutput").ap()
    O_dram = nc.dram_tensor("O_scratch", [128, NCH * S], F32, kind="Internal").ap()
    g_dram = nc.dram_tensor("g_scratch", [64, 128, S], BF16, kind="Internal").ap()
    hT_dram = nc.dram_tensor("hT_scratch", [64, 128, S], BF16, kind="Internal").ap()

    x_fm = p.sbuf("x_fm", [128, NCH * S], F32)
    p.setup_consts()
    emit_load_x(p, x, x_fm)
    emit_l0_attn(p, x_fm, Wq, bq, Wk, bk, Wv, bv, O_dram=O_dram)
    emit_ln(p, x_fm, "ln0")
    emit_l0_ffn(p, x_fm, w_gate0, w_up0, w_down0, g_dram)
    emit_ln(p, x_fm, "ln1")
    emit_l1_attn(p, x_fm, w_qkv, w_proj, b_proj)
    emit_ln(p, x_fm, "ln2")
    emit_l1_ffn(p, x_fm, w_gate1, w_up1, w_down1, hT_dram)
    emit_ln(p, x_fm, "ln3")
    sST = emit_store_x(p, x_fm, y)
    sST.wait(p.sy)
    return p.finish()


def kernel(**inputs):
    g = lambda k: np.ascontiguousarray(inputs[k])
    x = g("L0_attn_x")
    n = 8
    nc = build_fused()
    maps = [{"x": np.ascontiguousarray(x[b]),
             "Wq": g("L0_attn_Wq"), "bq": g("L0_attn_bq"),
             "Wk": g("L0_attn_Wk"), "bk": g("L0_attn_bk"),
             "Wv": g("L0_attn_Wv"), "bv": g("L0_attn_bv"),
             "w_gate0": g("L0_ffn_w_gate"), "w_up0": g("L0_ffn_w_up"),
             "w_down0": g("L0_ffn_w_down"),
             "w_qkv": g("L1_attn_w_qkv"), "w_proj": g("L1_attn_w_proj"),
             "b_proj": g("L1_attn_b_proj"),
             "w_gate1": g("L1_ffn_w_gate"), "w_up1": g("L1_ffn_w_up"),
             "w_down1": g("L1_ffn_w_down")} for b in range(n)]
    r = run_bass_kernel_spmd(nc, maps, core_ids=list(range(n)))
    return np.stack([np.asarray(r.results[b]["y"]) for b in range(n)], axis=0)


def kernel_unfused(**inputs):
    g = lambda k: np.ascontiguousarray(inputs[k])
    x = g("L0_attn_x")
    n = 8
    cores = list(range(n))

    nc1 = build_l0_attn()
    maps1 = [{"x": np.ascontiguousarray(x[b]), "Wq": g("L0_attn_Wq"), "bq": g("L0_attn_bq"),
              "Wk": g("L0_attn_Wk"), "bk": g("L0_attn_bk"),
              "Wv": g("L0_attn_Wv"), "bv": g("L0_attn_bv")} for b in range(n)]
    r1 = run_bass_kernel_spmd(nc1, maps1, core_ids=cores)
    x1 = [r1.results[b]["y"] for b in range(n)]

    nc2 = build_l0_ffn()
    maps2 = [{"x": x1[b], "w_gate": g("L0_ffn_w_gate"), "w_up": g("L0_ffn_w_up"),
              "w_down": g("L0_ffn_w_down")} for b in range(n)]
    r2 = run_bass_kernel_spmd(nc2, maps2, core_ids=cores)
    x2 = [r2.results[b]["y"] for b in range(n)]

    nc3 = build_l1_attn()
    maps3 = [{"x": x2[b], "w_qkv": g("L1_attn_w_qkv"), "w_proj": g("L1_attn_w_proj"),
              "b_proj": g("L1_attn_b_proj")} for b in range(n)]
    r3 = run_bass_kernel_spmd(nc3, maps3, core_ids=cores)
    x3 = [r3.results[b]["y"] for b in range(n)]

    nc4 = build_l1_ffn()
    maps4 = [{"x": x3[b], "w_gate": g("L1_ffn_w_gate"), "w_up": g("L1_ffn_w_up"),
              "w_down": g("L1_ffn_w_down")} for b in range(n)]
    r4 = run_bass_kernel_spmd(nc4, maps4, core_ids=cores)
    return np.stack([np.asarray(r4.results[b]["y"]) for b in range(n)], axis=0)


# version 06
# speedup vs baseline: 1.3801x; 1.3133x over previous
import numpy as np
from contextlib import ExitStack

import concourse.bass as bass
import concourse.mybir as mybir
from concourse.bass_utils import run_bass_kernel_spmd

F32 = mybir.dt.float32
BF16 = mybir.dt.bfloat16
AF = mybir.ActivationFunctionType
ALU = mybir.AluOpType
AX = mybir.AxisListType

S = 2048
E = 2048
NCH = 16
HID = 8192
QSCALE = float(1.0 / np.sqrt(128.0))
LN_EPS = 1e-5


class Sem:
    __slots__ = ("h", "n")

    def __init__(self, nc, name=None):
        self.h = nc.alloc_semaphore(name=name)
        self.n = 0

    def inc(self, instr, k=1):
        instr.then_inc(self.h, k)
        self.n += k
        return self.n

    def wait(self, eng, tok=None):
        tok = self.n if tok is None else tok
        if tok > 0:
            eng.wait_ge(self.h, tok)


class Prog:
    def __init__(self):
        nc = bass.Bass("TRN2", target_bir_lowering=False)
        self.nc = nc
        self.t = nc.tensor
        self.v = nc.vector
        self.sc = nc.scalar
        self.g = nc.gpsimd
        self.sy = nc.sync
        self.ctx = ExitStack()
        self.sems = {}
        self.PS = self.sbuf_like = None
        self._arena = self.ctx.enter_context(nc.psum_tensor("arena", [128, 8 * 512], F32))

    def bank(self, i, cols=512):
        return self._arena[:, i * 512: i * 512 + cols]

    def bankb(self, i, cols=1024):
        return self._arena[:, i * 512:(i + 1) * 512].bitcast(BF16)[:, 0:cols]

    def bankrow(self, i, cols=512):
        return self._arena[0:1, i * 512: i * 512 + cols]

    def sem(self, name):
        if name not in self.sems:
            self.sems[name] = Sem(self.nc, name)
        return self.sems[name]

    def sbuf(self, name, shape, dtype, stack=None):
        return (stack or self.ctx).enter_context(self.nc.sbuf_tensor(name, shape, dtype))

    def barrier(self):
        self.nc.all_engine_barrier()

    def setup_consts(self):
        p = self
        self.identf = p.sbuf("identf", [128, 128], F32)
        self.identb = p.sbuf("identb", [128, 128], BF16)
        self.mask01 = p.sbuf("mask01", [128, 128], BF16)
        self.ones_col_b = p.sbuf("ones_col_b", [128, 1], BF16)
        self.ones_row_f = p.sbuf("ones_row_f", [1, 128], F32)
        sC = p.sem("sConst")
        g = p.g
        g.memset(self.identf[:], 0.0)
        g.affine_select(out=self.identf[:], in_=self.identf[:],
                        compare_op=ALU.not_equal, fill=1.0, base=0,
                        pattern=[[-1, 128]], channel_multiplier=1)
        g.memset(self.identb[:], 0.0)
        g.affine_select(out=self.identb[:], in_=self.identb[:],
                        compare_op=ALU.not_equal, fill=1.0, base=0,
                        pattern=[[-1, 128]], channel_multiplier=1)
        g.memset(self.mask01[:], 1.0)
        g.affine_select(out=self.mask01[:], in_=self.mask01[:],
                        compare_op=ALU.is_ge, fill=0.0, base=0,
                        pattern=[[-1, 128]], channel_multiplier=1)
        g.memset(self.ones_col_b[:], 1.0)
        sC.inc(g.memset(self.ones_row_f[:], 1.0))
        for eng in (p.t, p.v, p.sc):
            sC.wait(eng)

    def finish(self):
        self.ctx.close()
        return self.nc



def emit_load_x(p, x_dram, x_fm):
    st_ = ExitStack()
    stage = p.sbuf("ld_stage", [128, E], F32, stack=st_)
    sDMA = p.sem("sLdX")
    sPE = p.sem("sLdPE")
    sDVE = p.sem("sLdDVE")
    ctok = [0, 0]
    for tc in range(S // 128):
        tok = sDMA.inc(p.sy.dma_start(out=stage[:], in_=x_dram[tc * 128:(tc + 1) * 128, :]), 16)
        sDMA.wait(p.t, tok)
        for c in range(NCH):
            par = c % 2
            sDVE.wait(p.t, ctok[par])
            tp = sPE.inc(p.t.transpose(p.bank(par, 128), stage[:, c * 128:(c + 1) * 128], p.identf[:]))
            sPE.wait(p.v, tp)
            ctok[par] = sDVE.inc(p.v.tensor_copy(
                x_fm[:, c * S + tc * 128: c * S + tc * 128 + 128], p.bank(par, 128)))
        sPE.wait(p.sy)
    p.barrier()
    st_.close()


def emit_store_x(p, x_fm, y_dram):
    stage = p.sbuf("st_stage", [128, E], F32)
    sPE = p.sem("sStPE")
    sDVE = p.sem("sStDVE")
    sST = p.sem("sStDMA")
    ctok = [0, 0]
    for tc in range(S // 128):
        sST.wait(p.v)
        for c in range(NCH):
            par = c % 2
            sDVE.wait(p.t, ctok[par])
            tp = sPE.inc(p.t.transpose(p.bank(par, 128),
                                       x_fm[:, c * S + tc * 128: c * S + tc * 128 + 128],
                                       p.identf[:]))
            sPE.wait(p.v, tp)
            ctok[par] = sDVE.inc(p.v.tensor_copy(stage[:, c * 128:(c + 1) * 128], p.bank(par, 128)))
        sDVE.wait(p.sy)
        sST.inc(p.sy.dma_start(out=y_dram[tc * 128:(tc + 1) * 128, :], in_=stage[:]), 16)
    return sST


def emit_bias_cols(p, dram_2d, out_tile, rows, scale=None):
    st_ = ExitStack()
    raw = p.sbuf(f"braw_{out_tile.name}", [128, 128], F32, stack=st_)
    sB = p.sem("sBias")
    tokz = sB.inc(p.g.memset(raw[:], 0.0))
    sB.wait(p.sy, tokz)
    tok = sB.inc(p.sy.dma_start(out=raw[0:rows, :], in_=dram_2d), 16)
    sB.wait(p.t, tok)
    tp = sB.inc(p.t.transpose(p.bank(7, 128), raw[:], p.identf[:]))
    sB.wait(p.v, tp)
    if scale is None:
        sB.inc(p.v.tensor_copy(out_tile[:], p.bank(7, 128)[:, 0:rows]))
    else:
        sB.inc(p.v.tensor_scalar_mul(out_tile[:], p.bank(7, 128)[:, 0:rows], scale))
    sB.wait(p.t)
    sB.wait(p.v)
    sB.wait(p.g)
    sB.wait(p.sy)
    st_.close()


def emit_ln(p, x_fm, tag):
    st_ = ExitStack()
    rbf = p.sbuf(f"{tag}_rbf", [128, 512], BF16, stack=st_)
    rsq = p.sbuf(f"{tag}_rsq", [128, 512], BF16, stack=st_)
    rowA = p.sbuf(f"{tag}_rowA", [1, S], F32, stack=st_)
    rowB = p.sbuf(f"{tag}_rowB", [1, S], F32, stack=st_)
    rowC = p.sbuf(f"{tag}_rowC", [1, S], F32, stack=st_)
    sA = p.sem(f"{tag}A")
    sP = p.sem(f"{tag}P")
    sV = p.sem(f"{tag}V")
    ps0 = p.bankrow(0)
    ps1 = p.bankrow(1)
    for tt in range(4):
        for c in range(NCH):
            sP.wait(p.sc)
            p.sc.activation(out=rbf[:], in_=x_fm[:, c * S + tt * 512: c * S + tt * 512 + 512], func=AF.Copy)
            ta = sA.inc(p.sc.activation(out=rsq[:], in_=x_fm[:, c * S + tt * 512: c * S + tt * 512 + 512], func=AF.Square))
            sA.wait(p.t, ta)
            p.t.matmul(ps0, p.ones_col_b[:], rbf[:], start=(c == 0), stop=(c == NCH - 1))
            sP.inc(p.t.matmul(ps1, p.ones_col_b[:], rsq[:], start=(c == 0), stop=(c == NCH - 1)))
        sP.wait(p.v)
        p.v.tensor_copy(rowA[0:1, tt * 512: tt * 512 + 512], ps0)
        tv = sV.inc(p.v.tensor_copy(rowB[0:1, tt * 512: tt * 512 + 512], ps1))
        sV.wait(p.t, tv)
    def step(instr):
        tok = sV.inc(instr)
        sV.wait(p.v, tok)
        return tok
    step(p.v.tensor_scalar_mul(rowA[:], rowA[:], 1.0 / E))
    step(p.v.tensor_mul(rowC[:], rowA[:], rowA[:]))
    tokA = sV.inc(p.v.tensor_scalar_mul(rowA[:], rowA[:], -1.0))
    sV.wait(p.v, tokA)
    tokB = sV.inc(p.v.scalar_tensor_tensor(rowB[:], rowB[:], 1.0 / E, rowC[:], ALU.mult, ALU.subtract))
    sV.wait(p.v, tokB)
    tokB = sV.inc(p.v.tensor_single_scalar(rowB[:], rowB[:], LN_EPS, ALU.add))
    sV.wait(p.sc, tokB)
    ta = sA.inc(p.sc.activation(out=rowB[:], in_=rowB[:], func=AF.Sqrt))
    sA.wait(p.v, ta)
    step(p.v.reciprocal(rowB[:], rowB[:]))
    sV.wait(p.t)
    for tt in range(4):
        sV.wait(p.t)
        p.t.matmul(p.bank(2), p.ones_row_f[:], rowA[0:1, tt * 512: tt * 512 + 512], start=True, stop=True)
        tp = sP.inc(p.t.matmul(p.bank(3), p.ones_row_f[:], rowB[0:1, tt * 512: tt * 512 + 512], start=True, stop=True))
        sP.wait(p.v, tp)
        for c in range(NCH):
            sl = x_fm[:, c * S + tt * 512: c * S + tt * 512 + 512]
            p.v.tensor_tensor(sl, sl, p.bank(2), ALU.add)
            tv = sV.inc(p.v.tensor_tensor(sl, sl, p.bank(3), ALU.mult))
            sV.wait(p.v, tv)
    p.barrier()
    st_.close()



def emit_l0_attn(p, x_fm, Wq, bq, Wk, bk, Wv, bv, O_dram=None, parts="AB", nheads=16, dbg=None):
    stb = ExitStack()
    Kt = p.sbuf("Kt", [128, 4 * S], BF16, stack=stb)
    Vt = p.sbuf("Vt", [128, 4 * S], BF16, stack=stb)
    xb = [p.sbuf(f"at_xb{i}", [128, 512], BF16, stack=stb) for i in range(2)]
    expb = p.sbuf("expb", [128, S], BF16, stack=stb)
    expT = [p.sbuf(f"expT{i}", [128, 512], BF16, stack=stb) for i in range(2)]
    pmax = p.sbuf("pmax", [128, 4], F32, stack=stb)
    negmax = p.sbuf("negmax", [128, 1], F32, stack=stb)
    zcol = p.sbuf("zcol", [128, 1], F32, stack=stb)
    rz = p.sbuf("rz", [128, 1], F32, stack=stb)
    zrow = p.sbuf("zrow", [1, 128], F32, stack=stb)
    rzbc = p.sbuf("rzbc", [128, 128], BF16, stack=stb)
    bqs = p.sbuf("bqs", [128, 16], F32, stack=stb)
    bks = p.sbuf("bks", [128, 4], F32, stack=stb)
    bvs = p.sbuf("bvs", [128, 4], F32, stack=stb)

    sW = p.sem("atW")
    sA = p.sem("atA")
    sP = p.sem("atP")
    sV = p.sem("atV")

    emit_bias_cols(p, bq, bqs, 16, scale=QSCALE)
    emit_bias_cols(p, bk, bks, 4)
    emit_bias_cols(p, bv, bvs, 4)
    p.barrier()

    with ExitStack() as st_:
        wkv = [p.sbuf(f"wkv{i}", [128, E], BF16, stack=st_) for i in range(4)]
        Vf = p.sbuf("Vf", [128, 2 * S], BF16, stack=st_)
        wkv_lastuse = 0
        for sweep in range(2):
            g0 = sweep * 2
            sP.wait(p.g, wkv_lastuse)
            for i, (W, g) in enumerate(((Wk, g0), (Wk, g0 + 1), (Wv, g0), (Wv, g0 + 1))):
                sW.inc(p.g.dma_start(out=wkv[i][:].rearrange("p (c d) -> p c d", c=NCH),
                                     in_=W[g].rearrange("(c p) d -> p c d", p=128)), 16)
            sW.wait(p.t)
            casttok = [0, 0]
            for tt in range(4):
                for c in range(NCH):
                    par = c % 2
                    sP.wait(p.sc, casttok[par])
                    ta = sA.inc(p.sc.activation(out=xb[par][:],
                                                in_=x_fm[:, c * S + tt * 512: c * S + tt * 512 + 512],
                                                func=AF.Copy))
                    sA.wait(p.t, ta)
                    for o in range(4):
                        mm = p.t.matmul(p.bank(o), wkv[o][:, c * 128:(c + 1) * 128], xb[par][:],
                                        start=(c == 0), stop=(c == NCH - 1))
                    casttok[par] = sP.inc(mm)
                wkv_lastuse = sP.n
                sP.wait(p.sc)
                for o in range(4):
                    g = g0 + (o % 2)
                    if o < 2:
                        dst = Kt[:, g * S + tt * 512: g * S + tt * 512 + 512]
                        bias = bks[:, g:g + 1]
                    else:
                        dst = Vf[:, (o - 2) * S + tt * 512: (o - 2) * S + tt * 512 + 512]
                        bias = bvs[:, g:g + 1]
                    ta = sA.inc(p.sc.activation(out=dst, in_=p.bank(o), func=AF.Identity, bias=bias))
                sA.wait(p.t, ta)
            ctokT = [0, 0]
            for gg in range(2):
                g = g0 + gg
                for kt in range(NCH):
                    par = kt % 2
                    sV.wait(p.t, ctokT[par])
                    tp = sP.inc(p.t.transpose(p.bankb(4 + par)[:, 0:128],
                                              Vf[:, gg * S + kt * 128: gg * S + kt * 128 + 128],
                                              p.identb[:]))
                    sP.wait(p.v, tp)
                    ctokT[par] = sV.inc(p.v.tensor_copy(
                        Vt[:, g * S + kt * 128: g * S + kt * 128 + 128], p.bankb(4 + par)[:, 0:128]))
            sV.wait(p.t)
    p.barrier()

    Qh = p.sbuf("Qh", [128, 2 * S], BF16, stack=stb)
    wq = [p.sbuf(f"wq{i}", [128, E], BF16, stack=stb) for i in range(2)]
    ostg = [p.sbuf(f"ostg{i}", [128, 128], F32, stack=stb) for i in range(2)]
    sO = p.sem("atO")
    otok = [0, 0]
    onum = 0
    if "B" not in parts:
        p.barrier()
        stb.close()
        return
    wq_lastuse = 0
    for hp in range(max(0, nheads) // 2):
        heads = (2 * hp, 2 * hp + 1)
        sV.wait(p.t)
        sP.wait(p.g, wq_lastuse)
        for i, h in enumerate(heads):
            sW.inc(p.g.dma_start(out=wq[i][:].rearrange("p (c d) -> p c d", c=NCH),
                                 in_=Wq[h].rearrange("(c p) d -> p c d", p=128)), 16)
        sW.wait(p.t)
        casttok = [0, 0]
        for tt in range(4):
            for c in range(NCH):
                par = c % 2
                sP.wait(p.sc, casttok[par])
                ta = sA.inc(p.sc.activation(out=xb[par][:],
                                            in_=x_fm[:, c * S + tt * 512: c * S + tt * 512 + 512],
                                            func=AF.Copy))
                sA.wait(p.t, ta)
                for i in range(2):
                    mm = p.t.matmul(p.bank(6 + i), wq[i][:, c * 128:(c + 1) * 128], xb[par][:],
                                    start=(c == 0), stop=(c == NCH - 1))
                casttok[par] = sP.inc(mm)
            wq_lastuse = sP.n
            sP.wait(p.sc)
            for i, h in enumerate(heads):
                ta = sA.inc(p.sc.activation(out=Qh[:, i * S + tt * 512: i * S + tt * 512 + 512],
                                            in_=p.bank(6 + i), func=AF.Identity,
                                            scale=QSCALE, bias=bqs[:, h:h + 1]))
            sA.wait(p.t, ta)
        for i, h in enumerate(heads):
            g = h // 4
            exptok = 0
            evactok = 0
            for qi in range(16):
                nk = qi + 1
                ns = (nk + 3) // 4
                sA.wait(p.t)
                sV.wait(p.t, evactok)
                sctok = []
                for s in range(ns):
                    cols = min(512, nk * 128 - s * 512)
                    mm = p.t.matmul(p.bank(s, cols), Qh[:, i * S + qi * 128: i * S + qi * 128 + 128],
                                    Kt[:, g * S + s * 512: g * S + s * 512 + cols],
                                    start=True, stop=True)
                    sctok.append(sP.inc(mm))
                for s in range(ns):
                    cols = min(512, nk * 128 - s * 512)
                    sP.wait(p.v, sctok[s])
                    mr = p.v.reduce_max(pmax[:, s:s + 1], p.bank(s, cols), axis=AX.X)
                tv = sV.inc(mr)
                sV.wait(p.v, tv)
                tv = sV.inc(p.v.reduce_max(negmax[:], pmax[:, 0:ns], axis=AX.X, negate=True))
                if dbg is not None and "s0" in dbg and h == dbg["_h"] and qi == dbg["_qi"]:
                    p.v.tensor_copy(dbg["_sb"][:], p.bank(0, 128))
                    sV.inc(p.v.tensor_copy(dbg["_qb"][:], Qh[:, i * S + qi * 128: i * S + qi * 128 + 128]))
                sV.wait(p.sc, tv)
                sP.wait(p.sc, exptok)
                for s in range(ns):
                    cols = min(512, nk * 128 - s * 512)
                    ta = sA.inc(p.sc.activation(out=expb[:, s * 512: s * 512 + cols],
                                                in_=p.bank(s, cols), func=AF.Exp,
                                                bias=negmax[:, 0:1]))
                sA.wait(p.v, ta)
                dsl = expb[:, qi * 128: qi * 128 + 128]
                tv = sV.inc(p.v.tensor_tensor(dsl, dsl, p.mask01[:], ALU.mult))
                sV.wait(p.v, tv)
                tv = sV.inc(p.v.reduce_sum(zcol[:], expb[:, 0:nk * 128], axis=AX.X))
                sV.wait(p.v, tv)
                tv = sV.inc(p.v.reciprocal(rz[:], zcol[:]))
                sV.wait(p.t, tv)
                tp = sP.inc(p.t.transpose(p.bankrow(4, 128), rz[:], p.identf[:]))
                sP.wait(p.v, tp)
                tv = sV.inc(p.v.tensor_copy(zrow[:], p.bankrow(4, 128)))
                sV.wait(p.t, tv)
                tp = sP.inc(p.t.matmul(p.bank(5, 128), p.ones_row_f[:], zrow[:], start=True, stop=True))
                sP.wait(p.v, tp)
                tv = sV.inc(p.v.tensor_copy(rzbc[:], p.bank(5, 128)))
                sV.wait(p.t, tv)
                ngrp = ns
                Ttok = []
                Mtok = []
                mults_tok = []

                def emit_T(grp):
                    gpar = grp % 2
                    base = grp * 4
                    cnt = min(4, nk - base)
                    if grp >= 2:
                        sV.wait(p.t, mults_tok[grp - 2])
                    for j in range(cnt):
                        tpj = p.t.transpose(p.bankb(4 + gpar)[:, j * 128:(j + 1) * 128],
                                            expb[:, (base + j) * 128:(base + j + 1) * 128],
                                            p.identb[:])
                    Ttok.append(sP.inc(tpj))

                emit_T(0)
                if ngrp > 1:
                    emit_T(1)
                for grp in range(ngrp):
                    gpar = grp % 2
                    base = grp * 4
                    cnt = min(4, nk - base)
                    sP.wait(p.v, Ttok[grp])
                    for j in range(cnt):
                        mv = p.v.tensor_tensor(expT[gpar][:, j * 128:(j + 1) * 128],
                                               p.bankb(4 + gpar)[:, j * 128:(j + 1) * 128],
                                               rzbc[:], ALU.mult)
                    mults_tok.append(sV.inc(mv))
                    sV.wait(p.t, mults_tok[grp])
                    for j in range(cnt):
                        kt = base + j
                        mm = p.t.matmul(p.bank(6, 128), Vt[:, g * S + kt * 128: g * S + kt * 128 + 128],
                                        expT[gpar][:, j * 128:(j + 1) * 128],
                                        start=(kt == 0), stop=(kt == nk - 1))
                    Mtok.append(sP.inc(mm))
                    if grp + 2 < ngrp:
                        emit_T(grp + 2)
                exptok = sP.n
                sP.wait(p.v, Mtok[-1])
                if dbg is not None and "e0" in dbg and h == dbg["_h"] and qi == dbg["_qi"]:
                    p.v.tensor_copy(dbg["_eb"][:], expb[:])
                    p.v.tensor_copy(dbg["_zb"][:], zcol[:])
                    p.v.tensor_copy(dbg["_rb"][:], rzbc[:])
                    sV.inc(p.v.tensor_copy(dbg["_ob"][:], p.bank(6, 128)))
                par = onum % 2
                sO.wait(p.v, otok[par])
                evactok = sV.inc(p.v.tensor_copy(ostg[par][:], p.bank(6, 128)))
                sV.wait(p.sy, evactok)
                otok[par] = sO.inc(p.sy.dma_start(
                    out=O_dram[:, h * S + qi * 128: h * S + qi * 128 + 128], in_=ostg[par][:]), 16)
                onum += 1
    obig = [p.sbuf(f"obig{i}", [128, E], F32, stack=stb) for i in range(2)]
    sO.wait(p.sy)
    btok = [0, 0]
    for c in range(NCH):
        par = c % 2
        sV.wait(p.sy, btok[par])
        tokl = sO.inc(p.sy.dma_start(out=obig[par][:], in_=O_dram[:, c * S:(c + 1) * S]), 16)
        sO.wait(p.v, tokl)
        xsl = x_fm[:, c * S:(c + 1) * S]
        btok[par] = sV.inc(p.v.tensor_tensor(xsl, xsl, obig[par][:], ALU.add))
    sV.wait(p.t)
    p.barrier()
    if dbg is not None and "e0" in dbg:
        sD0 = p.sem("sDbg0")
        sV.wait(p.sy)
        sD0.inc(p.sy.dma_start(out=dbg["e0"], in_=dbg["_eb"][:]), 16)
        sD0.inc(p.sy.dma_start(out=dbg["z0"], in_=dbg["_zb"][:]), 16)
        sD0.inc(p.sy.dma_start(out=dbg["r0"], in_=dbg["_rb"][:]), 16)
        sD0.inc(p.sy.dma_start(out=dbg["o0"], in_=dbg["_ob"][:]), 16)
        if "s0" in dbg:
            sD0.inc(p.sy.dma_start(out=dbg["s0"], in_=dbg["_sb"][:]), 16)
            sD0.inc(p.sy.dma_start(out=dbg["q0"], in_=dbg["_qb"][:]), 16)
        sD0.wait(p.sy)
    if dbg is not None:
        sD = p.sem("sDbg")
        sD.inc(p.sy.dma_start(out=dbg["q"], in_=Qh[:]), 16)
        sD.inc(p.sy.dma_start(out=dbg["k"], in_=Kt[:]), 16)
        sD.inc(p.sy.dma_start(out=dbg["v"], in_=Vt[:]), 16)
        sD.inc(p.sy.dma_start(out=dbg["e"], in_=expb[:]), 16)
        sD.inc(p.sy.dma_start(out=dbg["z"], in_=zcol[:]), 16)
        sD.inc(p.sy.dma_start(out=dbg["rzbc"], in_=rzbc[:]), 16)
        sD.wait(p.sy)
        p.barrier()
    stb.close()



class WStg:

    def __init__(self, p, name, stack, cols=E):
        self.bufs = [p.sbuf(f"{name}_{i}", [128, cols], BF16, stack=stack) for i in range(2)]
        self.tok = [0, 0]
        self.i = 0


def emit_wT_prep(p, dram_rows_ap, wT, sW, sP, sV, stg, nch=NCH, out_off=0, out_stride=128):
    par_s = stg.i % 2
    buf = stg.bufs[par_s]
    sP.wait(p.g, stg.tok[par_s])
    tokw = sW.inc(p.g.dma_start(out=buf[:, 0:nch * 128].rearrange("p (c d) -> p c d", c=nch),
                                in_=dram_rows_ap), 16)
    sW.wait(p.t, tokw)
    for c0 in range(0, nch, 4):
        par = (c0 // 4) % 2
        cnt = min(4, nch - c0)
        sV.wait(p.t, max(0, sV.n - 1))
        for j in range(cnt):
            tp = p.t.transpose(p.bankb(4 + par)[:, j * 128:(j + 1) * 128],
                               buf[:, (c0 + j) * 128:(c0 + j + 1) * 128], p.identb[:])
        tptok = sP.inc(tp)
        sP.wait(p.v, tptok)
        if out_stride == 128:
            o = out_off + c0 * 128
            sV.inc(p.v.tensor_copy(wT[:, o: o + cnt * 128], p.bankb(4 + par)[:, 0:cnt * 128]))
        else:
            for j in range(cnt):
                o = out_off + (c0 + j) * out_stride
                sV.inc(p.v.tensor_copy(wT[:, o: o + 128], p.bankb(4 + par)[:, j * 128:(j + 1) * 128]))
    stg.tok[par_s] = sP.n
    stg.i += 1
    return sV.n


def emit_l0_ffn(p, x_fm, w_gate, w_up, w_down, g_dram, parts="AB"):
    sW = p.sem("f0W")
    sA = p.sem("f0A")
    sP = p.sem("f0P")
    sV = p.sem("f0V")
    sG = p.sem("f0G")

    with ExitStack() as st_:
        xbh = p.sbuf("f0_xbh", [128, NCH * 1024], BF16, stack=st_)
        wgT = [p.sbuf(f"f0_wgT{i}", [128, NCH * 128], BF16, stack=st_) for i in range(4)]
        wuT = [p.sbuf(f"f0_wuT{i}", [128, NCH * 128], BF16, stack=st_) for i in range(4)]
        stgA = WStg(p, "f0_stgA", st_)
        sil = [p.sbuf(f"f0_sil{i}", [128, 512], F32, stack=st_) for i in range(2)]
        gt = [p.sbuf(f"f0_gt{i}", [128, 512], BF16, stack=st_) for i in range(2)]
        siltok = [0, 0]
        gtok = [0, 0]
        en = 0
        for half in range(2):
            t0 = half * 1024
            sP.wait(p.sc)
            ta = 0
            for c in range(NCH):
                ta = sA.inc(p.sc.activation(out=xbh[:, c * 1024:(c + 1) * 1024],
                                            in_=x_fm[:, c * S + t0: c * S + t0 + 1024],
                                            func=AF.Copy))
            sA.wait(p.t, ta)
            for Hq in range(16):
                sA.wait(p.t)
                sV.wait(p.t)
                for i in range(4):
                    h = Hq * 4 + i
                    emit_wT_prep(p, w_gate[h * 128:(h + 1) * 128, :].rearrange("r (c pp) -> r c pp", pp=128),
                                 wgT[i], sW, sP, sV, stgA)
                    emit_wT_prep(p, w_up[h * 128:(h + 1) * 128, :].rearrange("r (c pp) -> r c pp", pp=128),
                                 wuT[i], sW, sP, sV, stgA)
                sV.wait(p.t)
                for tsl in range(2):
                    tb = tsl * 512
                    for c in range(NCH):
                        for i in range(4):
                            p.t.matmul(p.bank(i), wgT[i][:, c * 128:(c + 1) * 128],
                                       xbh[:, c * 1024 + tb: c * 1024 + tb + 512],
                                       start=(c == 0), stop=(c == NCH - 1))
                            mm = p.t.matmul(p.bank(4 + i), wuT[i][:, c * 128:(c + 1) * 128],
                                            xbh[:, c * 1024 + tb: c * 1024 + tb + 512],
                                            start=(c == 0), stop=(c == NCH - 1))
                    tp = sP.inc(mm)
                    sP.wait(p.sc, tp)
                    for i in range(4):
                        h = Hq * 4 + i
                        pa = en % 2
                        en += 1
                        sV.wait(p.sc, siltok[pa])
                        ta = sA.inc(p.sc.activation(out=sil[pa][:], in_=p.bank(i), func=AF.Silu))
                        sA.wait(p.v, ta)
                        sG.wait(p.v, gtok[pa])
                        tv = sV.inc(p.v.tensor_tensor(gt[pa][:], p.bank(4 + i), sil[pa][:], ALU.mult))
                        siltok[pa] = tv
                        sV.wait(p.g, tv)
                        gtok[pa] = sG.inc(p.g.dma_start(out=g_dram[h, :, t0 + tb: t0 + tb + 512],
                                                        in_=gt[pa][:]), 16)
                    sA.wait(p.t)
                    sV.wait(p.t)
    p.barrier()
    if "B" not in parts:
        return

    with ExitStack() as st_:
        wdT = [p.sbuf(f"f0_wdT{i}", [128, NCH * 128], BF16, stack=st_) for i in range(8)]
        gh = [p.sbuf(f"f0_gh{i}", [128, S], BF16, stack=st_) for i in range(8)]
        stgB = WStg(p, "f0_stgB", st_)
        slot_tok = [0] * 8
        half_drain = [0, 0]
        xv = x_fm[:].rearrange("p (ch t) -> p ch t", ch=NCH)
        for hg in range(16):
            hf = hg % 2
            slots = [hf * 4 + i for i in range(4)]
            sV.wait(p.t)
            for k, sl in enumerate(slots):
                h = hg * 4 + k
                sP.wait(p.g, slot_tok[sl])
                sG.inc(p.g.dma_start(out=gh[sl][:], in_=g_dram[h, :, :]), 16)
                emit_wT_prep(p, w_down[:, h * 128:(h + 1) * 128].rearrange("(c pp) h -> pp c h", pp=128),
                             wdT[sl], sW, sP, sV, stgB)
            sV.wait(p.t)
            sG.wait(p.t)
            for win in range(16):
                dq, tsl = win // 4, win % 4
                tb = tsl * 512
                bh = (win % 2) * 4
                sV.wait(p.t, half_drain[win % 2])
                for k, sl in enumerate(slots):
                    for dd in range(4):
                        D = dq * 4 + dd
                        mm = p.t.matmul(p.bank(bh + dd), wdT[sl][:, D * 128:(D + 1) * 128],
                                        gh[sl][:, tb: tb + 512],
                                        start=(k == 0), stop=(k == 3))
                tok_mm = sP.inc(mm)
                sP.wait(p.v, tok_mm)
                half_drain[win % 2] = sV.inc(p.v.tensor_tensor(
                    xv[:, dq * 4:(dq + 1) * 4, tb: tb + 512],
                    xv[:, dq * 4:(dq + 1) * 4, tb: tb + 512],
                    p._arena[:, bh * 512:(bh + 4) * 512].rearrange("p (b t) -> p b t", b=4),
                    ALU.add))
            for sl in slots:
                slot_tok[sl] = sP.n
    p.barrier()


def build_l0_ffn(parts="ABLN"):
    import os
    p = Prog()
    nc = p.nc
    x = nc.dram_tensor("x", [S, E], F32, kind="ExternalInput").ap()
    w_gate = nc.dram_tensor("w_gate", [HID, E], F32, kind="ExternalInput").ap()
    w_up = nc.dram_tensor("w_up", [HID, E], F32, kind="ExternalInput").ap()
    w_down = nc.dram_tensor("w_down", [E, HID], F32, kind="ExternalInput").ap()
    y = nc.dram_tensor("y", [S, E], F32, kind="ExternalOutput").ap()
    gkind = "ExternalOutput" if os.environ.get("DBG") else "Internal"
    g_dram = nc.dram_tensor("g_scratch", [64, 128, S], BF16, kind=gkind).ap()

    x_fm = p.sbuf("x_fm", [128, NCH * S], F32)
    p.setup_consts()
    emit_load_x(p, x, x_fm)
    if "A" in parts:
        emit_l0_ffn(p, x_fm, w_gate, w_up, w_down, g_dram, parts=parts)
    if "LN" in parts:
        emit_ln(p, x_fm, "ln1")
    sST = emit_store_x(p, x_fm, y)
    sST.wait(p.sy)
    return p.finish()



def emit_l1_attn(p, x_fm, w_qkv, w_proj, b_proj, parts="12"):
    sW = p.sem("a1W")
    sA = p.sem("a1A")
    sP = p.sem("a1P")
    sV = p.sem("a1V")

    blockdiag = p.sbuf("a1_bd", [128, 16 * 128], BF16)
    ksums = p.sbuf("a1_ks", [128, 16 * 4], BF16)
    sel4 = p.sbuf("a1_sel4", [4, 128], F32)
    bps = p.sbuf("a1_bps", [128, 16], F32)
    sC = p.sem("a1C")
    g = p.g
    g.memset(blockdiag[:], 0.0)
    g.memset(ksums[:], 0.0)
    g.memset(sel4[:], 1.0)
    g.affine_select(out=sel4[:], in_=sel4[:], compare_op=ALU.is_ge, fill=0.0,
                    base=0, pattern=[[1, 128]], channel_multiplier=-32)
    sC.inc(g.affine_select(out=sel4[:], in_=sel4[:], compare_op=ALU.is_ge, fill=0.0,
                           base=31, pattern=[[-1, 128]], channel_multiplier=32))
    for eng in (p.t, p.v, p.sc):
        sC.wait(eng)
    emit_bias_cols(p, b_proj.rearrange("(c pp) -> c pp", pp=128), bps, 16)
    p.barrier()

    if "1" in parts:
      with ExitStack() as st_:
        wkT = p.sbuf("a1_wkT", [128, NCH * 512], BF16, stack=st_)
        wvT = p.sbuf("a1_wvT", [128, NCH * 512], BF16, stack=st_)
        wstg = WStg(p, "a1_stg1", st_)
        xb = p.sbuf("a1_xb", [128, NCH * 128], BF16, stack=st_)
        k_tm = p.sbuf("a1_ktm", [128, 512], BF16, stack=st_)
        vext = p.sbuf("a1_vext", [128, 4 * 132], BF16, stack=st_)
        sC2 = p.sem("a1C2")
        sC2.inc(p.g.memset(vext[:], 1.0))
        sC2.wait(p.sc)
        for si in range(4):
            for rr in range(4):
                krow = 2048 + si * 512 + rr * 128
                vrow = 4096 + si * 512 + rr * 128
                emit_wT_prep(p, w_qkv[krow:krow + 128, :].rearrange("r (c pp) -> r c pp", pp=128),
                             wkT, sW, sP, sV, wstg,
                             out_off=rr * 128, out_stride=512)
                emit_wT_prep(p, w_qkv[vrow:vrow + 128, :].rearrange("r (c pp) -> r c pp", pp=128),
                             wvT, sW, sP, sV, wstg,
                             out_off=rr * 128, out_stride=512)
            sV.wait(p.t)
            for t128 in range(16):
                tb = t128 * 128
                sP.wait(p.sc)
                ta = 0
                for c in range(NCH):
                    ta = sA.inc(p.sc.activation(out=xb[:, c * 128:(c + 1) * 128],
                                                in_=x_fm[:, c * S + tb: c * S + tb + 128],
                                                func=AF.Copy))
                sA.wait(p.t, ta)
                for c in range(NCH):
                    p.t.matmul(p.bank(0), xb[:, c * 128:(c + 1) * 128], wkT[:, c * 512:(c + 1) * 512],
                               start=(c == 0), stop=(c == NCH - 1))
                    mm = p.t.matmul(p.bank(1), xb[:, c * 128:(c + 1) * 128], wvT[:, c * 512:(c + 1) * 512],
                                    start=(c == 0), stop=(c == NCH - 1))
                tp = sP.inc(mm)
                sP.wait(p.sc, tp)
                ta = sA.inc(p.sc.activation(out=k_tm[:], in_=p.bank(0), func=AF.Relu))
                for q in range(4):
                    ta = sA.inc(p.sc.activation(out=vext[:, q * 132: q * 132 + 128],
                                                in_=p.bank(1, 512)[:, q * 128:(q + 1) * 128],
                                                func=AF.Copy))
                sA.wait(p.t, ta)
                for q in range(4):
                    mm = p.t.matmul(p.bank(2 + q, 129), k_tm[:, q * 128:(q + 1) * 128],
                                    vext[:, q * 132: q * 132 + 129],
                                    start=(t128 == 0), stop=(t128 == 15))
                sP.inc(mm)
            sP.wait(p.v)
            for q in range(4):
                Q = si * 4 + q
                for qp in range(4):
                    b0 = 32 * qp
                    p.v.tensor_copy(blockdiag[b0:b0 + 32, Q * 128 + b0: Q * 128 + b0 + 32],
                                    p.bank(2 + q, 129)[b0:b0 + 32, b0:b0 + 32])
                    tv = sV.inc(p.v.tensor_copy(ksums[b0:b0 + 32, Q * 4 + qp: Q * 4 + qp + 1],
                                                p.bank(2 + q, 129)[b0:b0 + 32, 128:129]))
            sV.wait(p.t, tv)
      p.barrier()

    if "2" in parts:
      with ExitStack() as st_:
        xb5 = p.sbuf("a1_xb5", [128, NCH * 512], BF16, stack=st_)
        wT = p.sbuf("a1_wT", [128, NCH * 128], BF16, stack=st_)
        wstg = WStg(p, "a1_stg2", st_)
        q4 = p.sbuf("a1_q4", [128, 512], BF16, stack=st_)
        den_r = p.sbuf("a1_denr", [4, 512], F32, stack=st_)
        denb_s = p.sbuf("a1_denb", [128, 512], F32, stack=st_)
        o_fm = p.sbuf("a1_ofm", [128, NCH * 512], BF16, stack=st_)
        for tt in range(4):
            tb = tt * 512
            sP.wait(p.sc)
            ta = 0
            for c in range(NCH):
                ta = sA.inc(p.sc.activation(out=xb5[:, c * 512:(c + 1) * 512],
                                            in_=x_fm[:, c * S + tb: c * S + tb + 512],
                                            func=AF.Copy))
            for Q in range(16):
                emit_wT_prep(p, w_qkv[Q * 128:(Q + 1) * 128, :].rearrange("r (c pp) -> r c pp", pp=128),
                             wT, sW, sP, sV, wstg)
                sV.wait(p.t)
                sA.wait(p.t, ta)
                for c in range(NCH):
                    mm = p.t.matmul(p.bank(0), wT[:, c * 128:(c + 1) * 128],
                                    xb5[:, c * 512:(c + 1) * 512],
                                    start=(c == 0), stop=(c == NCH - 1))
                tp = sP.inc(mm)
                sP.wait(p.sc, tp)
                tq = sA.inc(p.sc.activation(out=q4[:], in_=p.bank(0), func=AF.Relu))
                sA.wait(p.t, tq)
                p.t.matmul(p.bank(1), blockdiag[:, Q * 128:(Q + 1) * 128], q4[:],
                           start=True, stop=True)
                tp = sP.inc(p.t.matmul(p.bank(2, 512)[0:4, :], ksums[:, Q * 4:(Q + 1) * 4], q4[:],
                                       start=True, stop=True))
                sP.wait(p.v, tp)
                tv = sV.inc(p.v.tensor_single_scalar(den_r[:], p.bank(2, 512)[0:4, :], 1e-15, ALU.add))
                sV.wait(p.v, tv)
                tv = sV.inc(p.v.reciprocal(den_r[:], den_r[:]))
                sV.wait(p.t, tv)
                tp = sP.inc(p.t.matmul(p.bank(3), sel4[:], den_r[:], start=True, stop=True))
                sP.wait(p.v, tp)
                tv = sV.inc(p.v.tensor_copy(denb_s[:], p.bank(3)))
                sV.wait(p.v, tv)
                tv = sV.inc(p.v.tensor_tensor(o_fm[:, Q * 512:(Q + 1) * 512], p.bank(1), denb_s[:], ALU.mult))
                sV.wait(p.t, tv)
            for D in range(NCH):
                emit_wT_prep(p, w_proj[D * 128:(D + 1) * 128, :].rearrange("r (c pp) -> r c pp", pp=128),
                             wT, sW, sP, sV, wstg)
                sV.wait(p.t)
                for c in range(NCH):
                    mm = p.t.matmul(p.bank(0), wT[:, c * 128:(c + 1) * 128],
                                    o_fm[:, c * 512:(c + 1) * 512],
                                    start=(c == 0), stop=(c == NCH - 1))
                tp = sP.inc(mm)
                sP.wait(p.v, tp)
                sl = x_fm[:, D * S + tb: D * S + tb + 512]
                tv = sV.inc(p.v.scalar_tensor_tensor(sl, p.bank(0), bps[:, D:D + 1], sl,
                                                     ALU.add, ALU.add))
                sV.wait(p.t, tv)
      p.barrier()


def build_l1_attn(parts="12LN"):
    p = Prog()
    nc = p.nc
    x = nc.dram_tensor("x", [S, E], F32, kind="ExternalInput").ap()
    w_qkv = nc.dram_tensor("w_qkv", [3 * E, E], F32, kind="ExternalInput").ap()
    w_proj = nc.dram_tensor("w_proj", [E, E], F32, kind="ExternalInput").ap()
    b_proj = nc.dram_tensor("b_proj", [E], F32, kind="ExternalInput").ap()
    y = nc.dram_tensor("y", [S, E], F32, kind="ExternalOutput").ap()

    x_fm = p.sbuf("x_fm", [128, NCH * S], F32)
    p.setup_consts()
    emit_load_x(p, x, x_fm)
    emit_l1_attn(p, x_fm, w_qkv, w_proj, b_proj, parts=parts)
    if "LN" in parts:
        emit_ln(p, x_fm, "ln2")
    sST = emit_store_x(p, x_fm, y)
    sST.wait(p.sy)
    return p.finish()



def emit_l1_ffn(p, x_fm, w_gate, w_up, w_down, hT_dram, parts="AB"):
    sW = p.sem("f1W")
    sA = p.sem("f1A")
    sP = p.sem("f1P")
    sV = p.sem("f1V")
    sG = p.sem("f1G")

    if "A" in parts:
      with ExitStack() as st_:
        wg_sl = p.sbuf("f1_wg", [128, NCH * 512], BF16, stack=st_)
        wu_sl = p.sbuf("f1_wu", [128, NCH * 512], BF16, stack=st_)
        xb = [p.sbuf(f"f1_xb{i}", [128, NCH * 128], BF16, stack=st_) for i in range(2)]
        sil = [p.sbuf(f"f1_sil{i}", [128, 512], F32, stack=st_) for i in range(2)]
        h = [p.sbuf(f"f1_h{i}", [128, 512], F32, stack=st_) for i in range(2)]
        hm = [p.sbuf(f"f1_hm{i}", [128, 512], BF16, stack=st_) for i in range(2)]
        hTt = [p.sbuf(f"f1_hTt{i}", [128, 512], BF16, stack=st_) for i in range(2)]
        a4 = p.sbuf("f1_a4", [128, 512], F32, stack=st_)
        cmp6 = p.sbuf("f1_cmp", [128, 6 * 128], F32, stack=st_)
        sacc = p.sbuf("f1_sacc", [128, 128], F32, stack=st_)
        keep = p.sbuf("f1_keep", [128, 4 * 128], F32, stack=st_)

        xbtok = [0, 0]
        banktok = [0, 0]
        sparstok = [0, 0]
        dmatok = [0, 0]
        t3tok = [0, 0]
        wtok = 0

        def stage3(t):
            pt = t % 2
            tb = (t % 16) * 128
            hs = t // 16
            sV.wait(p.t, sparstok[pt])
            sV.wait(p.t)
            for j in range(4):
                parb = j % 2
                if j >= 2:
                    sV.wait(p.t, sV.n - 1)
                tp = p.t.transpose(p.bankb(6 + parb)[:, 0:128],
                                   hm[pt][:, j * 128:(j + 1) * 128], p.identb[:])
                tpt = sP.inc(tp)
                sP.wait(p.v, tpt)
                if j == 0:
                    sG.wait(p.v, dmatok[pt])
                sV.inc(p.v.tensor_copy(hTt[pt][:, j * 128:(j + 1) * 128], p.bankb(6 + parb)[:, 0:128]))
            tv = sV.n
            sV.wait(p.g, tv)
            dmatok[pt] = sG.inc(p.g.dma_start(
                out=hT_dram[4 * hs: 4 * hs + 4, :, tb: tb + 128].rearrange("j p t -> p j t"),
                in_=hTt[pt][:].rearrange("p (j t) -> p j t", j=4)), 16)
            t3tok[pt] = sP.n

        tglob = 0
        for hs in range(16):
            sP.wait(p.g, wtok)
            sW.inc(p.g.dma_start(
                out=wg_sl[:].rearrange("p (c i) -> p c i", c=NCH),
                in_=w_gate.rearrange("(c p) i -> p c i", p=128)[:, :, hs * 512:(hs + 1) * 512]), 16)
            tokw = sW.inc(p.g.dma_start(
                out=wu_sl[:].rearrange("p (c i) -> p c i", c=NCH),
                in_=w_up.rearrange("(c p) i -> p c i", p=128)[:, :, hs * 512:(hs + 1) * 512]), 16)
            sW.wait(p.t, tokw)
            for t128 in range(16):
                t = tglob
                pt = t % 2
                tb = t128 * 128
                sP.wait(p.sc, xbtok[pt])
                ta = 0
                for c in range(NCH):
                    ta = sA.inc(p.sc.activation(out=xb[pt][:, c * 128:(c + 1) * 128],
                                                in_=x_fm[:, c * S + tb: c * S + tb + 128],
                                                func=AF.Copy))
                sA.wait(p.t, ta)
                sA.wait(p.t, banktok[pt][0] if False else 0)
                bg, bu = pt * 2, pt * 2 + 1
                sV.wait(p.t, banktok[pt])
                for c in range(NCH):
                    p.t.matmul(p.bank(bg), xb[pt][:, c * 128:(c + 1) * 128], wg_sl[:, c * 512:(c + 1) * 512],
                               start=(c == 0), stop=(c == NCH - 1))
                    mm = p.t.matmul(p.bank(bu), xb[pt][:, c * 128:(c + 1) * 128], wu_sl[:, c * 512:(c + 1) * 512],
                                    start=(c == 0), stop=(c == NCH - 1))
                tp = sP.inc(mm)
                xbtok[pt] = tp
                if t128 == 15:
                    wtok = tp
                sP.wait(p.sc, tp)
                sV.wait(p.sc, sparstok[pt])
                ta = sA.inc(p.sc.activation(out=sil[pt][:], in_=p.bank(bg), func=AF.Silu))
                sA.wait(p.v, ta)
                tv = sV.inc(p.v.tensor_tensor(h[pt][:], p.bank(bu), sil[pt][:], ALU.mult))
                banktok[pt] = tv
                sV.wait(p.sc, tv)
                ta = sA.inc(p.sc.activation(out=a4[:], in_=h[pt][:], func=AF.Abs))
                sA.wait(p.v, ta)
                v = p.v
                def a_(i):
                    return a4[:].rearrange("p (g f) -> p g f", f=4)[:, :, i:i + 1].rearrange("p g o -> p (g o)")
                def hs_(i):
                    return h[pt][:].rearrange("p (g f) -> p g f", f=4)[:, :, i:i + 1].rearrange("p g o -> p (g o)")
                def hms_(i):
                    return hm[pt][:].rearrange("p (g f) -> p g f", f=4)[:, :, i:i + 1].rearrange("p g o -> p (g o)")
                pair = [(0, 1), (0, 2), (0, 3), (1, 2), (1, 3), (2, 3)]
                for n, (i, j) in enumerate(pair):
                    v.tensor_tensor(cmp6[:, n * 128:(n + 1) * 128], a_(i), a_(j), ALU.is_ge)
                c01, c02, c03, c12, c13, c23 = [cmp6[:, n * 128:(n + 1) * 128] for n in range(6)]
                v.tensor_tensor(sacc[:], c01, c02, ALU.add)
                v.tensor_tensor(sacc[:], sacc[:], c03, ALU.add)
                v.tensor_single_scalar(keep[:, 0:128], sacc[:], 1.5, ALU.is_ge)
                v.tensor_tensor(sacc[:], c12, c13, ALU.add)
                v.tensor_tensor(sacc[:], sacc[:], c01, ALU.subtract)
                v.tensor_single_scalar(keep[:, 128:256], sacc[:], 0.5, ALU.is_ge)
                v.tensor_tensor(sacc[:], c02, c12, ALU.add)
                v.tensor_tensor(sacc[:], c23, sacc[:], ALU.subtract)
                v.tensor_single_scalar(keep[:, 256:384], sacc[:], -0.5, ALU.is_ge)
                v.tensor_tensor(sacc[:], c03, c13, ALU.add)
                v.tensor_tensor(sacc[:], sacc[:], c23, ALU.add)
                v.tensor_single_scalar(sacc[:], sacc[:], 1.5, ALU.is_ge)
                v.tensor_scalar(keep[:, 384:512], sacc[:], -1.0, 1.0, ALU.mult, ALU.add)
                sP.wait(p.v, t3tok[pt])
                for i in range(4):
                    mmv = v.tensor_tensor(hms_(i), hs_(i), keep[:, i * 128:(i + 1) * 128], ALU.mult)
                sparstok[pt] = sV.inc(mmv)
                if t >= 1:
                    stage3(t - 1)
                tglob += 1
        stage3(tglob - 1)
      p.barrier()

    if "B" in parts:
      with ExitStack() as st_:
        wds = [p.sbuf(f"f1_wds{i}", [128, E], BF16, stack=st_) for i in range(8)]
        gh = [p.sbuf(f"f1_gh{i}", [128, S], BF16, stack=st_) for i in range(8)]
        slot_tok = [0] * 8
        half_drain = [0, 0]
        xv = x_fm[:].rearrange("p (ch t) -> p ch t", ch=NCH)
        for hg in range(16):
            hf = hg % 2
            slots = [hf * 4 + i for i in range(4)]
            for k, sl in enumerate(slots):
                hh = hg * 4 + k
                sP.wait(p.g, slot_tok[sl])
                sW.inc(p.g.dma_start(out=wds[sl][:].rearrange("p (c d) -> p c d", c=NCH),
                                     in_=w_down[hh * 128:(hh + 1) * 128, :].rearrange("r (c d) -> r c d", d=128)), 16)
                sG.inc(p.g.dma_start(out=gh[sl][:], in_=hT_dram[hh, :, :]), 16)
            sW.wait(p.t)
            sG.wait(p.t)
            for win in range(16):
                dq, tsl = win // 4, win % 4
                tb = tsl * 512
                bh = (win % 2) * 4
                sV.wait(p.t, half_drain[win % 2])
                for k, sl in enumerate(slots):
                    for dd in range(4):
                        D = dq * 4 + dd
                        mm = p.t.matmul(p.bank(bh + dd), wds[sl][:, D * 128:(D + 1) * 128],
                                        gh[sl][:, tb: tb + 512],
                                        start=(k == 0), stop=(k == 3))
                tok_mm = sP.inc(mm)
                sP.wait(p.v, tok_mm)
                half_drain[win % 2] = sV.inc(p.v.tensor_tensor(
                    xv[:, dq * 4:(dq + 1) * 4, tb: tb + 512],
                    xv[:, dq * 4:(dq + 1) * 4, tb: tb + 512],
                    p._arena[:, bh * 512:(bh + 4) * 512].rearrange("p (b t) -> p b t", b=4),
                    ALU.add))
            for sl in slots:
                slot_tok[sl] = sP.n
      p.barrier()


def build_l1_ffn(parts="ABLN"):
    import os
    p = Prog()
    nc = p.nc
    x = nc.dram_tensor("x", [S, E], F32, kind="ExternalInput").ap()
    w_gate = nc.dram_tensor("w_gate", [E, HID], F32, kind="ExternalInput").ap()
    w_up = nc.dram_tensor("w_up", [E, HID], F32, kind="ExternalInput").ap()
    w_down = nc.dram_tensor("w_down", [HID, E], F32, kind="ExternalInput").ap()
    y = nc.dram_tensor("y", [S, E], F32, kind="ExternalOutput").ap()
    hkind = "ExternalOutput" if os.environ.get("DBG") else "Internal"
    hT_dram = nc.dram_tensor("hT_scratch", [64, 128, S], BF16, kind=hkind).ap()

    x_fm = p.sbuf("x_fm", [128, NCH * S], F32)
    p.setup_consts()
    emit_load_x(p, x, x_fm)
    emit_l1_ffn(p, x_fm, w_gate, w_up, w_down, hT_dram, parts=parts)
    if "LN" in parts:
        emit_ln(p, x_fm, "ln3")
    sST = emit_store_x(p, x_fm, y)
    sST.wait(p.sy)
    return p.finish()


def build_l0_attn(parts="ABLN"):
    p = Prog()
    nc = p.nc
    x = nc.dram_tensor("x", [S, E], F32, kind="ExternalInput").ap()
    Wq = nc.dram_tensor("Wq", [16, E, 128], F32, kind="ExternalInput").ap()
    bq = nc.dram_tensor("bq", [16, 128], F32, kind="ExternalInput").ap()
    Wk = nc.dram_tensor("Wk", [4, E, 128], F32, kind="ExternalInput").ap()
    bk = nc.dram_tensor("bk", [4, 128], F32, kind="ExternalInput").ap()
    Wv = nc.dram_tensor("Wv", [4, E, 128], F32, kind="ExternalInput").ap()
    bv = nc.dram_tensor("bv", [4, 128], F32, kind="ExternalInput").ap()
    y = nc.dram_tensor("y", [S, E], F32, kind="ExternalOutput").ap()

    x_fm = p.sbuf("x_fm", [128, NCH * S], F32)
    p.setup_consts()
    emit_load_x(p, x, x_fm)
    if "A" in parts:
        import os
        nh = int(os.environ.get("NHEADS", "16"))
        dbg = None
        if os.environ.get("DBG"):
            dbg = {
                "q": nc.dram_tensor("dbg_q", [128, 2 * S], BF16, kind="ExternalOutput").ap(),
                "k": nc.dram_tensor("dbg_k", [128, 4 * S], BF16, kind="ExternalOutput").ap(),
                "v": nc.dram_tensor("dbg_v", [128, 4 * S], BF16, kind="ExternalOutput").ap(),
                "e": nc.dram_tensor("dbg_e", [128, S], BF16, kind="ExternalOutput").ap(),
                "z": nc.dram_tensor("dbg_z", [128, 1], F32, kind="ExternalOutput").ap(),
                "rzbc": nc.dram_tensor("dbg_rzbc", [128, 128], BF16, kind="ExternalOutput").ap(),
            }
            if os.environ.get("DBG_QI") is not None:
                dbg["_h"] = int(os.environ.get("DBG_H", str(nh - 1)))
                dbg["_qi"] = int(os.environ["DBG_QI"])
                dbg["e0"] = nc.dram_tensor("dbg_e0", [128, S], BF16, kind="ExternalOutput").ap()
                dbg["z0"] = nc.dram_tensor("dbg_z0", [128, 1], F32, kind="ExternalOutput").ap()
                dbg["r0"] = nc.dram_tensor("dbg_r0", [128, 128], BF16, kind="ExternalOutput").ap()
                dbg["o0"] = nc.dram_tensor("dbg_o0", [128, 128], F32, kind="ExternalOutput").ap()
                dbg["_eb"] = p.sbuf("dbg_eb", [128, S], BF16)
                dbg["_zb"] = p.sbuf("dbg_zb", [128, 1], F32)
                dbg["_rb"] = p.sbuf("dbg_rb", [128, 128], BF16)
                dbg["_ob"] = p.sbuf("dbg_ob", [128, 128], F32)
                dbg["s0"] = nc.dram_tensor("dbg_s0", [128, 128], F32, kind="ExternalOutput").ap()
                dbg["q0"] = nc.dram_tensor("dbg_q0", [128, 128], BF16, kind="ExternalOutput").ap()
                dbg["_sb"] = p.sbuf("dbg_sb", [128, 128], F32)
                dbg["_qb"] = p.sbuf("dbg_qb", [128, 128], BF16)
        O_dram = nc.dram_tensor("O_scratch", [128, NCH * S], F32, kind="Internal").ap()
        emit_l0_attn(p, x_fm, Wq, bq, Wk, bk, Wv, bv, O_dram=O_dram, parts=parts, nheads=nh, dbg=dbg)
    if "LN" in parts:
        emit_ln(p, x_fm, "ln0")
    sST = emit_store_x(p, x_fm, y)
    sST.wait(p.sy)
    return p.finish()


def build_fused():
    p = Prog()
    nc = p.nc
    x = nc.dram_tensor("x", [S, E], F32, kind="ExternalInput").ap()
    Wq = nc.dram_tensor("Wq", [16, E, 128], F32, kind="ExternalInput").ap()
    bq = nc.dram_tensor("bq", [16, 128], F32, kind="ExternalInput").ap()
    Wk = nc.dram_tensor("Wk", [4, E, 128], F32, kind="ExternalInput").ap()
    bk = nc.dram_tensor("bk", [4, 128], F32, kind="ExternalInput").ap()
    Wv = nc.dram_tensor("Wv", [4, E, 128], F32, kind="ExternalInput").ap()
    bv = nc.dram_tensor("bv", [4, 128], F32, kind="ExternalInput").ap()
    w_gate0 = nc.dram_tensor("w_gate0", [HID, E], F32, kind="ExternalInput").ap()
    w_up0 = nc.dram_tensor("w_up0", [HID, E], F32, kind="ExternalInput").ap()
    w_down0 = nc.dram_tensor("w_down0", [E, HID], F32, kind="ExternalInput").ap()
    w_qkv = nc.dram_tensor("w_qkv", [3 * E, E], F32, kind="ExternalInput").ap()
    w_proj = nc.dram_tensor("w_proj", [E, E], F32, kind="ExternalInput").ap()
    b_proj = nc.dram_tensor("b_proj", [E], F32, kind="ExternalInput").ap()
    w_gate1 = nc.dram_tensor("w_gate1", [E, HID], F32, kind="ExternalInput").ap()
    w_up1 = nc.dram_tensor("w_up1", [E, HID], F32, kind="ExternalInput").ap()
    w_down1 = nc.dram_tensor("w_down1", [HID, E], F32, kind="ExternalInput").ap()
    y = nc.dram_tensor("y", [S, E], F32, kind="ExternalOutput").ap()
    O_dram = nc.dram_tensor("O_scratch", [128, NCH * S], F32, kind="Internal").ap()
    g_dram = nc.dram_tensor("g_scratch", [64, 128, S], BF16, kind="Internal").ap()
    hT_dram = nc.dram_tensor("hT_scratch", [64, 128, S], BF16, kind="Internal").ap()

    x_fm = p.sbuf("x_fm", [128, NCH * S], F32)
    p.setup_consts()
    with nc.named_scope("loadx"):
        emit_load_x(p, x, x_fm)
    with nc.named_scope("l0attn"):
        emit_l0_attn(p, x_fm, Wq, bq, Wk, bk, Wv, bv, O_dram=O_dram)
    with nc.named_scope("ln0"):
        emit_ln(p, x_fm, "ln0")
    with nc.named_scope("l0ffn"):
        emit_l0_ffn(p, x_fm, w_gate0, w_up0, w_down0, g_dram)
    with nc.named_scope("ln1"):
        emit_ln(p, x_fm, "ln1")
    with nc.named_scope("l1attn"):
        emit_l1_attn(p, x_fm, w_qkv, w_proj, b_proj)
    with nc.named_scope("ln2"):
        emit_ln(p, x_fm, "ln2")
    with nc.named_scope("l1ffn"):
        emit_l1_ffn(p, x_fm, w_gate1, w_up1, w_down1, hT_dram)
    with nc.named_scope("ln3"):
        emit_ln(p, x_fm, "ln3")
    with nc.named_scope("storex"):
        sST = emit_store_x(p, x_fm, y)
    sST.wait(p.sy)
    return p.finish()


def kernel(**inputs):
    g = lambda k: np.ascontiguousarray(inputs[k])
    x = g("L0_attn_x")
    n = 8
    nc = build_fused()
    maps = [{"x": np.ascontiguousarray(x[b]),
             "Wq": g("L0_attn_Wq"), "bq": g("L0_attn_bq"),
             "Wk": g("L0_attn_Wk"), "bk": g("L0_attn_bk"),
             "Wv": g("L0_attn_Wv"), "bv": g("L0_attn_bv"),
             "w_gate0": g("L0_ffn_w_gate"), "w_up0": g("L0_ffn_w_up"),
             "w_down0": g("L0_ffn_w_down"),
             "w_qkv": g("L1_attn_w_qkv"), "w_proj": g("L1_attn_w_proj"),
             "b_proj": g("L1_attn_b_proj"),
             "w_gate1": g("L1_ffn_w_gate"), "w_up1": g("L1_ffn_w_up"),
             "w_down1": g("L1_ffn_w_down")} for b in range(n)]
    r = run_bass_kernel_spmd(nc, maps, core_ids=list(range(n)))
    return np.stack([np.asarray(r.results[b]["y"]) for b in range(n)], axis=0)


def kernel_unfused(**inputs):
    g = lambda k: np.ascontiguousarray(inputs[k])
    x = g("L0_attn_x")
    n = 8
    cores = list(range(n))

    nc1 = build_l0_attn()
    maps1 = [{"x": np.ascontiguousarray(x[b]), "Wq": g("L0_attn_Wq"), "bq": g("L0_attn_bq"),
              "Wk": g("L0_attn_Wk"), "bk": g("L0_attn_bk"),
              "Wv": g("L0_attn_Wv"), "bv": g("L0_attn_bv")} for b in range(n)]
    r1 = run_bass_kernel_spmd(nc1, maps1, core_ids=cores)
    x1 = [r1.results[b]["y"] for b in range(n)]

    nc2 = build_l0_ffn()
    maps2 = [{"x": x1[b], "w_gate": g("L0_ffn_w_gate"), "w_up": g("L0_ffn_w_up"),
              "w_down": g("L0_ffn_w_down")} for b in range(n)]
    r2 = run_bass_kernel_spmd(nc2, maps2, core_ids=cores)
    x2 = [r2.results[b]["y"] for b in range(n)]

    nc3 = build_l1_attn()
    maps3 = [{"x": x2[b], "w_qkv": g("L1_attn_w_qkv"), "w_proj": g("L1_attn_w_proj"),
              "b_proj": g("L1_attn_b_proj")} for b in range(n)]
    r3 = run_bass_kernel_spmd(nc3, maps3, core_ids=cores)
    x3 = [r3.results[b]["y"] for b in range(n)]

    nc4 = build_l1_ffn()
    maps4 = [{"x": x3[b], "w_gate": g("L1_ffn_w_gate"), "w_up": g("L1_ffn_w_up"),
              "w_down": g("L1_ffn_w_down")} for b in range(n)]
    r4 = run_bass_kernel_spmd(nc4, maps4, core_ids=cores)
    return np.stack([np.asarray(r4.results[b]["y"]) for b in range(n)], axis=0)


# version 07
# speedup vs baseline: 1.4040x; 1.0174x over previous
import numpy as np
from contextlib import ExitStack

import concourse.bass as bass
import concourse.mybir as mybir
from concourse.bass_utils import run_bass_kernel_spmd

F32 = mybir.dt.float32
BF16 = mybir.dt.bfloat16
AF = mybir.ActivationFunctionType
ALU = mybir.AluOpType
AX = mybir.AxisListType

S = 2048
E = 2048
NCH = 16
HID = 8192
QSCALE = float(1.0 / np.sqrt(128.0))
LN_EPS = 1e-5


class Sem:
    __slots__ = ("h", "n")

    def __init__(self, nc, name=None):
        self.h = nc.alloc_semaphore(name=name)
        self.n = 0

    def inc(self, instr, k=1):
        instr.then_inc(self.h, k)
        self.n += k
        return self.n

    def wait(self, eng, tok=None):
        tok = self.n if tok is None else tok
        if tok > 0:
            eng.wait_ge(self.h, tok)


class Prog:
    def __init__(self):
        nc = bass.Bass("TRN2", target_bir_lowering=False)
        self.nc = nc
        self.t = nc.tensor
        self.v = nc.vector
        self.sc = nc.scalar
        self.g = nc.gpsimd
        self.sy = nc.sync
        self.ctx = ExitStack()
        self.sems = {}
        self.PS = self.sbuf_like = None
        self._arena = self.ctx.enter_context(nc.psum_tensor("arena", [128, 8 * 512], F32))

    def bank(self, i, cols=512):
        return self._arena[:, i * 512: i * 512 + cols]

    def bankb(self, i, cols=1024):
        return self._arena[:, i * 512:(i + 1) * 512].bitcast(BF16)[:, 0:cols]

    def bankrow(self, i, cols=512):
        return self._arena[0:1, i * 512: i * 512 + cols]

    def sem(self, name):
        if name not in self.sems:
            self.sems[name] = Sem(self.nc, name)
        return self.sems[name]

    def sbuf(self, name, shape, dtype, stack=None):
        return (stack or self.ctx).enter_context(self.nc.sbuf_tensor(name, shape, dtype))

    def barrier(self):
        self.nc.all_engine_barrier()

    def setup_consts(self):
        p = self
        self.identf = p.sbuf("identf", [128, 128], F32)
        self.identb = p.sbuf("identb", [128, 128], BF16)
        self.mask01 = p.sbuf("mask01", [128, 128], BF16)
        self.ones_col_b = p.sbuf("ones_col_b", [128, 1], BF16)
        self.ones_row_f = p.sbuf("ones_row_f", [1, 128], F32)
        sC = p.sem("sConst")
        g = p.g
        g.memset(self.identf[:], 0.0)
        g.affine_select(out=self.identf[:], in_=self.identf[:],
                        compare_op=ALU.not_equal, fill=1.0, base=0,
                        pattern=[[-1, 128]], channel_multiplier=1)
        g.memset(self.identb[:], 0.0)
        g.affine_select(out=self.identb[:], in_=self.identb[:],
                        compare_op=ALU.not_equal, fill=1.0, base=0,
                        pattern=[[-1, 128]], channel_multiplier=1)
        g.memset(self.mask01[:], 1.0)
        g.affine_select(out=self.mask01[:], in_=self.mask01[:],
                        compare_op=ALU.is_ge, fill=0.0, base=0,
                        pattern=[[-1, 128]], channel_multiplier=1)
        g.memset(self.ones_col_b[:], 1.0)
        sC.inc(g.memset(self.ones_row_f[:], 1.0))
        for eng in (p.t, p.v, p.sc):
            sC.wait(eng)

    def finish(self):
        self.ctx.close()
        return self.nc



def emit_load_x(p, x_dram, x_fm):
    st_ = ExitStack()
    stage = p.sbuf("ld_stage", [128, E], F32, stack=st_)
    sDMA = p.sem("sLdX")
    sPE = p.sem("sLdPE")
    sDVE = p.sem("sLdDVE")
    ctok = [0, 0]
    for tc in range(S // 128):
        tok = sDMA.inc(p.sy.dma_start(out=stage[:], in_=x_dram[tc * 128:(tc + 1) * 128, :]), 16)
        sDMA.wait(p.t, tok)
        for c in range(NCH):
            par = c % 2
            sDVE.wait(p.t, ctok[par])
            tp = sPE.inc(p.t.transpose(p.bank(par, 128), stage[:, c * 128:(c + 1) * 128], p.identf[:]))
            sPE.wait(p.v, tp)
            ctok[par] = sDVE.inc(p.v.tensor_copy(
                x_fm[:, c * S + tc * 128: c * S + tc * 128 + 128], p.bank(par, 128)))
        sPE.wait(p.sy)
    p.barrier()
    st_.close()


def emit_store_x(p, x_fm, y_dram):
    stage = p.sbuf("st_stage", [128, E], F32)
    sPE = p.sem("sStPE")
    sDVE = p.sem("sStDVE")
    sST = p.sem("sStDMA")
    ctok = [0, 0]
    for tc in range(S // 128):
        sST.wait(p.v)
        for c in range(NCH):
            par = c % 2
            sDVE.wait(p.t, ctok[par])
            tp = sPE.inc(p.t.transpose(p.bank(par, 128),
                                       x_fm[:, c * S + tc * 128: c * S + tc * 128 + 128],
                                       p.identf[:]))
            sPE.wait(p.v, tp)
            ctok[par] = sDVE.inc(p.v.tensor_copy(stage[:, c * 128:(c + 1) * 128], p.bank(par, 128)))
        sDVE.wait(p.sy)
        sST.inc(p.sy.dma_start(out=y_dram[tc * 128:(tc + 1) * 128, :], in_=stage[:]), 16)
    return sST


def emit_bias_cols(p, dram_2d, out_tile, rows, scale=None):
    st_ = ExitStack()
    raw = p.sbuf(f"braw_{out_tile.name}", [128, 128], F32, stack=st_)
    sB = p.sem("sBias")
    tokz = sB.inc(p.g.memset(raw[:], 0.0))
    sB.wait(p.sy, tokz)
    tok = sB.inc(p.sy.dma_start(out=raw[0:rows, :], in_=dram_2d), 16)
    sB.wait(p.t, tok)
    tp = sB.inc(p.t.transpose(p.bank(7, 128), raw[:], p.identf[:]))
    sB.wait(p.v, tp)
    if scale is None:
        sB.inc(p.v.tensor_copy(out_tile[:], p.bank(7, 128)[:, 0:rows]))
    else:
        sB.inc(p.v.tensor_scalar_mul(out_tile[:], p.bank(7, 128)[:, 0:rows], scale))
    sB.wait(p.t)
    sB.wait(p.v)
    sB.wait(p.g)
    sB.wait(p.sy)
    st_.close()


def emit_ln(p, x_fm, tag):
    st_ = ExitStack()
    rbf = [p.sbuf(f"{tag}_rbf{i}", [128, 512], BF16, stack=st_) for i in range(2)]
    rsq = [p.sbuf(f"{tag}_rsq{i}", [128, 512], BF16, stack=st_) for i in range(2)]
    rowA = p.sbuf(f"{tag}_rowA", [1, S], F32, stack=st_)
    rowB = p.sbuf(f"{tag}_rowB", [1, S], F32, stack=st_)
    rowC = p.sbuf(f"{tag}_rowC", [1, S], F32, stack=st_)
    sA = p.sem(f"{tag}A")
    sP = p.sem(f"{tag}P")
    sV = p.sem(f"{tag}V")
    ps0 = p.bankrow(0)
    ps1 = p.bankrow(1)
    casttok = [0, 0]
    for tt in range(4):
        for c in range(NCH):
            par = c % 2
            sP.wait(p.sc, casttok[par])
            p.sc.activation(out=rbf[par][:], in_=x_fm[:, c * S + tt * 512: c * S + tt * 512 + 512], func=AF.Copy)
            ta = sA.inc(p.sc.activation(out=rsq[par][:], in_=x_fm[:, c * S + tt * 512: c * S + tt * 512 + 512], func=AF.Square))
            sA.wait(p.t, ta)
            p.t.matmul(ps0, p.ones_col_b[:], rbf[par][:], start=(c == 0), stop=(c == NCH - 1))
            casttok[par] = sP.inc(p.t.matmul(ps1, p.ones_col_b[:], rsq[par][:], start=(c == 0), stop=(c == NCH - 1)))
        sP.wait(p.v)
        p.v.tensor_copy(rowA[0:1, tt * 512: tt * 512 + 512], ps0)
        tv = sV.inc(p.v.tensor_copy(rowB[0:1, tt * 512: tt * 512 + 512], ps1))
        sV.wait(p.t, tv)
    def step(instr):
        tok = sV.inc(instr)
        sV.wait(p.v, tok)
        return tok
    step(p.v.tensor_scalar_mul(rowA[:], rowA[:], 1.0 / E))
    step(p.v.tensor_mul(rowC[:], rowA[:], rowA[:]))
    tokA = sV.inc(p.v.tensor_scalar_mul(rowA[:], rowA[:], -1.0))
    sV.wait(p.v, tokA)
    tokB = sV.inc(p.v.scalar_tensor_tensor(rowB[:], rowB[:], 1.0 / E, rowC[:], ALU.mult, ALU.subtract))
    sV.wait(p.v, tokB)
    tokB = sV.inc(p.v.tensor_single_scalar(rowB[:], rowB[:], LN_EPS, ALU.add))
    sV.wait(p.sc, tokB)
    ta = sA.inc(p.sc.activation(out=rowB[:], in_=rowB[:], func=AF.Sqrt))
    sA.wait(p.v, ta)
    step(p.v.reciprocal(rowB[:], rowB[:]))
    sV.wait(p.t)
    normtok = [0, 0]
    for tt in range(4):
        bp = (tt % 2) * 2
        sV.wait(p.t, normtok[tt % 2])
        p.t.matmul(p.bank(bp), p.ones_row_f[:], rowA[0:1, tt * 512: tt * 512 + 512], start=True, stop=True)
        tp = sP.inc(p.t.matmul(p.bank(bp + 1), p.ones_row_f[:], rowB[0:1, tt * 512: tt * 512 + 512], start=True, stop=True))
        sP.wait(p.v, tp)
        for c in range(NCH):
            sl = x_fm[:, c * S + tt * 512: c * S + tt * 512 + 512]
            p.v.tensor_tensor(sl, sl, p.bank(bp), ALU.add)
            tv = sV.inc(p.v.tensor_tensor(sl, sl, p.bank(bp + 1), ALU.mult))
        normtok[tt % 2] = tv
    sV.wait(p.t)
    sV.wait(p.sc)
    sV.wait(p.g)
    p.barrier()
    st_.close()



def emit_l0_attn(p, x_fm, Wq, bq, Wk, bk, Wv, bv, O_dram=None, parts="AB", nheads=16, dbg=None):
    stb = ExitStack()
    Kt = p.sbuf("Kt", [128, 4 * S], BF16, stack=stb)
    Vt = p.sbuf("Vt", [128, 4 * S], BF16, stack=stb)
    xb = [p.sbuf(f"at_xb{i}", [128, 512], BF16, stack=stb) for i in range(2)]
    expb = p.sbuf("expb", [128, S], BF16, stack=stb)
    expT = [p.sbuf(f"expT{i}", [128, 512], BF16, stack=stb) for i in range(2)]
    pmax = p.sbuf("pmax", [128, 4], F32, stack=stb)
    negmax = p.sbuf("negmax", [128, 1], F32, stack=stb)
    zcol = p.sbuf("zcol", [128, 1], F32, stack=stb)
    rz = p.sbuf("rz", [128, 1], F32, stack=stb)
    zrow = p.sbuf("zrow", [1, 128], F32, stack=stb)
    rzbc = p.sbuf("rzbc", [128, 128], BF16, stack=stb)
    bqs = p.sbuf("bqs", [128, 16], F32, stack=stb)
    bks = p.sbuf("bks", [128, 4], F32, stack=stb)
    bvs = p.sbuf("bvs", [128, 4], F32, stack=stb)

    sW = p.sem("atW")
    sA = p.sem("atA")
    sP = p.sem("atP")
    sV = p.sem("atV")

    emit_bias_cols(p, bq, bqs, 16, scale=QSCALE)
    emit_bias_cols(p, bk, bks, 4)
    emit_bias_cols(p, bv, bvs, 4)
    p.barrier()

    with ExitStack() as st_:
        wkv = [p.sbuf(f"wkv{i}", [128, E], BF16, stack=st_) for i in range(4)]
        Vf = p.sbuf("Vf", [128, 2 * S], BF16, stack=st_)
        wkv_lastuse = 0
        for sweep in range(2):
            g0 = sweep * 2
            sP.wait(p.g, wkv_lastuse)
            for i, (W, g) in enumerate(((Wk, g0), (Wk, g0 + 1), (Wv, g0), (Wv, g0 + 1))):
                sW.inc(p.g.dma_start(out=wkv[i][:].rearrange("p (c d) -> p c d", c=NCH),
                                     in_=W[g].rearrange("(c p) d -> p c d", p=128)), 16)
            sW.wait(p.t)
            casttok = [0, 0]
            for tt in range(4):
                for c in range(NCH):
                    par = c % 2
                    sP.wait(p.sc, casttok[par])
                    ta = sA.inc(p.sc.activation(out=xb[par][:],
                                                in_=x_fm[:, c * S + tt * 512: c * S + tt * 512 + 512],
                                                func=AF.Copy))
                    sA.wait(p.t, ta)
                    for o in range(4):
                        mm = p.t.matmul(p.bank(o), wkv[o][:, c * 128:(c + 1) * 128], xb[par][:],
                                        start=(c == 0), stop=(c == NCH - 1))
                    casttok[par] = sP.inc(mm)
                wkv_lastuse = sP.n
                sP.wait(p.sc)
                for o in range(4):
                    g = g0 + (o % 2)
                    if o < 2:
                        dst = Kt[:, g * S + tt * 512: g * S + tt * 512 + 512]
                        bias = bks[:, g:g + 1]
                    else:
                        dst = Vf[:, (o - 2) * S + tt * 512: (o - 2) * S + tt * 512 + 512]
                        bias = bvs[:, g:g + 1]
                    ta = sA.inc(p.sc.activation(out=dst, in_=p.bank(o), func=AF.Identity, bias=bias))
                sA.wait(p.t, ta)
            ctokT = [0, 0]
            for gg in range(2):
                g = g0 + gg
                for kt in range(NCH):
                    par = kt % 2
                    sV.wait(p.t, ctokT[par])
                    tp = sP.inc(p.t.transpose(p.bankb(4 + par)[:, 0:128],
                                              Vf[:, gg * S + kt * 128: gg * S + kt * 128 + 128],
                                              p.identb[:]))
                    sP.wait(p.v, tp)
                    ctokT[par] = sV.inc(p.v.tensor_copy(
                        Vt[:, g * S + kt * 128: g * S + kt * 128 + 128], p.bankb(4 + par)[:, 0:128]))
            sV.wait(p.t)
    p.barrier()

    Qh = p.sbuf("Qh", [128, 2 * S], BF16, stack=stb)
    wq = [p.sbuf(f"wq{i}", [128, E], BF16, stack=stb) for i in range(2)]
    ostg = [p.sbuf(f"ostg{i}", [128, 128], F32, stack=stb) for i in range(2)]
    sO = p.sem("atO")
    otok = [0, 0]
    onum = 0
    if "B" not in parts:
        p.barrier()
        stb.close()
        return
    wq_lastuse = 0
    for hp in range(max(0, nheads) // 2):
        heads = (2 * hp, 2 * hp + 1)
        sV.wait(p.t)
        sP.wait(p.g, wq_lastuse)
        for i, h in enumerate(heads):
            sW.inc(p.g.dma_start(out=wq[i][:].rearrange("p (c d) -> p c d", c=NCH),
                                 in_=Wq[h].rearrange("(c p) d -> p c d", p=128)), 16)
        sW.wait(p.t)
        casttok = [0, 0]
        for tt in range(4):
            for c in range(NCH):
                par = c % 2
                sP.wait(p.sc, casttok[par])
                ta = sA.inc(p.sc.activation(out=xb[par][:],
                                            in_=x_fm[:, c * S + tt * 512: c * S + tt * 512 + 512],
                                            func=AF.Copy))
                sA.wait(p.t, ta)
                for i in range(2):
                    mm = p.t.matmul(p.bank(6 + i), wq[i][:, c * 128:(c + 1) * 128], xb[par][:],
                                    start=(c == 0), stop=(c == NCH - 1))
                casttok[par] = sP.inc(mm)
            wq_lastuse = sP.n
            sP.wait(p.sc)
            for i, h in enumerate(heads):
                ta = sA.inc(p.sc.activation(out=Qh[:, i * S + tt * 512: i * S + tt * 512 + 512],
                                            in_=p.bank(6 + i), func=AF.Identity,
                                            scale=QSCALE, bias=bqs[:, h:h + 1]))
            sA.wait(p.t, ta)
        for i, h in enumerate(heads):
            g = h // 4
            exptok = 0
            evactok = 0
            for qi in range(16):
                nk = qi + 1
                ns = (nk + 3) // 4
                sA.wait(p.t)
                sV.wait(p.t, evactok)
                sctok = []
                for s in range(ns):
                    cols = min(512, nk * 128 - s * 512)
                    mm = p.t.matmul(p.bank(s, cols), Qh[:, i * S + qi * 128: i * S + qi * 128 + 128],
                                    Kt[:, g * S + s * 512: g * S + s * 512 + cols],
                                    start=True, stop=True)
                    sctok.append(sP.inc(mm))
                for s in range(ns):
                    cols = min(512, nk * 128 - s * 512)
                    sP.wait(p.v, sctok[s])
                    mr = p.v.reduce_max(pmax[:, s:s + 1], p.bank(s, cols), axis=AX.X)
                tv = sV.inc(mr)
                sV.wait(p.v, tv)
                tv = sV.inc(p.v.reduce_max(negmax[:], pmax[:, 0:ns], axis=AX.X, negate=True))
                if dbg is not None and "s0" in dbg and h == dbg["_h"] and qi == dbg["_qi"]:
                    p.v.tensor_copy(dbg["_sb"][:], p.bank(0, 128))
                    sV.inc(p.v.tensor_copy(dbg["_qb"][:], Qh[:, i * S + qi * 128: i * S + qi * 128 + 128]))
                sV.wait(p.sc, tv)
                sP.wait(p.sc, exptok)
                for s in range(ns):
                    cols = min(512, nk * 128 - s * 512)
                    ta = sA.inc(p.sc.activation(out=expb[:, s * 512: s * 512 + cols],
                                                in_=p.bank(s, cols), func=AF.Exp,
                                                bias=negmax[:, 0:1]))
                sA.wait(p.v, ta)
                dsl = expb[:, qi * 128: qi * 128 + 128]
                tv = sV.inc(p.v.tensor_tensor(dsl, dsl, p.mask01[:], ALU.mult))
                sV.wait(p.v, tv)
                tv = sV.inc(p.v.reduce_sum(zcol[:], expb[:, 0:nk * 128], axis=AX.X))
                sV.wait(p.v, tv)
                tv = sV.inc(p.v.reciprocal(rz[:], zcol[:]))
                sV.wait(p.t, tv)
                tp = sP.inc(p.t.transpose(p.bankrow(4, 128), rz[:], p.identf[:]))
                sP.wait(p.v, tp)
                tv = sV.inc(p.v.tensor_copy(zrow[:], p.bankrow(4, 128)))
                sV.wait(p.t, tv)
                tp = sP.inc(p.t.matmul(p.bank(5, 128), p.ones_row_f[:], zrow[:], start=True, stop=True))
                sP.wait(p.v, tp)
                tv = sV.inc(p.v.tensor_copy(rzbc[:], p.bank(5, 128)))
                sV.wait(p.t, tv)
                ngrp = ns
                Ttok = []
                Mtok = []
                mults_tok = []

                def emit_T(grp):
                    gpar = grp % 2
                    base = grp * 4
                    cnt = min(4, nk - base)
                    if grp >= 2:
                        sV.wait(p.t, mults_tok[grp - 2])
                    for j in range(cnt):
                        tpj = p.t.transpose(p.bankb(4 + gpar)[:, j * 128:(j + 1) * 128],
                                            expb[:, (base + j) * 128:(base + j + 1) * 128],
                                            p.identb[:])
                    Ttok.append(sP.inc(tpj))

                emit_T(0)
                if ngrp > 1:
                    emit_T(1)
                for grp in range(ngrp):
                    gpar = grp % 2
                    base = grp * 4
                    cnt = min(4, nk - base)
                    sP.wait(p.v, Ttok[grp])
                    for j in range(cnt):
                        mv = p.v.tensor_tensor(expT[gpar][:, j * 128:(j + 1) * 128],
                                               p.bankb(4 + gpar)[:, j * 128:(j + 1) * 128],
                                               rzbc[:], ALU.mult)
                    mults_tok.append(sV.inc(mv))
                    sV.wait(p.t, mults_tok[grp])
                    for j in range(cnt):
                        kt = base + j
                        mm = p.t.matmul(p.bank(6, 128), Vt[:, g * S + kt * 128: g * S + kt * 128 + 128],
                                        expT[gpar][:, j * 128:(j + 1) * 128],
                                        start=(kt == 0), stop=(kt == nk - 1))
                    Mtok.append(sP.inc(mm))
                    if grp + 2 < ngrp:
                        emit_T(grp + 2)
                exptok = sP.n
                sP.wait(p.v, Mtok[-1])
                if dbg is not None and "e0" in dbg and h == dbg["_h"] and qi == dbg["_qi"]:
                    p.v.tensor_copy(dbg["_eb"][:], expb[:])
                    p.v.tensor_copy(dbg["_zb"][:], zcol[:])
                    p.v.tensor_copy(dbg["_rb"][:], rzbc[:])
                    sV.inc(p.v.tensor_copy(dbg["_ob"][:], p.bank(6, 128)))
                par = onum % 2
                sO.wait(p.v, otok[par])
                evactok = sV.inc(p.v.tensor_copy(ostg[par][:], p.bank(6, 128)))
                sV.wait(p.sy, evactok)
                otok[par] = sO.inc(p.sy.dma_start(
                    out=O_dram[:, h * S + qi * 128: h * S + qi * 128 + 128], in_=ostg[par][:]), 16)
                onum += 1
    obig = [p.sbuf(f"obig{i}", [128, E], F32, stack=stb) for i in range(2)]
    sO.wait(p.sy)
    btok = [0, 0]
    for c in range(NCH):
        par = c % 2
        sV.wait(p.sy, btok[par])
        tokl = sO.inc(p.sy.dma_start(out=obig[par][:], in_=O_dram[:, c * S:(c + 1) * S]), 16)
        sO.wait(p.v, tokl)
        xsl = x_fm[:, c * S:(c + 1) * S]
        btok[par] = sV.inc(p.v.tensor_tensor(xsl, xsl, obig[par][:], ALU.add))
    sV.wait(p.t)
    p.barrier()
    if dbg is not None and "e0" in dbg:
        sD0 = p.sem("sDbg0")
        sV.wait(p.sy)
        sD0.inc(p.sy.dma_start(out=dbg["e0"], in_=dbg["_eb"][:]), 16)
        sD0.inc(p.sy.dma_start(out=dbg["z0"], in_=dbg["_zb"][:]), 16)
        sD0.inc(p.sy.dma_start(out=dbg["r0"], in_=dbg["_rb"][:]), 16)
        sD0.inc(p.sy.dma_start(out=dbg["o0"], in_=dbg["_ob"][:]), 16)
        if "s0" in dbg:
            sD0.inc(p.sy.dma_start(out=dbg["s0"], in_=dbg["_sb"][:]), 16)
            sD0.inc(p.sy.dma_start(out=dbg["q0"], in_=dbg["_qb"][:]), 16)
        sD0.wait(p.sy)
    if dbg is not None:
        sD = p.sem("sDbg")
        sD.inc(p.sy.dma_start(out=dbg["q"], in_=Qh[:]), 16)
        sD.inc(p.sy.dma_start(out=dbg["k"], in_=Kt[:]), 16)
        sD.inc(p.sy.dma_start(out=dbg["v"], in_=Vt[:]), 16)
        sD.inc(p.sy.dma_start(out=dbg["e"], in_=expb[:]), 16)
        sD.inc(p.sy.dma_start(out=dbg["z"], in_=zcol[:]), 16)
        sD.inc(p.sy.dma_start(out=dbg["rzbc"], in_=rzbc[:]), 16)
        sD.wait(p.sy)
        p.barrier()
    stb.close()



class WStg:

    def __init__(self, p, name, stack, cols=E):
        self.bufs = [p.sbuf(f"{name}_{i}", [128, cols], BF16, stack=stack) for i in range(2)]
        self.tok = [0, 0]
        self.i = 0


def emit_wT_prep(p, dram_rows_ap, wT, sW, sP, sV, stg, nch=NCH, out_off=0, out_stride=128):
    par_s = stg.i % 2
    buf = stg.bufs[par_s]
    sP.wait(p.g, stg.tok[par_s])
    tokw = sW.inc(p.g.dma_start(out=buf[:, 0:nch * 128].rearrange("p (c d) -> p c d", c=nch),
                                in_=dram_rows_ap), 16)
    sW.wait(p.t, tokw)
    for c0 in range(0, nch, 4):
        par = (c0 // 4) % 2
        cnt = min(4, nch - c0)
        sV.wait(p.t, max(0, sV.n - 1))
        for j in range(cnt):
            tp = p.t.transpose(p.bankb(4 + par)[:, j * 128:(j + 1) * 128],
                               buf[:, (c0 + j) * 128:(c0 + j + 1) * 128], p.identb[:])
        tptok = sP.inc(tp)
        sP.wait(p.v, tptok)
        if out_stride == 128:
            o = out_off + c0 * 128
            sV.inc(p.v.tensor_copy(wT[:, o: o + cnt * 128], p.bankb(4 + par)[:, 0:cnt * 128]))
        else:
            for j in range(cnt):
                o = out_off + (c0 + j) * out_stride
                sV.inc(p.v.tensor_copy(wT[:, o: o + 128], p.bankb(4 + par)[:, j * 128:(j + 1) * 128]))
    stg.tok[par_s] = sP.n
    stg.i += 1
    return sV.n


def emit_l0_ffn(p, x_fm, w_gate, w_up, w_down, g_dram, parts="AB"):
    sW = p.sem("f0W")
    sA = p.sem("f0A")
    sP = p.sem("f0P")
    sV = p.sem("f0V")
    sG = p.sem("f0G")

    with ExitStack() as st_:
        xbh = p.sbuf("f0_xbh", [128, NCH * 1024], BF16, stack=st_)
        wgT = [p.sbuf(f"f0_wgT{i}", [128, NCH * 128], BF16, stack=st_) for i in range(4)]
        wuT = [p.sbuf(f"f0_wuT{i}", [128, NCH * 128], BF16, stack=st_) for i in range(4)]
        stgA = WStg(p, "f0_stgA", st_)
        sil = [p.sbuf(f"f0_sil{i}", [128, 512], F32, stack=st_) for i in range(2)]
        gt = [p.sbuf(f"f0_gt{i}", [128, 512], BF16, stack=st_) for i in range(2)]
        siltok = [0, 0]
        gtok = [0, 0]
        en = 0
        for half in range(2):
            t0 = half * 1024
            sP.wait(p.sc)
            ta = 0
            for c in range(NCH):
                ta = sA.inc(p.sc.activation(out=xbh[:, c * 1024:(c + 1) * 1024],
                                            in_=x_fm[:, c * S + t0: c * S + t0 + 1024],
                                            func=AF.Copy))
            sA.wait(p.t, ta)
            for Hq in range(16):
                sA.wait(p.t)
                sV.wait(p.t)
                for i in range(4):
                    h = Hq * 4 + i
                    emit_wT_prep(p, w_gate[h * 128:(h + 1) * 128, :].rearrange("r (c pp) -> r c pp", pp=128),
                                 wgT[i], sW, sP, sV, stgA)
                    emit_wT_prep(p, w_up[h * 128:(h + 1) * 128, :].rearrange("r (c pp) -> r c pp", pp=128),
                                 wuT[i], sW, sP, sV, stgA)
                sV.wait(p.t)
                for tsl in range(2):
                    tb = tsl * 512
                    for c in range(NCH):
                        for i in range(4):
                            p.t.matmul(p.bank(i), wgT[i][:, c * 128:(c + 1) * 128],
                                       xbh[:, c * 1024 + tb: c * 1024 + tb + 512],
                                       start=(c == 0), stop=(c == NCH - 1))
                            mm = p.t.matmul(p.bank(4 + i), wuT[i][:, c * 128:(c + 1) * 128],
                                            xbh[:, c * 1024 + tb: c * 1024 + tb + 512],
                                            start=(c == 0), stop=(c == NCH - 1))
                    tp = sP.inc(mm)
                    sP.wait(p.sc, tp)
                    for i in range(4):
                        h = Hq * 4 + i
                        pa = en % 2
                        en += 1
                        sV.wait(p.sc, siltok[pa])
                        ta = sA.inc(p.sc.activation(out=sil[pa][:], in_=p.bank(i), func=AF.Silu))
                        sA.wait(p.v, ta)
                        sG.wait(p.v, gtok[pa])
                        tv = sV.inc(p.v.tensor_tensor(gt[pa][:], p.bank(4 + i), sil[pa][:], ALU.mult))
                        siltok[pa] = tv
                        sV.wait(p.g, tv)
                        gtok[pa] = sG.inc(p.g.dma_start(out=g_dram[h, :, t0 + tb: t0 + tb + 512],
                                                        in_=gt[pa][:]), 16)
                    sA.wait(p.t)
                    sV.wait(p.t)
    p.barrier()
    if "B" not in parts:
        return

    with ExitStack() as st_:
        wdT = [p.sbuf(f"f0_wdT{i}", [128, NCH * 128], BF16, stack=st_) for i in range(8)]
        gh = [p.sbuf(f"f0_gh{i}", [128, S], BF16, stack=st_) for i in range(8)]
        stgB = WStg(p, "f0_stgB", st_)
        slot_tok = [0] * 8
        half_drain = [0, 0]
        xv = x_fm[:].rearrange("p (ch t) -> p ch t", ch=NCH)
        for hg in range(16):
            hf = hg % 2
            slots = [hf * 4 + i for i in range(4)]
            sV.wait(p.t)
            for k, sl in enumerate(slots):
                h = hg * 4 + k
                sP.wait(p.g, slot_tok[sl])
                sG.inc(p.g.dma_start(out=gh[sl][:], in_=g_dram[h, :, :]), 16)
                emit_wT_prep(p, w_down[:, h * 128:(h + 1) * 128].rearrange("(c pp) h -> pp c h", pp=128),
                             wdT[sl], sW, sP, sV, stgB)
            sV.wait(p.t)
            sG.wait(p.t)
            for win in range(16):
                dq, tsl = win // 4, win % 4
                tb = tsl * 512
                bh = (win % 2) * 4
                sV.wait(p.t, half_drain[win % 2])
                for k, sl in enumerate(slots):
                    for dd in range(4):
                        D = dq * 4 + dd
                        mm = p.t.matmul(p.bank(bh + dd), wdT[sl][:, D * 128:(D + 1) * 128],
                                        gh[sl][:, tb: tb + 512],
                                        start=(k == 0), stop=(k == 3))
                tok_mm = sP.inc(mm)
                sP.wait(p.v, tok_mm)
                half_drain[win % 2] = sV.inc(p.v.tensor_tensor(
                    xv[:, dq * 4:(dq + 1) * 4, tb: tb + 512],
                    xv[:, dq * 4:(dq + 1) * 4, tb: tb + 512],
                    p._arena[:, bh * 512:(bh + 4) * 512].rearrange("p (b t) -> p b t", b=4),
                    ALU.add))
            for sl in slots:
                slot_tok[sl] = sP.n
    p.barrier()


def build_l0_ffn(parts="ABLN"):
    import os
    p = Prog()
    nc = p.nc
    x = nc.dram_tensor("x", [S, E], F32, kind="ExternalInput").ap()
    w_gate = nc.dram_tensor("w_gate", [HID, E], F32, kind="ExternalInput").ap()
    w_up = nc.dram_tensor("w_up", [HID, E], F32, kind="ExternalInput").ap()
    w_down = nc.dram_tensor("w_down", [E, HID], F32, kind="ExternalInput").ap()
    y = nc.dram_tensor("y", [S, E], F32, kind="ExternalOutput").ap()
    gkind = "ExternalOutput" if os.environ.get("DBG") else "Internal"
    g_dram = nc.dram_tensor("g_scratch", [64, 128, S], BF16, kind=gkind).ap()

    x_fm = p.sbuf("x_fm", [128, NCH * S], F32)
    p.setup_consts()
    emit_load_x(p, x, x_fm)
    if "A" in parts:
        emit_l0_ffn(p, x_fm, w_gate, w_up, w_down, g_dram, parts=parts)
    if "LN" in parts:
        emit_ln(p, x_fm, "ln1")
    sST = emit_store_x(p, x_fm, y)
    sST.wait(p.sy)
    return p.finish()



def emit_l1_attn(p, x_fm, w_qkv, w_proj, b_proj, parts="12"):
    sW = p.sem("a1W")
    sA = p.sem("a1A")
    sP = p.sem("a1P")
    sV = p.sem("a1V")

    blockdiag = p.sbuf("a1_bd", [128, 16 * 128], BF16)
    ksums = p.sbuf("a1_ks", [128, 16 * 4], BF16)
    sel4 = p.sbuf("a1_sel4", [4, 128], F32)
    bps = p.sbuf("a1_bps", [128, 16], F32)
    sC = p.sem("a1C")
    g = p.g
    g.memset(blockdiag[:], 0.0)
    g.memset(ksums[:], 0.0)
    g.memset(sel4[:], 1.0)
    g.affine_select(out=sel4[:], in_=sel4[:], compare_op=ALU.is_ge, fill=0.0,
                    base=0, pattern=[[1, 128]], channel_multiplier=-32)
    sC.inc(g.affine_select(out=sel4[:], in_=sel4[:], compare_op=ALU.is_ge, fill=0.0,
                           base=31, pattern=[[-1, 128]], channel_multiplier=32))
    for eng in (p.t, p.v, p.sc):
        sC.wait(eng)
    emit_bias_cols(p, b_proj.rearrange("(c pp) -> c pp", pp=128), bps, 16)
    p.barrier()

    if "1" in parts:
      with ExitStack() as st_:
        wkT = p.sbuf("a1_wkT", [128, NCH * 512], BF16, stack=st_)
        wvT = p.sbuf("a1_wvT", [128, NCH * 512], BF16, stack=st_)
        wstg = WStg(p, "a1_stg1", st_)
        xb = p.sbuf("a1_xb", [128, NCH * 128], BF16, stack=st_)
        k_tm = p.sbuf("a1_ktm", [128, 512], BF16, stack=st_)
        vext = p.sbuf("a1_vext", [128, 4 * 132], BF16, stack=st_)
        sC2 = p.sem("a1C2")
        sC2.inc(p.g.memset(vext[:], 1.0))
        sC2.wait(p.sc)
        for si in range(4):
            for rr in range(4):
                krow = 2048 + si * 512 + rr * 128
                vrow = 4096 + si * 512 + rr * 128
                emit_wT_prep(p, w_qkv[krow:krow + 128, :].rearrange("r (c pp) -> r c pp", pp=128),
                             wkT, sW, sP, sV, wstg,
                             out_off=rr * 128, out_stride=512)
                emit_wT_prep(p, w_qkv[vrow:vrow + 128, :].rearrange("r (c pp) -> r c pp", pp=128),
                             wvT, sW, sP, sV, wstg,
                             out_off=rr * 128, out_stride=512)
            sV.wait(p.t)
            for t128 in range(16):
                tb = t128 * 128
                sP.wait(p.sc)
                ta = 0
                for c in range(NCH):
                    ta = sA.inc(p.sc.activation(out=xb[:, c * 128:(c + 1) * 128],
                                                in_=x_fm[:, c * S + tb: c * S + tb + 128],
                                                func=AF.Copy))
                sA.wait(p.t, ta)
                for c in range(NCH):
                    p.t.matmul(p.bank(0), xb[:, c * 128:(c + 1) * 128], wkT[:, c * 512:(c + 1) * 512],
                               start=(c == 0), stop=(c == NCH - 1))
                    mm = p.t.matmul(p.bank(1), xb[:, c * 128:(c + 1) * 128], wvT[:, c * 512:(c + 1) * 512],
                                    start=(c == 0), stop=(c == NCH - 1))
                tp = sP.inc(mm)
                sP.wait(p.sc, tp)
                ta = sA.inc(p.sc.activation(out=k_tm[:], in_=p.bank(0), func=AF.Relu))
                for q in range(4):
                    ta = sA.inc(p.sc.activation(out=vext[:, q * 132: q * 132 + 128],
                                                in_=p.bank(1, 512)[:, q * 128:(q + 1) * 128],
                                                func=AF.Copy))
                sA.wait(p.t, ta)
                for q in range(4):
                    mm = p.t.matmul(p.bank(2 + q, 129), k_tm[:, q * 128:(q + 1) * 128],
                                    vext[:, q * 132: q * 132 + 129],
                                    start=(t128 == 0), stop=(t128 == 15))
                sP.inc(mm)
            sP.wait(p.v)
            for q in range(4):
                Q = si * 4 + q
                for qp in range(4):
                    b0 = 32 * qp
                    p.v.tensor_copy(blockdiag[b0:b0 + 32, Q * 128 + b0: Q * 128 + b0 + 32],
                                    p.bank(2 + q, 129)[b0:b0 + 32, b0:b0 + 32])
                    tv = sV.inc(p.v.tensor_copy(ksums[b0:b0 + 32, Q * 4 + qp: Q * 4 + qp + 1],
                                                p.bank(2 + q, 129)[b0:b0 + 32, 128:129]))
            sV.wait(p.t, tv)
      p.barrier()

    if "2" in parts:
      with ExitStack() as st_:
        xb5 = p.sbuf("a1_xb5", [128, NCH * 512], BF16, stack=st_)
        wT = p.sbuf("a1_wT", [128, NCH * 128], BF16, stack=st_)
        wstg = WStg(p, "a1_stg2", st_)
        q4 = p.sbuf("a1_q4", [128, 512], BF16, stack=st_)
        den_r = p.sbuf("a1_denr", [4, 512], F32, stack=st_)
        denb_s = p.sbuf("a1_denb", [128, 512], F32, stack=st_)
        o_fm = p.sbuf("a1_ofm", [128, NCH * 512], BF16, stack=st_)
        for tt in range(4):
            tb = tt * 512
            sP.wait(p.sc)
            ta = 0
            for c in range(NCH):
                ta = sA.inc(p.sc.activation(out=xb5[:, c * 512:(c + 1) * 512],
                                            in_=x_fm[:, c * S + tb: c * S + tb + 512],
                                            func=AF.Copy))
            for Q in range(16):
                emit_wT_prep(p, w_qkv[Q * 128:(Q + 1) * 128, :].rearrange("r (c pp) -> r c pp", pp=128),
                             wT, sW, sP, sV, wstg)
                sV.wait(p.t)
                sA.wait(p.t, ta)
                for c in range(NCH):
                    mm = p.t.matmul(p.bank(0), wT[:, c * 128:(c + 1) * 128],
                                    xb5[:, c * 512:(c + 1) * 512],
                                    start=(c == 0), stop=(c == NCH - 1))
                tp = sP.inc(mm)
                sP.wait(p.sc, tp)
                tq = sA.inc(p.sc.activation(out=q4[:], in_=p.bank(0), func=AF.Relu))
                sA.wait(p.t, tq)
                p.t.matmul(p.bank(1), blockdiag[:, Q * 128:(Q + 1) * 128], q4[:],
                           start=True, stop=True)
                tp = sP.inc(p.t.matmul(p.bank(2, 512)[0:4, :], ksums[:, Q * 4:(Q + 1) * 4], q4[:],
                                       start=True, stop=True))
                sP.wait(p.v, tp)
                tv = sV.inc(p.v.tensor_single_scalar(den_r[:], p.bank(2, 512)[0:4, :], 1e-15, ALU.add))
                sV.wait(p.v, tv)
                tv = sV.inc(p.v.reciprocal(den_r[:], den_r[:]))
                sV.wait(p.t, tv)
                tp = sP.inc(p.t.matmul(p.bank(3), sel4[:], den_r[:], start=True, stop=True))
                sP.wait(p.v, tp)
                tv = sV.inc(p.v.tensor_copy(denb_s[:], p.bank(3)))
                sV.wait(p.v, tv)
                tv = sV.inc(p.v.tensor_tensor(o_fm[:, Q * 512:(Q + 1) * 512], p.bank(1), denb_s[:], ALU.mult))
                sV.wait(p.t, tv)
            for D in range(NCH):
                emit_wT_prep(p, w_proj[D * 128:(D + 1) * 128, :].rearrange("r (c pp) -> r c pp", pp=128),
                             wT, sW, sP, sV, wstg)
                sV.wait(p.t)
                for c in range(NCH):
                    mm = p.t.matmul(p.bank(0), wT[:, c * 128:(c + 1) * 128],
                                    o_fm[:, c * 512:(c + 1) * 512],
                                    start=(c == 0), stop=(c == NCH - 1))
                tp = sP.inc(mm)
                sP.wait(p.v, tp)
                sl = x_fm[:, D * S + tb: D * S + tb + 512]
                tv = sV.inc(p.v.scalar_tensor_tensor(sl, p.bank(0), bps[:, D:D + 1], sl,
                                                     ALU.add, ALU.add))
                sV.wait(p.t, tv)
      p.barrier()


def build_l1_attn(parts="12LN"):
    p = Prog()
    nc = p.nc
    x = nc.dram_tensor("x", [S, E], F32, kind="ExternalInput").ap()
    w_qkv = nc.dram_tensor("w_qkv", [3 * E, E], F32, kind="ExternalInput").ap()
    w_proj = nc.dram_tensor("w_proj", [E, E], F32, kind="ExternalInput").ap()
    b_proj = nc.dram_tensor("b_proj", [E], F32, kind="ExternalInput").ap()
    y = nc.dram_tensor("y", [S, E], F32, kind="ExternalOutput").ap()

    x_fm = p.sbuf("x_fm", [128, NCH * S], F32)
    p.setup_consts()
    emit_load_x(p, x, x_fm)
    emit_l1_attn(p, x_fm, w_qkv, w_proj, b_proj, parts=parts)
    if "LN" in parts:
        emit_ln(p, x_fm, "ln2")
    sST = emit_store_x(p, x_fm, y)
    sST.wait(p.sy)
    return p.finish()



def emit_l1_ffn(p, x_fm, w_gate, w_up, w_down, hT_dram, parts="AB"):
    sW = p.sem("f1W")
    sA = p.sem("f1A")
    sP = p.sem("f1P")
    sV = p.sem("f1V")
    sG = p.sem("f1G")

    if "A" in parts:
      with ExitStack() as st_:
        wg_sl = p.sbuf("f1_wg", [128, NCH * 512], BF16, stack=st_)
        wu_sl = p.sbuf("f1_wu", [128, NCH * 512], BF16, stack=st_)
        xb = [p.sbuf(f"f1_xb{i}", [128, NCH * 128], BF16, stack=st_) for i in range(2)]
        sil = [p.sbuf(f"f1_sil{i}", [128, 512], F32, stack=st_) for i in range(2)]
        h = [p.sbuf(f"f1_h{i}", [128, 512], F32, stack=st_) for i in range(2)]
        hm = [p.sbuf(f"f1_hm{i}", [128, 512], BF16, stack=st_) for i in range(2)]
        hTt = [p.sbuf(f"f1_hTt{i}", [128, 512], BF16, stack=st_) for i in range(2)]
        a4 = p.sbuf("f1_a4", [128, 512], F32, stack=st_)
        cmp6 = p.sbuf("f1_cmp", [128, 6 * 128], F32, stack=st_)
        sacc = p.sbuf("f1_sacc", [128, 128], F32, stack=st_)
        keep = p.sbuf("f1_keep", [128, 4 * 128], F32, stack=st_)

        xbtok = [0, 0]
        banktok = [0, 0]
        sparstok = [0, 0]
        dmatok = [0, 0]
        t3tok = [0, 0]
        s3evac = [0]
        wtok = 0

        def stage3(t):
            pt = t % 2
            tb = (t % 16) * 128
            hs = t // 16
            sV.wait(p.t, sparstok[pt])
            sV.wait(p.t, s3evac[0])
            for j in range(4):
                parb = j % 2
                if j >= 2:
                    sV.wait(p.t, sV.n - 1)
                tp = p.t.transpose(p.bankb(6 + parb)[:, 0:128],
                                   hm[pt][:, j * 128:(j + 1) * 128], p.identb[:])
                tpt = sP.inc(tp)
                sP.wait(p.v, tpt)
                if j == 0:
                    sG.wait(p.v, dmatok[pt])
                sV.inc(p.v.tensor_copy(hTt[pt][:, j * 128:(j + 1) * 128], p.bankb(6 + parb)[:, 0:128]))
            tv = sV.n
            s3evac[0] = tv
            sV.wait(p.g, tv)
            dmatok[pt] = sG.inc(p.g.dma_start(
                out=hT_dram[4 * hs: 4 * hs + 4, :, tb: tb + 128].rearrange("j p t -> p j t"),
                in_=hTt[pt][:].rearrange("p (j t) -> p j t", j=4)), 16)
            t3tok[pt] = sP.n

        tglob = 0
        for hs in range(16):
            sP.wait(p.g, wtok)
            sW.inc(p.g.dma_start(
                out=wg_sl[:].rearrange("p (c i) -> p c i", c=NCH),
                in_=w_gate.rearrange("(c p) i -> p c i", p=128)[:, :, hs * 512:(hs + 1) * 512]), 16)
            tokw = sW.inc(p.g.dma_start(
                out=wu_sl[:].rearrange("p (c i) -> p c i", c=NCH),
                in_=w_up.rearrange("(c p) i -> p c i", p=128)[:, :, hs * 512:(hs + 1) * 512]), 16)
            sW.wait(p.t, tokw)
            for t128 in range(16):
                t = tglob
                pt = t % 2
                tb = t128 * 128
                sP.wait(p.sc, xbtok[pt])
                ta = 0
                for c in range(NCH):
                    ta = sA.inc(p.sc.activation(out=xb[pt][:, c * 128:(c + 1) * 128],
                                                in_=x_fm[:, c * S + tb: c * S + tb + 128],
                                                func=AF.Copy))
                sA.wait(p.t, ta)
                sA.wait(p.t, banktok[pt][0] if False else 0)
                bg, bu = pt * 2, pt * 2 + 1
                sV.wait(p.t, banktok[pt])
                for c in range(NCH):
                    p.t.matmul(p.bank(bg), xb[pt][:, c * 128:(c + 1) * 128], wg_sl[:, c * 512:(c + 1) * 512],
                               start=(c == 0), stop=(c == NCH - 1))
                    mm = p.t.matmul(p.bank(bu), xb[pt][:, c * 128:(c + 1) * 128], wu_sl[:, c * 512:(c + 1) * 512],
                                    start=(c == 0), stop=(c == NCH - 1))
                tp = sP.inc(mm)
                xbtok[pt] = tp
                if t128 == 15:
                    wtok = tp
                sP.wait(p.sc, tp)
                sV.wait(p.sc, sparstok[pt])
                ta = sA.inc(p.sc.activation(out=sil[pt][:], in_=p.bank(bg), func=AF.Silu))
                sA.wait(p.v, ta)
                tv = sV.inc(p.v.tensor_tensor(h[pt][:], p.bank(bu), sil[pt][:], ALU.mult))
                banktok[pt] = tv
                sV.wait(p.sc, tv)
                ta = sA.inc(p.sc.activation(out=a4[:], in_=h[pt][:], func=AF.Abs))
                sA.wait(p.v, ta)
                v = p.v
                def a_(i):
                    return a4[:].rearrange("p (g f) -> p g f", f=4)[:, :, i:i + 1].rearrange("p g o -> p (g o)")
                def hs_(i):
                    return h[pt][:].rearrange("p (g f) -> p g f", f=4)[:, :, i:i + 1].rearrange("p g o -> p (g o)")
                def hms_(i):
                    return hm[pt][:].rearrange("p (g f) -> p g f", f=4)[:, :, i:i + 1].rearrange("p g o -> p (g o)")
                pair = [(0, 1), (0, 2), (0, 3), (1, 2), (1, 3), (2, 3)]
                for n, (i, j) in enumerate(pair):
                    v.tensor_tensor(cmp6[:, n * 128:(n + 1) * 128], a_(i), a_(j), ALU.is_ge)
                c01, c02, c03, c12, c13, c23 = [cmp6[:, n * 128:(n + 1) * 128] for n in range(6)]
                v.tensor_tensor(sacc[:], c01, c02, ALU.add)
                v.tensor_tensor(sacc[:], sacc[:], c03, ALU.add)
                v.tensor_single_scalar(keep[:, 0:128], sacc[:], 1.5, ALU.is_ge)
                v.tensor_tensor(sacc[:], c12, c13, ALU.add)
                v.tensor_tensor(sacc[:], sacc[:], c01, ALU.subtract)
                v.tensor_single_scalar(keep[:, 128:256], sacc[:], 0.5, ALU.is_ge)
                v.tensor_tensor(sacc[:], c02, c12, ALU.add)
                v.tensor_tensor(sacc[:], c23, sacc[:], ALU.subtract)
                v.tensor_single_scalar(keep[:, 256:384], sacc[:], -0.5, ALU.is_ge)
                v.tensor_tensor(sacc[:], c03, c13, ALU.add)
                v.tensor_tensor(sacc[:], sacc[:], c23, ALU.add)
                v.tensor_single_scalar(sacc[:], sacc[:], 1.5, ALU.is_ge)
                v.tensor_scalar(keep[:, 384:512], sacc[:], -1.0, 1.0, ALU.mult, ALU.add)
                sP.wait(p.v, t3tok[pt])
                for i in range(4):
                    mmv = v.tensor_tensor(hms_(i), hs_(i), keep[:, i * 128:(i + 1) * 128], ALU.mult)
                sparstok[pt] = sV.inc(mmv)
                if t >= 1:
                    stage3(t - 1)
                tglob += 1
        stage3(tglob - 1)
      p.barrier()

    if "B" in parts:
      with ExitStack() as st_:
        wds = [p.sbuf(f"f1_wds{i}", [128, E], BF16, stack=st_) for i in range(8)]
        gh = [p.sbuf(f"f1_gh{i}", [128, S], BF16, stack=st_) for i in range(8)]
        slot_tok = [0] * 8
        half_drain = [0, 0]
        xv = x_fm[:].rearrange("p (ch t) -> p ch t", ch=NCH)
        for hg in range(16):
            hf = hg % 2
            slots = [hf * 4 + i for i in range(4)]
            for k, sl in enumerate(slots):
                hh = hg * 4 + k
                sP.wait(p.g, slot_tok[sl])
                sW.inc(p.g.dma_start(out=wds[sl][:].rearrange("p (c d) -> p c d", c=NCH),
                                     in_=w_down[hh * 128:(hh + 1) * 128, :].rearrange("r (c d) -> r c d", d=128)), 16)
                sG.inc(p.g.dma_start(out=gh[sl][:], in_=hT_dram[hh, :, :]), 16)
            sW.wait(p.t)
            sG.wait(p.t)
            for win in range(16):
                dq, tsl = win // 4, win % 4
                tb = tsl * 512
                bh = (win % 2) * 4
                sV.wait(p.t, half_drain[win % 2])
                for k, sl in enumerate(slots):
                    for dd in range(4):
                        D = dq * 4 + dd
                        mm = p.t.matmul(p.bank(bh + dd), wds[sl][:, D * 128:(D + 1) * 128],
                                        gh[sl][:, tb: tb + 512],
                                        start=(k == 0), stop=(k == 3))
                tok_mm = sP.inc(mm)
                sP.wait(p.v, tok_mm)
                half_drain[win % 2] = sV.inc(p.v.tensor_tensor(
                    xv[:, dq * 4:(dq + 1) * 4, tb: tb + 512],
                    xv[:, dq * 4:(dq + 1) * 4, tb: tb + 512],
                    p._arena[:, bh * 512:(bh + 4) * 512].rearrange("p (b t) -> p b t", b=4),
                    ALU.add))
            for sl in slots:
                slot_tok[sl] = sP.n
      p.barrier()


def build_l1_ffn(parts="ABLN"):
    import os
    p = Prog()
    nc = p.nc
    x = nc.dram_tensor("x", [S, E], F32, kind="ExternalInput").ap()
    w_gate = nc.dram_tensor("w_gate", [E, HID], F32, kind="ExternalInput").ap()
    w_up = nc.dram_tensor("w_up", [E, HID], F32, kind="ExternalInput").ap()
    w_down = nc.dram_tensor("w_down", [HID, E], F32, kind="ExternalInput").ap()
    y = nc.dram_tensor("y", [S, E], F32, kind="ExternalOutput").ap()
    hkind = "ExternalOutput" if os.environ.get("DBG") else "Internal"
    hT_dram = nc.dram_tensor("hT_scratch", [64, 128, S], BF16, kind=hkind).ap()

    x_fm = p.sbuf("x_fm", [128, NCH * S], F32)
    p.setup_consts()
    emit_load_x(p, x, x_fm)
    emit_l1_ffn(p, x_fm, w_gate, w_up, w_down, hT_dram, parts=parts)
    if "LN" in parts:
        emit_ln(p, x_fm, "ln3")
    sST = emit_store_x(p, x_fm, y)
    sST.wait(p.sy)
    return p.finish()


def build_l0_attn(parts="ABLN"):
    p = Prog()
    nc = p.nc
    x = nc.dram_tensor("x", [S, E], F32, kind="ExternalInput").ap()
    Wq = nc.dram_tensor("Wq", [16, E, 128], F32, kind="ExternalInput").ap()
    bq = nc.dram_tensor("bq", [16, 128], F32, kind="ExternalInput").ap()
    Wk = nc.dram_tensor("Wk", [4, E, 128], F32, kind="ExternalInput").ap()
    bk = nc.dram_tensor("bk", [4, 128], F32, kind="ExternalInput").ap()
    Wv = nc.dram_tensor("Wv", [4, E, 128], F32, kind="ExternalInput").ap()
    bv = nc.dram_tensor("bv", [4, 128], F32, kind="ExternalInput").ap()
    y = nc.dram_tensor("y", [S, E], F32, kind="ExternalOutput").ap()

    x_fm = p.sbuf("x_fm", [128, NCH * S], F32)
    p.setup_consts()
    emit_load_x(p, x, x_fm)
    if "A" in parts:
        import os
        nh = int(os.environ.get("NHEADS", "16"))
        dbg = None
        if os.environ.get("DBG"):
            dbg = {
                "q": nc.dram_tensor("dbg_q", [128, 2 * S], BF16, kind="ExternalOutput").ap(),
                "k": nc.dram_tensor("dbg_k", [128, 4 * S], BF16, kind="ExternalOutput").ap(),
                "v": nc.dram_tensor("dbg_v", [128, 4 * S], BF16, kind="ExternalOutput").ap(),
                "e": nc.dram_tensor("dbg_e", [128, S], BF16, kind="ExternalOutput").ap(),
                "z": nc.dram_tensor("dbg_z", [128, 1], F32, kind="ExternalOutput").ap(),
                "rzbc": nc.dram_tensor("dbg_rzbc", [128, 128], BF16, kind="ExternalOutput").ap(),
            }
            if os.environ.get("DBG_QI") is not None:
                dbg["_h"] = int(os.environ.get("DBG_H", str(nh - 1)))
                dbg["_qi"] = int(os.environ["DBG_QI"])
                dbg["e0"] = nc.dram_tensor("dbg_e0", [128, S], BF16, kind="ExternalOutput").ap()
                dbg["z0"] = nc.dram_tensor("dbg_z0", [128, 1], F32, kind="ExternalOutput").ap()
                dbg["r0"] = nc.dram_tensor("dbg_r0", [128, 128], BF16, kind="ExternalOutput").ap()
                dbg["o0"] = nc.dram_tensor("dbg_o0", [128, 128], F32, kind="ExternalOutput").ap()
                dbg["_eb"] = p.sbuf("dbg_eb", [128, S], BF16)
                dbg["_zb"] = p.sbuf("dbg_zb", [128, 1], F32)
                dbg["_rb"] = p.sbuf("dbg_rb", [128, 128], BF16)
                dbg["_ob"] = p.sbuf("dbg_ob", [128, 128], F32)
                dbg["s0"] = nc.dram_tensor("dbg_s0", [128, 128], F32, kind="ExternalOutput").ap()
                dbg["q0"] = nc.dram_tensor("dbg_q0", [128, 128], BF16, kind="ExternalOutput").ap()
                dbg["_sb"] = p.sbuf("dbg_sb", [128, 128], F32)
                dbg["_qb"] = p.sbuf("dbg_qb", [128, 128], BF16)
        O_dram = nc.dram_tensor("O_scratch", [128, NCH * S], F32, kind="Internal").ap()
        emit_l0_attn(p, x_fm, Wq, bq, Wk, bk, Wv, bv, O_dram=O_dram, parts=parts, nheads=nh, dbg=dbg)
    if "LN" in parts:
        emit_ln(p, x_fm, "ln0")
    sST = emit_store_x(p, x_fm, y)
    sST.wait(p.sy)
    return p.finish()


def build_fused():
    p = Prog()
    nc = p.nc
    x = nc.dram_tensor("x", [S, E], F32, kind="ExternalInput").ap()
    Wq = nc.dram_tensor("Wq", [16, E, 128], F32, kind="ExternalInput").ap()
    bq = nc.dram_tensor("bq", [16, 128], F32, kind="ExternalInput").ap()
    Wk = nc.dram_tensor("Wk", [4, E, 128], F32, kind="ExternalInput").ap()
    bk = nc.dram_tensor("bk", [4, 128], F32, kind="ExternalInput").ap()
    Wv = nc.dram_tensor("Wv", [4, E, 128], F32, kind="ExternalInput").ap()
    bv = nc.dram_tensor("bv", [4, 128], F32, kind="ExternalInput").ap()
    w_gate0 = nc.dram_tensor("w_gate0", [HID, E], F32, kind="ExternalInput").ap()
    w_up0 = nc.dram_tensor("w_up0", [HID, E], F32, kind="ExternalInput").ap()
    w_down0 = nc.dram_tensor("w_down0", [E, HID], F32, kind="ExternalInput").ap()
    w_qkv = nc.dram_tensor("w_qkv", [3 * E, E], F32, kind="ExternalInput").ap()
    w_proj = nc.dram_tensor("w_proj", [E, E], F32, kind="ExternalInput").ap()
    b_proj = nc.dram_tensor("b_proj", [E], F32, kind="ExternalInput").ap()
    w_gate1 = nc.dram_tensor("w_gate1", [E, HID], F32, kind="ExternalInput").ap()
    w_up1 = nc.dram_tensor("w_up1", [E, HID], F32, kind="ExternalInput").ap()
    w_down1 = nc.dram_tensor("w_down1", [HID, E], F32, kind="ExternalInput").ap()
    y = nc.dram_tensor("y", [S, E], F32, kind="ExternalOutput").ap()
    O_dram = nc.dram_tensor("O_scratch", [128, NCH * S], F32, kind="Internal").ap()
    g_dram = nc.dram_tensor("g_scratch", [64, 128, S], BF16, kind="Internal").ap()
    hT_dram = nc.dram_tensor("hT_scratch", [64, 128, S], BF16, kind="Internal").ap()

    x_fm = p.sbuf("x_fm", [128, NCH * S], F32)
    p.setup_consts()
    with nc.named_scope("loadx"):
        emit_load_x(p, x, x_fm)
    with nc.named_scope("l0attn"):
        emit_l0_attn(p, x_fm, Wq, bq, Wk, bk, Wv, bv, O_dram=O_dram)
    with nc.named_scope("ln0"):
        emit_ln(p, x_fm, "ln0")
    with nc.named_scope("l0ffn"):
        emit_l0_ffn(p, x_fm, w_gate0, w_up0, w_down0, g_dram)
    with nc.named_scope("ln1"):
        emit_ln(p, x_fm, "ln1")
    with nc.named_scope("l1attn"):
        emit_l1_attn(p, x_fm, w_qkv, w_proj, b_proj)
    with nc.named_scope("ln2"):
        emit_ln(p, x_fm, "ln2")
    with nc.named_scope("l1ffn"):
        emit_l1_ffn(p, x_fm, w_gate1, w_up1, w_down1, hT_dram)
    with nc.named_scope("ln3"):
        emit_ln(p, x_fm, "ln3")
    with nc.named_scope("storex"):
        sST = emit_store_x(p, x_fm, y)
    sST.wait(p.sy)
    return p.finish()


def kernel(**inputs):
    g = lambda k: np.ascontiguousarray(inputs[k])
    x = g("L0_attn_x")
    n = 8
    nc = build_fused()
    maps = [{"x": np.ascontiguousarray(x[b]),
             "Wq": g("L0_attn_Wq"), "bq": g("L0_attn_bq"),
             "Wk": g("L0_attn_Wk"), "bk": g("L0_attn_bk"),
             "Wv": g("L0_attn_Wv"), "bv": g("L0_attn_bv"),
             "w_gate0": g("L0_ffn_w_gate"), "w_up0": g("L0_ffn_w_up"),
             "w_down0": g("L0_ffn_w_down"),
             "w_qkv": g("L1_attn_w_qkv"), "w_proj": g("L1_attn_w_proj"),
             "b_proj": g("L1_attn_b_proj"),
             "w_gate1": g("L1_ffn_w_gate"), "w_up1": g("L1_ffn_w_up"),
             "w_down1": g("L1_ffn_w_down")} for b in range(n)]
    r = run_bass_kernel_spmd(nc, maps, core_ids=list(range(n)))
    return np.stack([np.asarray(r.results[b]["y"]) for b in range(n)], axis=0)


def kernel_unfused(**inputs):
    g = lambda k: np.ascontiguousarray(inputs[k])
    x = g("L0_attn_x")
    n = 8
    cores = list(range(n))

    nc1 = build_l0_attn()
    maps1 = [{"x": np.ascontiguousarray(x[b]), "Wq": g("L0_attn_Wq"), "bq": g("L0_attn_bq"),
              "Wk": g("L0_attn_Wk"), "bk": g("L0_attn_bk"),
              "Wv": g("L0_attn_Wv"), "bv": g("L0_attn_bv")} for b in range(n)]
    r1 = run_bass_kernel_spmd(nc1, maps1, core_ids=cores)
    x1 = [r1.results[b]["y"] for b in range(n)]

    nc2 = build_l0_ffn()
    maps2 = [{"x": x1[b], "w_gate": g("L0_ffn_w_gate"), "w_up": g("L0_ffn_w_up"),
              "w_down": g("L0_ffn_w_down")} for b in range(n)]
    r2 = run_bass_kernel_spmd(nc2, maps2, core_ids=cores)
    x2 = [r2.results[b]["y"] for b in range(n)]

    nc3 = build_l1_attn()
    maps3 = [{"x": x2[b], "w_qkv": g("L1_attn_w_qkv"), "w_proj": g("L1_attn_w_proj"),
              "b_proj": g("L1_attn_b_proj")} for b in range(n)]
    r3 = run_bass_kernel_spmd(nc3, maps3, core_ids=cores)
    x3 = [r3.results[b]["y"] for b in range(n)]

    nc4 = build_l1_ffn()
    maps4 = [{"x": x3[b], "w_gate": g("L1_ffn_w_gate"), "w_up": g("L1_ffn_w_up"),
              "w_down": g("L1_ffn_w_down")} for b in range(n)]
    r4 = run_bass_kernel_spmd(nc4, maps4, core_ids=cores)
    return np.stack([np.asarray(r4.results[b]["y"]) for b in range(n)], axis=0)


# version 08
# speedup vs baseline: 1.4529x; 1.0348x over previous
import numpy as np
from contextlib import ExitStack

import concourse.bass as bass
import concourse.mybir as mybir
from concourse.bass_utils import run_bass_kernel_spmd

F32 = mybir.dt.float32
BF16 = mybir.dt.bfloat16
AF = mybir.ActivationFunctionType
ALU = mybir.AluOpType
AX = mybir.AxisListType

S = 2048
E = 2048
NCH = 16
HID = 8192
QSCALE = float(1.0 / np.sqrt(128.0))
LN_EPS = 1e-5


class Sem:
    __slots__ = ("h", "n")

    def __init__(self, nc, name=None):
        self.h = nc.alloc_semaphore(name=name)
        self.n = 0

    def inc(self, instr, k=1):
        instr.then_inc(self.h, k)
        self.n += k
        return self.n

    def wait(self, eng, tok=None):
        tok = self.n if tok is None else tok
        if tok > 0:
            eng.wait_ge(self.h, tok)


class Prog:
    def __init__(self):
        nc = bass.Bass("TRN2", target_bir_lowering=False)
        self.nc = nc
        self.t = nc.tensor
        self.v = nc.vector
        self.sc = nc.scalar
        self.g = nc.gpsimd
        self.sy = nc.sync
        self.ctx = ExitStack()
        self.sems = {}
        self.PS = self.sbuf_like = None
        self._arena = self.ctx.enter_context(nc.psum_tensor("arena", [128, 8 * 512], F32))

    def bank(self, i, cols=512):
        return self._arena[:, i * 512: i * 512 + cols]

    def bankb(self, i, cols=1024):
        return self._arena[:, i * 512:(i + 1) * 512].bitcast(BF16)[:, 0:cols]

    def bankrow(self, i, cols=512):
        return self._arena[0:1, i * 512: i * 512 + cols]

    def sem(self, name):
        if name not in self.sems:
            self.sems[name] = Sem(self.nc, name)
        return self.sems[name]

    def sbuf(self, name, shape, dtype, stack=None):
        return (stack or self.ctx).enter_context(self.nc.sbuf_tensor(name, shape, dtype))

    def barrier(self):
        self.nc.all_engine_barrier()

    def setup_consts(self):
        p = self
        self.identf = p.sbuf("identf", [128, 128], F32)
        self.identb = p.sbuf("identb", [128, 128], BF16)
        self.mask01 = p.sbuf("mask01", [128, 128], BF16)
        self.ones_col_b = p.sbuf("ones_col_b", [128, 1], BF16)
        self.ones_row_f = p.sbuf("ones_row_f", [1, 128], F32)
        sC = p.sem("sConst")
        g = p.g
        g.memset(self.identf[:], 0.0)
        g.affine_select(out=self.identf[:], in_=self.identf[:],
                        compare_op=ALU.not_equal, fill=1.0, base=0,
                        pattern=[[-1, 128]], channel_multiplier=1)
        g.memset(self.identb[:], 0.0)
        g.affine_select(out=self.identb[:], in_=self.identb[:],
                        compare_op=ALU.not_equal, fill=1.0, base=0,
                        pattern=[[-1, 128]], channel_multiplier=1)
        g.memset(self.mask01[:], 1.0)
        g.affine_select(out=self.mask01[:], in_=self.mask01[:],
                        compare_op=ALU.is_ge, fill=0.0, base=0,
                        pattern=[[-1, 128]], channel_multiplier=1)
        g.memset(self.ones_col_b[:], 1.0)
        sC.inc(g.memset(self.ones_row_f[:], 1.0))
        for eng in (p.t, p.v, p.sc):
            sC.wait(eng)

    def finish(self):
        self.ctx.close()
        return self.nc



def emit_load_x(p, x_dram, x_fm):
    st_ = ExitStack()
    stage = p.sbuf("ld_stage", [128, E], F32, stack=st_)
    sDMA = p.sem("sLdX")
    sPE = p.sem("sLdPE")
    sDVE = p.sem("sLdDVE")
    ctok = [0, 0]
    for tc in range(S // 128):
        tok = sDMA.inc(p.sy.dma_start(out=stage[:], in_=x_dram[tc * 128:(tc + 1) * 128, :]), 16)
        sDMA.wait(p.t, tok)
        for c in range(NCH):
            par = c % 2
            sDVE.wait(p.t, ctok[par])
            tp = sPE.inc(p.t.transpose(p.bank(par, 128), stage[:, c * 128:(c + 1) * 128], p.identf[:]))
            sPE.wait(p.v, tp)
            ctok[par] = sDVE.inc(p.v.tensor_copy(
                x_fm[:, c * S + tc * 128: c * S + tc * 128 + 128], p.bank(par, 128)))
        sPE.wait(p.sy)
    p.barrier()
    st_.close()


def emit_store_x(p, x_fm, y_dram):
    stage = p.sbuf("st_stage", [128, E], F32)
    sPE = p.sem("sStPE")
    sDVE = p.sem("sStDVE")
    sST = p.sem("sStDMA")
    ctok = [0, 0]
    for tc in range(S // 128):
        sST.wait(p.v)
        for c in range(NCH):
            par = c % 2
            sDVE.wait(p.t, ctok[par])
            tp = sPE.inc(p.t.transpose(p.bank(par, 128),
                                       x_fm[:, c * S + tc * 128: c * S + tc * 128 + 128],
                                       p.identf[:]))
            sPE.wait(p.v, tp)
            ctok[par] = sDVE.inc(p.v.tensor_copy(stage[:, c * 128:(c + 1) * 128], p.bank(par, 128)))
        sDVE.wait(p.sy)
        sST.inc(p.sy.dma_start(out=y_dram[tc * 128:(tc + 1) * 128, :], in_=stage[:]), 16)
    return sST


def emit_bias_cols(p, dram_2d, out_tile, rows, scale=None):
    st_ = ExitStack()
    raw = p.sbuf(f"braw_{out_tile.name}", [128, 128], F32, stack=st_)
    sB = p.sem("sBias")
    tokz = sB.inc(p.g.memset(raw[:], 0.0))
    sB.wait(p.sy, tokz)
    tok = sB.inc(p.sy.dma_start(out=raw[0:rows, :], in_=dram_2d), 16)
    sB.wait(p.t, tok)
    tp = sB.inc(p.t.transpose(p.bank(7, 128), raw[:], p.identf[:]))
    sB.wait(p.v, tp)
    if scale is None:
        sB.inc(p.v.tensor_copy(out_tile[:], p.bank(7, 128)[:, 0:rows]))
    else:
        sB.inc(p.v.tensor_scalar_mul(out_tile[:], p.bank(7, 128)[:, 0:rows], scale))
    sB.wait(p.t)
    sB.wait(p.v)
    sB.wait(p.g)
    sB.wait(p.sy)
    st_.close()


def emit_ln(p, x_fm, tag):
    st_ = ExitStack()
    rbf = [p.sbuf(f"{tag}_rbf{i}", [128, 512], BF16, stack=st_) for i in range(2)]
    rsq = [p.sbuf(f"{tag}_rsq{i}", [128, 512], BF16, stack=st_) for i in range(2)]
    rowA = p.sbuf(f"{tag}_rowA", [1, S], F32, stack=st_)
    rowB = p.sbuf(f"{tag}_rowB", [1, S], F32, stack=st_)
    rowC = p.sbuf(f"{tag}_rowC", [1, S], F32, stack=st_)
    sA = p.sem(f"{tag}A")
    sP = p.sem(f"{tag}P")
    sV = p.sem(f"{tag}V")
    ps0 = p.bankrow(0)
    ps1 = p.bankrow(1)
    casttok = [0, 0]
    for tt in range(4):
        for c in range(NCH):
            par = c % 2
            sP.wait(p.sc, casttok[par])
            p.sc.activation(out=rbf[par][:], in_=x_fm[:, c * S + tt * 512: c * S + tt * 512 + 512], func=AF.Copy)
            ta = sA.inc(p.sc.activation(out=rsq[par][:], in_=x_fm[:, c * S + tt * 512: c * S + tt * 512 + 512], func=AF.Square))
            sA.wait(p.t, ta)
            p.t.matmul(ps0, p.ones_col_b[:], rbf[par][:], start=(c == 0), stop=(c == NCH - 1))
            casttok[par] = sP.inc(p.t.matmul(ps1, p.ones_col_b[:], rsq[par][:], start=(c == 0), stop=(c == NCH - 1)))
        sP.wait(p.v)
        p.v.tensor_copy(rowA[0:1, tt * 512: tt * 512 + 512], ps0)
        tv = sV.inc(p.v.tensor_copy(rowB[0:1, tt * 512: tt * 512 + 512], ps1))
        sV.wait(p.t, tv)
    def step(instr):
        tok = sV.inc(instr)
        sV.wait(p.v, tok)
        return tok
    step(p.v.tensor_scalar_mul(rowA[:], rowA[:], 1.0 / E))
    step(p.v.tensor_mul(rowC[:], rowA[:], rowA[:]))
    tokA = sV.inc(p.v.tensor_scalar_mul(rowA[:], rowA[:], -1.0))
    sV.wait(p.v, tokA)
    tokB = sV.inc(p.v.scalar_tensor_tensor(rowB[:], rowB[:], 1.0 / E, rowC[:], ALU.mult, ALU.subtract))
    sV.wait(p.v, tokB)
    tokB = sV.inc(p.v.tensor_single_scalar(rowB[:], rowB[:], LN_EPS, ALU.add))
    sV.wait(p.sc, tokB)
    ta = sA.inc(p.sc.activation(out=rowB[:], in_=rowB[:], func=AF.Sqrt))
    sA.wait(p.v, ta)
    step(p.v.reciprocal(rowB[:], rowB[:]))
    sV.wait(p.t)
    normtok = [0, 0]
    for tt in range(4):
        bp = (tt % 2) * 2
        sV.wait(p.t, normtok[tt % 2])
        p.t.matmul(p.bank(bp), p.ones_row_f[:], rowA[0:1, tt * 512: tt * 512 + 512], start=True, stop=True)
        tp = sP.inc(p.t.matmul(p.bank(bp + 1), p.ones_row_f[:], rowB[0:1, tt * 512: tt * 512 + 512], start=True, stop=True))
        sP.wait(p.v, tp)
        for c in range(NCH):
            sl = x_fm[:, c * S + tt * 512: c * S + tt * 512 + 512]
            p.v.tensor_tensor(sl, sl, p.bank(bp), ALU.add)
            tv = sV.inc(p.v.tensor_tensor(sl, sl, p.bank(bp + 1), ALU.mult))
        normtok[tt % 2] = tv
    sV.wait(p.t)
    sV.wait(p.sc)
    sV.wait(p.g)
    p.barrier()
    st_.close()



def emit_l0_attn(p, x_fm, Wq, bq, Wk, bk, Wv, bv, O_dram=None, parts="AB", nheads=16, dbg=None):
    stb = ExitStack()
    Kt = p.sbuf("Kt", [128, 4 * S], BF16, stack=stb)
    Vt = p.sbuf("Vt", [128, 4 * S], BF16, stack=stb)
    xb = [p.sbuf(f"at_xb{i}", [128, 512], BF16, stack=stb) for i in range(2)]
    expb = p.sbuf("expb", [128, S], BF16, stack=stb)
    expT = [p.sbuf(f"expT{i}", [128, 512], BF16, stack=stb) for i in range(2)]
    pmax = p.sbuf("pmax", [128, 4], F32, stack=stb)
    negmax = p.sbuf("negmax", [128, 1], F32, stack=stb)
    zcol = p.sbuf("zcol", [128, 1], F32, stack=stb)
    rz = p.sbuf("rz", [128, 1], F32, stack=stb)
    zrow = p.sbuf("zrow", [1, 128], F32, stack=stb)
    rzbc = p.sbuf("rzbc", [128, 128], BF16, stack=stb)
    bqs = p.sbuf("bqs", [128, 16], F32, stack=stb)
    bks = p.sbuf("bks", [128, 4], F32, stack=stb)
    bvs = p.sbuf("bvs", [128, 4], F32, stack=stb)

    sW = p.sem("atW")
    sA = p.sem("atA")
    sP = p.sem("atP")
    sV = p.sem("atV")

    emit_bias_cols(p, bq, bqs, 16, scale=QSCALE)
    emit_bias_cols(p, bk, bks, 4)
    emit_bias_cols(p, bv, bvs, 4)
    p.barrier()

    with ExitStack() as st_:
        wkv = [p.sbuf(f"wkv{i}", [128, E], BF16, stack=st_) for i in range(4)]
        Vf = p.sbuf("Vf", [128, 2 * S], BF16, stack=st_)
        wkv_lastuse = 0
        for sweep in range(2):
            g0 = sweep * 2
            sP.wait(p.g, wkv_lastuse)
            for i, (W, g) in enumerate(((Wk, g0), (Wk, g0 + 1), (Wv, g0), (Wv, g0 + 1))):
                sW.inc(p.g.dma_start(out=wkv[i][:].rearrange("p (c d) -> p c d", c=NCH),
                                     in_=W[g].rearrange("(c p) d -> p c d", p=128)), 16)
            sW.wait(p.t)
            casttok = [0, 0]
            for tt in range(4):
                for c in range(NCH):
                    par = c % 2
                    sP.wait(p.sc, casttok[par])
                    ta = sA.inc(p.sc.activation(out=xb[par][:],
                                                in_=x_fm[:, c * S + tt * 512: c * S + tt * 512 + 512],
                                                func=AF.Copy))
                    sA.wait(p.t, ta)
                    for o in range(4):
                        mm = p.t.matmul(p.bank(o), wkv[o][:, c * 128:(c + 1) * 128], xb[par][:],
                                        start=(c == 0), stop=(c == NCH - 1))
                    casttok[par] = sP.inc(mm)
                wkv_lastuse = sP.n
                sP.wait(p.sc)
                for o in range(4):
                    g = g0 + (o % 2)
                    if o < 2:
                        dst = Kt[:, g * S + tt * 512: g * S + tt * 512 + 512]
                        bias = bks[:, g:g + 1]
                    else:
                        dst = Vf[:, (o - 2) * S + tt * 512: (o - 2) * S + tt * 512 + 512]
                        bias = bvs[:, g:g + 1]
                    ta = sA.inc(p.sc.activation(out=dst, in_=p.bank(o), func=AF.Identity, bias=bias))
                sA.wait(p.t, ta)
            ctokT = [0, 0]
            for gg in range(2):
                g = g0 + gg
                for kt in range(NCH):
                    par = kt % 2
                    sV.wait(p.t, ctokT[par])
                    tp = sP.inc(p.t.transpose(p.bankb(4 + par)[:, 0:128],
                                              Vf[:, gg * S + kt * 128: gg * S + kt * 128 + 128],
                                              p.identb[:]))
                    sP.wait(p.v, tp)
                    ctokT[par] = sV.inc(p.v.tensor_copy(
                        Vt[:, g * S + kt * 128: g * S + kt * 128 + 128], p.bankb(4 + par)[:, 0:128]))
            sV.wait(p.t)
    p.barrier()

    Qh = p.sbuf("Qh", [128, 2 * S], BF16, stack=stb)
    wq = [p.sbuf(f"wq{i}", [128, E], BF16, stack=stb) for i in range(2)]
    ostg = [p.sbuf(f"ostg{i}", [128, 128], F32, stack=stb) for i in range(2)]
    sO = p.sem("atO")
    otok = [0, 0]
    onum = 0
    if "B" not in parts:
        p.barrier()
        stb.close()
        return
    wq_lastuse = 0
    for hp in range(max(0, nheads) // 2):
        heads = (2 * hp, 2 * hp + 1)
        sV.wait(p.t)
        sP.wait(p.g, wq_lastuse)
        for i, h in enumerate(heads):
            sW.inc(p.g.dma_start(out=wq[i][:].rearrange("p (c d) -> p c d", c=NCH),
                                 in_=Wq[h].rearrange("(c p) d -> p c d", p=128)), 16)
        sW.wait(p.t)
        casttok = [0, 0]
        for tt in range(4):
            for c in range(NCH):
                par = c % 2
                sP.wait(p.sc, casttok[par])
                ta = sA.inc(p.sc.activation(out=xb[par][:],
                                            in_=x_fm[:, c * S + tt * 512: c * S + tt * 512 + 512],
                                            func=AF.Copy))
                sA.wait(p.t, ta)
                for i in range(2):
                    mm = p.t.matmul(p.bank(6 + i), wq[i][:, c * 128:(c + 1) * 128], xb[par][:],
                                    start=(c == 0), stop=(c == NCH - 1))
                casttok[par] = sP.inc(mm)
            wq_lastuse = sP.n
            sP.wait(p.sc)
            for i, h in enumerate(heads):
                ta = sA.inc(p.sc.activation(out=Qh[:, i * S + tt * 512: i * S + tt * 512 + 512],
                                            in_=p.bank(6 + i), func=AF.Identity,
                                            scale=QSCALE, bias=bqs[:, h:h + 1]))
            sA.wait(p.t, ta)
        for i, h in enumerate(heads):
            g = h // 4
            exptok = 0
            evactok = 0
            for qi in range(16):
                nk = qi + 1
                ns = (nk + 3) // 4
                sA.wait(p.t)
                sV.wait(p.t, evactok)
                sctok = []
                for s in range(ns):
                    cols = min(512, nk * 128 - s * 512)
                    mm = p.t.matmul(p.bank(s, cols), Qh[:, i * S + qi * 128: i * S + qi * 128 + 128],
                                    Kt[:, g * S + s * 512: g * S + s * 512 + cols],
                                    start=True, stop=True)
                    sctok.append(sP.inc(mm))
                for s in range(ns):
                    cols = min(512, nk * 128 - s * 512)
                    sP.wait(p.v, sctok[s])
                    mr = p.v.reduce_max(pmax[:, s:s + 1], p.bank(s, cols), axis=AX.X)
                tv = sV.inc(mr)
                sV.wait(p.v, tv)
                tv = sV.inc(p.v.reduce_max(negmax[:], pmax[:, 0:ns], axis=AX.X, negate=True))
                if dbg is not None and "s0" in dbg and h == dbg["_h"] and qi == dbg["_qi"]:
                    p.v.tensor_copy(dbg["_sb"][:], p.bank(0, 128))
                    sV.inc(p.v.tensor_copy(dbg["_qb"][:], Qh[:, i * S + qi * 128: i * S + qi * 128 + 128]))
                sV.wait(p.sc, tv)
                sP.wait(p.sc, exptok)
                for s in range(ns):
                    cols = min(512, nk * 128 - s * 512)
                    ta = sA.inc(p.sc.activation(out=expb[:, s * 512: s * 512 + cols],
                                                in_=p.bank(s, cols), func=AF.Exp,
                                                bias=negmax[:, 0:1]))
                sA.wait(p.v, ta)
                dsl = expb[:, qi * 128: qi * 128 + 128]
                tv = sV.inc(p.v.tensor_tensor(dsl, dsl, p.mask01[:], ALU.mult))
                sV.wait(p.v, tv)
                tv = sV.inc(p.v.reduce_sum(zcol[:], expb[:, 0:nk * 128], axis=AX.X))
                sV.wait(p.v, tv)
                tv = sV.inc(p.v.reciprocal(rz[:], zcol[:]))
                sV.wait(p.t, tv)
                tp = sP.inc(p.t.transpose(p.bankrow(4, 128), rz[:], p.identf[:]))
                sP.wait(p.v, tp)
                tv = sV.inc(p.v.tensor_copy(zrow[:], p.bankrow(4, 128)))
                sV.wait(p.t, tv)
                tp = sP.inc(p.t.matmul(p.bank(5, 128), p.ones_row_f[:], zrow[:], start=True, stop=True))
                sP.wait(p.v, tp)
                tv = sV.inc(p.v.tensor_copy(rzbc[:], p.bank(5, 128)))
                sV.wait(p.t, tv)
                ngrp = ns
                Ttok = []
                Mtok = []
                mults_tok = []

                def emit_T(grp):
                    gpar = grp % 2
                    base = grp * 4
                    cnt = min(4, nk - base)
                    if grp >= 2:
                        sV.wait(p.t, mults_tok[grp - 2])
                    for j in range(cnt):
                        tpj = p.t.transpose(p.bankb(4 + gpar)[:, j * 128:(j + 1) * 128],
                                            expb[:, (base + j) * 128:(base + j + 1) * 128],
                                            p.identb[:])
                    Ttok.append(sP.inc(tpj))

                emit_T(0)
                if ngrp > 1:
                    emit_T(1)
                for grp in range(ngrp):
                    gpar = grp % 2
                    base = grp * 4
                    cnt = min(4, nk - base)
                    sP.wait(p.v, Ttok[grp])
                    for j in range(cnt):
                        mv = p.v.tensor_tensor(expT[gpar][:, j * 128:(j + 1) * 128],
                                               p.bankb(4 + gpar)[:, j * 128:(j + 1) * 128],
                                               rzbc[:], ALU.mult)
                    mults_tok.append(sV.inc(mv))
                    sV.wait(p.t, mults_tok[grp])
                    for j in range(cnt):
                        kt = base + j
                        mm = p.t.matmul(p.bank(6, 128), Vt[:, g * S + kt * 128: g * S + kt * 128 + 128],
                                        expT[gpar][:, j * 128:(j + 1) * 128],
                                        start=(kt == 0), stop=(kt == nk - 1))
                    Mtok.append(sP.inc(mm))
                    if grp + 2 < ngrp:
                        emit_T(grp + 2)
                exptok = sP.n
                sP.wait(p.v, Mtok[-1])
                if dbg is not None and "e0" in dbg and h == dbg["_h"] and qi == dbg["_qi"]:
                    p.v.tensor_copy(dbg["_eb"][:], expb[:])
                    p.v.tensor_copy(dbg["_zb"][:], zcol[:])
                    p.v.tensor_copy(dbg["_rb"][:], rzbc[:])
                    sV.inc(p.v.tensor_copy(dbg["_ob"][:], p.bank(6, 128)))
                par = onum % 2
                sO.wait(p.v, otok[par])
                evactok = sV.inc(p.v.tensor_copy(ostg[par][:], p.bank(6, 128)))
                sV.wait(p.sy, evactok)
                otok[par] = sO.inc(p.sy.dma_start(
                    out=O_dram[:, h * S + qi * 128: h * S + qi * 128 + 128], in_=ostg[par][:]), 16)
                onum += 1
    obig = [p.sbuf(f"obig{i}", [128, E], F32, stack=stb) for i in range(2)]
    sO.wait(p.sy)
    btok = [0, 0]
    for c in range(NCH):
        par = c % 2
        sV.wait(p.sy, btok[par])
        tokl = sO.inc(p.sy.dma_start(out=obig[par][:], in_=O_dram[:, c * S:(c + 1) * S]), 16)
        sO.wait(p.v, tokl)
        xsl = x_fm[:, c * S:(c + 1) * S]
        btok[par] = sV.inc(p.v.tensor_tensor(xsl, xsl, obig[par][:], ALU.add))
    sV.wait(p.t)
    p.barrier()
    if dbg is not None and "e0" in dbg:
        sD0 = p.sem("sDbg0")
        sV.wait(p.sy)
        sD0.inc(p.sy.dma_start(out=dbg["e0"], in_=dbg["_eb"][:]), 16)
        sD0.inc(p.sy.dma_start(out=dbg["z0"], in_=dbg["_zb"][:]), 16)
        sD0.inc(p.sy.dma_start(out=dbg["r0"], in_=dbg["_rb"][:]), 16)
        sD0.inc(p.sy.dma_start(out=dbg["o0"], in_=dbg["_ob"][:]), 16)
        if "s0" in dbg:
            sD0.inc(p.sy.dma_start(out=dbg["s0"], in_=dbg["_sb"][:]), 16)
            sD0.inc(p.sy.dma_start(out=dbg["q0"], in_=dbg["_qb"][:]), 16)
        sD0.wait(p.sy)
    if dbg is not None:
        sD = p.sem("sDbg")
        sD.inc(p.sy.dma_start(out=dbg["q"], in_=Qh[:]), 16)
        sD.inc(p.sy.dma_start(out=dbg["k"], in_=Kt[:]), 16)
        sD.inc(p.sy.dma_start(out=dbg["v"], in_=Vt[:]), 16)
        sD.inc(p.sy.dma_start(out=dbg["e"], in_=expb[:]), 16)
        sD.inc(p.sy.dma_start(out=dbg["z"], in_=zcol[:]), 16)
        sD.inc(p.sy.dma_start(out=dbg["rzbc"], in_=rzbc[:]), 16)
        sD.wait(p.sy)
        p.barrier()
    stb.close()



class WStg:

    def __init__(self, p, name, stack, cols=E):
        self.bufs = [p.sbuf(f"{name}_{i}", [128, cols], BF16, stack=stack) for i in range(2)]
        self.tok = [0, 0]
        self.i = 0


def emit_wT_prep(p, dram_rows_ap, wT, sW, sP, sV, stg, nch=NCH, out_off=0, out_stride=128):
    par_s = stg.i % 2
    buf = stg.bufs[par_s]
    sP.wait(p.g, stg.tok[par_s])
    tokw = sW.inc(p.g.dma_start(out=buf[:, 0:nch * 128].rearrange("p (c d) -> p c d", c=nch),
                                in_=dram_rows_ap), 16)
    sW.wait(p.t, tokw)
    for c0 in range(0, nch, 4):
        par = (c0 // 4) % 2
        cnt = min(4, nch - c0)
        sV.wait(p.t, max(0, sV.n - 1))
        for j in range(cnt):
            tp = p.t.transpose(p.bankb(4 + par)[:, j * 128:(j + 1) * 128],
                               buf[:, (c0 + j) * 128:(c0 + j + 1) * 128], p.identb[:])
        tptok = sP.inc(tp)
        sP.wait(p.v, tptok)
        if out_stride == 128:
            o = out_off + c0 * 128
            sV.inc(p.v.tensor_copy(wT[:, o: o + cnt * 128], p.bankb(4 + par)[:, 0:cnt * 128]))
        else:
            for j in range(cnt):
                o = out_off + (c0 + j) * out_stride
                sV.inc(p.v.tensor_copy(wT[:, o: o + 128], p.bankb(4 + par)[:, j * 128:(j + 1) * 128]))
    stg.tok[par_s] = sP.n
    stg.i += 1
    return sV.n


def emit_l0_ffn(p, x_fm, w_gate, w_up, w_down, g_dram, parts="AB"):
    sW = p.sem("f0W")
    sA = p.sem("f0A")
    sP = p.sem("f0P")
    sV = p.sem("f0V")
    sG = p.sem("f0G")

    with ExitStack() as st_:
        xbh = p.sbuf("f0_xbh", [128, NCH * 1024], BF16, stack=st_)
        wgT = [p.sbuf(f"f0_wgT{i}", [128, NCH * 128], BF16, stack=st_) for i in range(4)]
        wuT = [p.sbuf(f"f0_wuT{i}", [128, NCH * 128], BF16, stack=st_) for i in range(4)]
        stgA = WStg(p, "f0_stgA", st_)
        sil = [p.sbuf(f"f0_sil{i}", [128, 512], F32, stack=st_) for i in range(2)]
        gt = [p.sbuf(f"f0_gt{i}", [128, 512], BF16, stack=st_) for i in range(2)]
        siltok = [0, 0]
        gtok = [0, 0]
        en = 0
        for half in range(2):
            t0 = half * 1024
            sP.wait(p.sc)
            ta = 0
            for c in range(NCH):
                ta = sA.inc(p.sc.activation(out=xbh[:, c * 1024:(c + 1) * 1024],
                                            in_=x_fm[:, c * S + t0: c * S + t0 + 1024],
                                            func=AF.Copy))
            sA.wait(p.t, ta)
            for Hq in range(16):
                sA.wait(p.t)
                sV.wait(p.t)
                for i in range(4):
                    h = Hq * 4 + i
                    emit_wT_prep(p, w_gate[h * 128:(h + 1) * 128, :].rearrange("r (c pp) -> r c pp", pp=128),
                                 wgT[i], sW, sP, sV, stgA)
                    emit_wT_prep(p, w_up[h * 128:(h + 1) * 128, :].rearrange("r (c pp) -> r c pp", pp=128),
                                 wuT[i], sW, sP, sV, stgA)
                sV.wait(p.t)
                for tsl in range(2):
                    tb = tsl * 512
                    for c in range(NCH):
                        for i in range(4):
                            p.t.matmul(p.bank(i), wgT[i][:, c * 128:(c + 1) * 128],
                                       xbh[:, c * 1024 + tb: c * 1024 + tb + 512],
                                       start=(c == 0), stop=(c == NCH - 1))
                            mm = p.t.matmul(p.bank(4 + i), wuT[i][:, c * 128:(c + 1) * 128],
                                            xbh[:, c * 1024 + tb: c * 1024 + tb + 512],
                                            start=(c == 0), stop=(c == NCH - 1))
                    tp = sP.inc(mm)
                    sP.wait(p.sc, tp)
                    for i in range(4):
                        h = Hq * 4 + i
                        pa = en % 2
                        en += 1
                        sV.wait(p.sc, siltok[pa])
                        ta = sA.inc(p.sc.activation(out=sil[pa][:], in_=p.bank(i), func=AF.Silu))
                        sA.wait(p.v, ta)
                        sG.wait(p.v, gtok[pa])
                        tv = sV.inc(p.v.tensor_tensor(gt[pa][:], p.bank(4 + i), sil[pa][:], ALU.mult))
                        siltok[pa] = tv
                        sV.wait(p.g, tv)
                        gtok[pa] = sG.inc(p.g.dma_start(out=g_dram[h, :, t0 + tb: t0 + tb + 512],
                                                        in_=gt[pa][:]), 16)
                    sA.wait(p.t)
                    sV.wait(p.t)
    p.barrier()
    if "B" not in parts:
        return

    with ExitStack() as st_:
        wdT = [p.sbuf(f"f0_wdT{i}", [128, NCH * 128], BF16, stack=st_) for i in range(8)]
        gh = [p.sbuf(f"f0_gh{i}", [128, S], BF16, stack=st_) for i in range(8)]
        stgB = WStg(p, "f0_stgB", st_)
        slot_tok = [0] * 8
        half_drain = [0, 0]
        xv = x_fm[:].rearrange("p (ch t) -> p ch t", ch=NCH)
        for hg in range(16):
            hf = hg % 2
            slots = [hf * 4 + i for i in range(4)]
            sV.wait(p.t)
            for k, sl in enumerate(slots):
                h = hg * 4 + k
                sP.wait(p.g, slot_tok[sl])
                sG.inc(p.g.dma_start(out=gh[sl][:], in_=g_dram[h, :, :]), 16)
                emit_wT_prep(p, w_down[:, h * 128:(h + 1) * 128].rearrange("(c pp) h -> pp c h", pp=128),
                             wdT[sl], sW, sP, sV, stgB)
            sV.wait(p.t)
            sG.wait(p.t)
            for win in range(16):
                dq, tsl = win // 4, win % 4
                tb = tsl * 512
                bh = (win % 2) * 4
                sV.wait(p.t, half_drain[win % 2])
                for k, sl in enumerate(slots):
                    for dd in range(4):
                        D = dq * 4 + dd
                        mm = p.t.matmul(p.bank(bh + dd), wdT[sl][:, D * 128:(D + 1) * 128],
                                        gh[sl][:, tb: tb + 512],
                                        start=(k == 0), stop=(k == 3))
                tok_mm = sP.inc(mm)
                sP.wait(p.v, tok_mm)
                half_drain[win % 2] = sV.inc(p.v.tensor_tensor(
                    xv[:, dq * 4:(dq + 1) * 4, tb: tb + 512],
                    xv[:, dq * 4:(dq + 1) * 4, tb: tb + 512],
                    p._arena[:, bh * 512:(bh + 4) * 512].rearrange("p (b t) -> p b t", b=4),
                    ALU.add))
            for sl in slots:
                slot_tok[sl] = sP.n
    p.barrier()


def build_l0_ffn(parts="ABLN"):
    import os
    p = Prog()
    nc = p.nc
    x = nc.dram_tensor("x", [S, E], F32, kind="ExternalInput").ap()
    w_gate = nc.dram_tensor("w_gate", [HID, E], F32, kind="ExternalInput").ap()
    w_up = nc.dram_tensor("w_up", [HID, E], F32, kind="ExternalInput").ap()
    w_down = nc.dram_tensor("w_down", [E, HID], F32, kind="ExternalInput").ap()
    y = nc.dram_tensor("y", [S, E], F32, kind="ExternalOutput").ap()
    gkind = "ExternalOutput" if os.environ.get("DBG") else "Internal"
    g_dram = nc.dram_tensor("g_scratch", [64, 128, S], BF16, kind=gkind).ap()

    x_fm = p.sbuf("x_fm", [128, NCH * S], F32)
    p.setup_consts()
    emit_load_x(p, x, x_fm)
    if "A" in parts:
        emit_l0_ffn(p, x_fm, w_gate, w_up, w_down, g_dram, parts=parts)
    if "LN" in parts:
        emit_ln(p, x_fm, "ln1")
    sST = emit_store_x(p, x_fm, y)
    sST.wait(p.sy)
    return p.finish()



def emit_l1_attn(p, x_fm, w_qkv, w_proj, b_proj, parts="12"):
    sW = p.sem("a1W")
    sA = p.sem("a1A")
    sP = p.sem("a1P")
    sV = p.sem("a1V")

    blockdiag = p.sbuf("a1_bd", [128, 16 * 128], BF16)
    ksums = p.sbuf("a1_ks", [128, 16 * 4], BF16)
    sel4 = p.sbuf("a1_sel4", [4, 128], F32)
    bps = p.sbuf("a1_bps", [128, 16], F32)
    sC = p.sem("a1C")
    g = p.g
    g.memset(blockdiag[:], 0.0)
    g.memset(ksums[:], 0.0)
    g.memset(sel4[:], 1.0)
    g.affine_select(out=sel4[:], in_=sel4[:], compare_op=ALU.is_ge, fill=0.0,
                    base=0, pattern=[[1, 128]], channel_multiplier=-32)
    sC.inc(g.affine_select(out=sel4[:], in_=sel4[:], compare_op=ALU.is_ge, fill=0.0,
                           base=31, pattern=[[-1, 128]], channel_multiplier=32))
    for eng in (p.t, p.v, p.sc):
        sC.wait(eng)
    emit_bias_cols(p, b_proj.rearrange("(c pp) -> c pp", pp=128), bps, 16)
    p.barrier()

    if "1" in parts:
      with ExitStack() as st_:
        wkT = p.sbuf("a1_wkT", [128, NCH * 512], BF16, stack=st_)
        wvT = p.sbuf("a1_wvT", [128, NCH * 512], BF16, stack=st_)
        wstg = WStg(p, "a1_stg1", st_)
        xb = [p.sbuf(f"a1_xb{i}", [128, NCH * 128], BF16, stack=st_) for i in range(2)]
        k_tm = p.sbuf("a1_ktm", [128, 512], BF16, stack=st_)
        vext = p.sbuf("a1_vext", [128, 4 * 132], BF16, stack=st_)
        sC2 = p.sem("a1C2")
        sC2.inc(p.g.memset(vext[:], 1.0))
        sC2.wait(p.sc)
        xbtok1 = [0, 0]

        def casts1(t):
            pt = t % 2
            tb = (t % 16) * 128
            sP.wait(p.sc, xbtok1[pt])
            ta = 0
            for c in range(NCH):
                ta = sA.inc(p.sc.activation(out=xb[pt][:, c * 128:(c + 1) * 128],
                                            in_=x_fm[:, c * S + tb: c * S + tb + 128],
                                            func=AF.Copy))
            return ta

        ct1 = casts1(0)
        tg1 = 0
        for si in range(4):
            for rr in range(4):
                krow = 2048 + si * 512 + rr * 128
                vrow = 4096 + si * 512 + rr * 128
                emit_wT_prep(p, w_qkv[krow:krow + 128, :].rearrange("r (c pp) -> r c pp", pp=128),
                             wkT, sW, sP, sV, wstg,
                             out_off=rr * 128, out_stride=512)
                emit_wT_prep(p, w_qkv[vrow:vrow + 128, :].rearrange("r (c pp) -> r c pp", pp=128),
                             wvT, sW, sP, sV, wstg,
                             out_off=rr * 128, out_stride=512)
            sV.wait(p.t)
            for t128 in range(16):
                tb = t128 * 128
                t = tg1
                pt = t % 2
                sA.wait(p.t, ct1)
                for c in range(NCH):
                    p.t.matmul(p.bank(0), xb[pt][:, c * 128:(c + 1) * 128], wkT[:, c * 512:(c + 1) * 512],
                               start=(c == 0), stop=(c == NCH - 1))
                    mm = p.t.matmul(p.bank(1), xb[pt][:, c * 128:(c + 1) * 128], wvT[:, c * 512:(c + 1) * 512],
                                    start=(c == 0), stop=(c == NCH - 1))
                xbtok1[pt] = sP.n + 1
                if tg1 + 1 < 64:
                    ct1 = casts1(tg1 + 1)
                tg1 += 1
                tp = sP.inc(mm)
                sP.wait(p.sc, tp)
                ta = sA.inc(p.sc.activation(out=k_tm[:], in_=p.bank(0), func=AF.Relu))
                for q in range(4):
                    ta = sA.inc(p.sc.activation(out=vext[:, q * 132: q * 132 + 128],
                                                in_=p.bank(1, 512)[:, q * 128:(q + 1) * 128],
                                                func=AF.Copy))
                sA.wait(p.t, ta)
                for q in range(4):
                    mm = p.t.matmul(p.bank(2 + q, 129), k_tm[:, q * 128:(q + 1) * 128],
                                    vext[:, q * 132: q * 132 + 129],
                                    start=(t128 == 0), stop=(t128 == 15))
                sP.inc(mm)
            sP.wait(p.v)
            for q in range(4):
                Q = si * 4 + q
                for qp in range(4):
                    b0 = 32 * qp
                    p.v.tensor_copy(blockdiag[b0:b0 + 32, Q * 128 + b0: Q * 128 + b0 + 32],
                                    p.bank(2 + q, 129)[b0:b0 + 32, b0:b0 + 32])
                    tv = sV.inc(p.v.tensor_copy(ksums[b0:b0 + 32, Q * 4 + qp: Q * 4 + qp + 1],
                                                p.bank(2 + q, 129)[b0:b0 + 32, 128:129]))
            sV.wait(p.t, tv)
      p.barrier()

    if "2" in parts:
      with ExitStack() as st_:
        xb5 = p.sbuf("a1_xb5", [128, NCH * 512], BF16, stack=st_)
        wT = p.sbuf("a1_wT", [128, NCH * 128], BF16, stack=st_)
        wstg = WStg(p, "a1_stg2", st_)
        q4 = p.sbuf("a1_q4", [128, 512], BF16, stack=st_)
        den_r = p.sbuf("a1_denr", [4, 512], F32, stack=st_)
        denb_s = p.sbuf("a1_denb", [128, 512], F32, stack=st_)
        o_fm = p.sbuf("a1_ofm", [128, NCH * 512], BF16, stack=st_)
        for tt in range(4):
            tb = tt * 512
            sP.wait(p.sc)
            ta = 0
            for c in range(NCH):
                ta = sA.inc(p.sc.activation(out=xb5[:, c * 512:(c + 1) * 512],
                                            in_=x_fm[:, c * S + tb: c * S + tb + 512],
                                            func=AF.Copy))
            for Q in range(16):
                emit_wT_prep(p, w_qkv[Q * 128:(Q + 1) * 128, :].rearrange("r (c pp) -> r c pp", pp=128),
                             wT, sW, sP, sV, wstg)
                sV.wait(p.t)
                sA.wait(p.t, ta)
                for c in range(NCH):
                    mm = p.t.matmul(p.bank(0), wT[:, c * 128:(c + 1) * 128],
                                    xb5[:, c * 512:(c + 1) * 512],
                                    start=(c == 0), stop=(c == NCH - 1))
                tp = sP.inc(mm)
                sP.wait(p.sc, tp)
                tq = sA.inc(p.sc.activation(out=q4[:], in_=p.bank(0), func=AF.Relu))
                sA.wait(p.t, tq)
                p.t.matmul(p.bank(1), blockdiag[:, Q * 128:(Q + 1) * 128], q4[:],
                           start=True, stop=True)
                tp = sP.inc(p.t.matmul(p.bank(2, 512)[0:4, :], ksums[:, Q * 4:(Q + 1) * 4], q4[:],
                                       start=True, stop=True))
                sP.wait(p.v, tp)
                tv = sV.inc(p.v.tensor_single_scalar(den_r[:], p.bank(2, 512)[0:4, :], 1e-15, ALU.add))
                sV.wait(p.v, tv)
                tv = sV.inc(p.v.reciprocal(den_r[:], den_r[:]))
                sV.wait(p.t, tv)
                tp = sP.inc(p.t.matmul(p.bank(3), sel4[:], den_r[:], start=True, stop=True))
                sP.wait(p.v, tp)
                tv = sV.inc(p.v.tensor_copy(denb_s[:], p.bank(3)))
                sV.wait(p.v, tv)
                tv = sV.inc(p.v.tensor_tensor(o_fm[:, Q * 512:(Q + 1) * 512], p.bank(1), denb_s[:], ALU.mult))
                sV.wait(p.t, tv)
            for D in range(NCH):
                emit_wT_prep(p, w_proj[D * 128:(D + 1) * 128, :].rearrange("r (c pp) -> r c pp", pp=128),
                             wT, sW, sP, sV, wstg)
                sV.wait(p.t)
                for c in range(NCH):
                    mm = p.t.matmul(p.bank(0), wT[:, c * 128:(c + 1) * 128],
                                    o_fm[:, c * 512:(c + 1) * 512],
                                    start=(c == 0), stop=(c == NCH - 1))
                tp = sP.inc(mm)
                sP.wait(p.v, tp)
                sl = x_fm[:, D * S + tb: D * S + tb + 512]
                tv = sV.inc(p.v.scalar_tensor_tensor(sl, p.bank(0), bps[:, D:D + 1], sl,
                                                     ALU.add, ALU.add))
                sV.wait(p.t, tv)
      p.barrier()


def build_l1_attn(parts="12LN"):
    p = Prog()
    nc = p.nc
    x = nc.dram_tensor("x", [S, E], F32, kind="ExternalInput").ap()
    w_qkv = nc.dram_tensor("w_qkv", [3 * E, E], F32, kind="ExternalInput").ap()
    w_proj = nc.dram_tensor("w_proj", [E, E], F32, kind="ExternalInput").ap()
    b_proj = nc.dram_tensor("b_proj", [E], F32, kind="ExternalInput").ap()
    y = nc.dram_tensor("y", [S, E], F32, kind="ExternalOutput").ap()

    x_fm = p.sbuf("x_fm", [128, NCH * S], F32)
    p.setup_consts()
    emit_load_x(p, x, x_fm)
    emit_l1_attn(p, x_fm, w_qkv, w_proj, b_proj, parts=parts)
    if "LN" in parts:
        emit_ln(p, x_fm, "ln2")
    sST = emit_store_x(p, x_fm, y)
    sST.wait(p.sy)
    return p.finish()



def emit_l1_ffn(p, x_fm, w_gate, w_up, w_down, hT_dram, parts="AB"):
    sW = p.sem("f1W")
    sA = p.sem("f1A")
    sP = p.sem("f1P")
    sV = p.sem("f1V")
    sG = p.sem("f1G")

    if "A" in parts:
      with ExitStack() as st_:
        wg_sl = p.sbuf("f1_wg", [128, NCH * 512], BF16, stack=st_)
        wu_sl = p.sbuf("f1_wu", [128, NCH * 512], BF16, stack=st_)
        xb = [p.sbuf(f"f1_xb{i}", [128, NCH * 128], BF16, stack=st_) for i in range(2)]
        sil = [p.sbuf(f"f1_sil{i}", [128, 512], F32, stack=st_) for i in range(2)]
        h = [p.sbuf(f"f1_h{i}", [128, 512], F32, stack=st_) for i in range(2)]
        hm = [p.sbuf(f"f1_hm{i}", [128, 512], BF16, stack=st_) for i in range(2)]
        hTt = [p.sbuf(f"f1_hTt{i}", [128, 512], BF16, stack=st_) for i in range(2)]
        a4 = p.sbuf("f1_a4", [128, 512], F32, stack=st_)
        cmp6 = p.sbuf("f1_cmp", [128, 6 * 128], F32, stack=st_)
        sacc = p.sbuf("f1_sacc", [128, 128], F32, stack=st_)
        keep = p.sbuf("f1_keep", [128, 4 * 128], F32, stack=st_)

        xbtok = [0, 0]
        banktok = [0, 0]
        sparstok = [0, 0]
        dmatok = [0, 0]
        t3tok = [0, 0]
        s3evac = [0]
        wtok = 0

        def stage3(t):
            pt = t % 2
            tb = (t % 16) * 128
            hs = t // 16
            sV.wait(p.t, sparstok[pt])
            sV.wait(p.t, s3evac[0])
            for j in range(4):
                parb = j % 2
                if j >= 2:
                    sV.wait(p.t, sV.n - 1)
                tp = p.t.transpose(p.bankb(6 + parb)[:, 0:128],
                                   hm[pt][:, j * 128:(j + 1) * 128], p.identb[:])
                tpt = sP.inc(tp)
                sP.wait(p.v, tpt)
                if j == 0:
                    sG.wait(p.v, dmatok[pt])
                sV.inc(p.v.tensor_copy(hTt[pt][:, j * 128:(j + 1) * 128], p.bankb(6 + parb)[:, 0:128]))
            tv = sV.n
            s3evac[0] = tv
            sV.wait(p.g, tv)
            dmatok[pt] = sG.inc(p.g.dma_start(
                out=hT_dram[4 * hs: 4 * hs + 4, :, tb: tb + 128].rearrange("j p t -> p j t"),
                in_=hTt[pt][:].rearrange("p (j t) -> p j t", j=4)), 16)
            t3tok[pt] = sP.n

        def emit_casts(t):
            pt = t % 2
            hs_, t128 = t // 16, t % 16
            tb = t128 * 128
            sP.wait(p.sc, xbtok[pt])
            ta = 0
            for c in range(NCH):
                ta = sA.inc(p.sc.activation(out=xb[pt][:, c * 128:(c + 1) * 128],
                                            in_=x_fm[:, c * S + tb: c * S + tb + 128],
                                            func=AF.Copy))
            return ta

        casttok_a = emit_casts(0)
        tglob = 0
        for hs in range(16):
            sP.wait(p.g, wtok)
            sW.inc(p.g.dma_start(
                out=wg_sl[:].rearrange("p (c i) -> p c i", c=NCH),
                in_=w_gate.rearrange("(c p) i -> p c i", p=128)[:, :, hs * 512:(hs + 1) * 512]), 16)
            tokw = sW.inc(p.g.dma_start(
                out=wu_sl[:].rearrange("p (c i) -> p c i", c=NCH),
                in_=w_up.rearrange("(c p) i -> p c i", p=128)[:, :, hs * 512:(hs + 1) * 512]), 16)
            sW.wait(p.t, tokw)
            for t128 in range(16):
                t = tglob
                pt = t % 2
                sA.wait(p.t, casttok_a)
                bg, bu = pt * 2, pt * 2 + 1
                sV.wait(p.t, banktok[pt])
                for c in range(NCH):
                    p.t.matmul(p.bank(bg), xb[pt][:, c * 128:(c + 1) * 128], wg_sl[:, c * 512:(c + 1) * 512],
                               start=(c == 0), stop=(c == NCH - 1))
                    mm = p.t.matmul(p.bank(bu), xb[pt][:, c * 128:(c + 1) * 128], wu_sl[:, c * 512:(c + 1) * 512],
                                    start=(c == 0), stop=(c == NCH - 1))
                tp = sP.inc(mm)
                xbtok[pt] = tp
                if t128 == 15:
                    wtok = tp
                if t + 1 < 256:
                    casttok_a = emit_casts(t + 1)
                sP.wait(p.sc, tp)
                sV.wait(p.sc, sparstok[pt])
                ta = sA.inc(p.sc.activation(out=sil[pt][:], in_=p.bank(bg), func=AF.Silu))
                sA.wait(p.v, ta)
                tv = sV.inc(p.v.tensor_tensor(h[pt][:], p.bank(bu), sil[pt][:], ALU.mult))
                banktok[pt] = tv
                sV.wait(p.sc, tv)
                ta = sA.inc(p.sc.activation(out=a4[:], in_=h[pt][:], func=AF.Abs))
                sA.wait(p.v, ta)
                v = p.v
                def a_(i):
                    return a4[:].rearrange("p (g f) -> p g f", f=4)[:, :, i:i + 1].rearrange("p g o -> p (g o)")
                def hs_(i):
                    return h[pt][:].rearrange("p (g f) -> p g f", f=4)[:, :, i:i + 1].rearrange("p g o -> p (g o)")
                def hms_(i):
                    return hm[pt][:].rearrange("p (g f) -> p g f", f=4)[:, :, i:i + 1].rearrange("p g o -> p (g o)")
                pair = [(0, 1), (0, 2), (0, 3), (1, 2), (1, 3), (2, 3)]
                for n, (i, j) in enumerate(pair):
                    v.tensor_tensor(cmp6[:, n * 128:(n + 1) * 128], a_(i), a_(j), ALU.is_ge)
                c01, c02, c03, c12, c13, c23 = [cmp6[:, n * 128:(n + 1) * 128] for n in range(6)]
                v.tensor_tensor(sacc[:], c01, c02, ALU.add)
                v.tensor_tensor(sacc[:], sacc[:], c03, ALU.add)
                v.tensor_single_scalar(keep[:, 0:128], sacc[:], 1.5, ALU.is_ge)
                v.tensor_tensor(sacc[:], c12, c13, ALU.add)
                v.tensor_tensor(sacc[:], sacc[:], c01, ALU.subtract)
                v.tensor_single_scalar(keep[:, 128:256], sacc[:], 0.5, ALU.is_ge)
                v.tensor_tensor(sacc[:], c02, c12, ALU.add)
                v.tensor_tensor(sacc[:], c23, sacc[:], ALU.subtract)
                v.tensor_single_scalar(keep[:, 256:384], sacc[:], -0.5, ALU.is_ge)
                v.tensor_tensor(sacc[:], c03, c13, ALU.add)
                v.tensor_tensor(sacc[:], sacc[:], c23, ALU.add)
                v.tensor_single_scalar(sacc[:], sacc[:], 1.5, ALU.is_ge)
                v.tensor_scalar(keep[:, 384:512], sacc[:], -1.0, 1.0, ALU.mult, ALU.add)
                sP.wait(p.v, t3tok[pt])
                for i in range(4):
                    mmv = v.tensor_tensor(hms_(i), hs_(i), keep[:, i * 128:(i + 1) * 128], ALU.mult)
                sparstok[pt] = sV.inc(mmv)
                if t >= 1:
                    stage3(t - 1)
                tglob += 1
        stage3(tglob - 1)
      p.barrier()

    if "B" in parts:
      with ExitStack() as st_:
        wds = [p.sbuf(f"f1_wds{i}", [128, E], BF16, stack=st_) for i in range(8)]
        gh = [p.sbuf(f"f1_gh{i}", [128, S], BF16, stack=st_) for i in range(8)]
        slot_tok = [0] * 8
        half_drain = [0, 0]
        xv = x_fm[:].rearrange("p (ch t) -> p ch t", ch=NCH)
        for hg in range(16):
            hf = hg % 2
            slots = [hf * 4 + i for i in range(4)]
            for k, sl in enumerate(slots):
                hh = hg * 4 + k
                sP.wait(p.g, slot_tok[sl])
                sW.inc(p.g.dma_start(out=wds[sl][:].rearrange("p (c d) -> p c d", c=NCH),
                                     in_=w_down[hh * 128:(hh + 1) * 128, :].rearrange("r (c d) -> r c d", d=128)), 16)
                sG.inc(p.g.dma_start(out=gh[sl][:], in_=hT_dram[hh, :, :]), 16)
            sW.wait(p.t)
            sG.wait(p.t)
            for win in range(16):
                dq, tsl = win // 4, win % 4
                tb = tsl * 512
                bh = (win % 2) * 4
                sV.wait(p.t, half_drain[win % 2])
                for k, sl in enumerate(slots):
                    for dd in range(4):
                        D = dq * 4 + dd
                        mm = p.t.matmul(p.bank(bh + dd), wds[sl][:, D * 128:(D + 1) * 128],
                                        gh[sl][:, tb: tb + 512],
                                        start=(k == 0), stop=(k == 3))
                tok_mm = sP.inc(mm)
                sP.wait(p.v, tok_mm)
                half_drain[win % 2] = sV.inc(p.v.tensor_tensor(
                    xv[:, dq * 4:(dq + 1) * 4, tb: tb + 512],
                    xv[:, dq * 4:(dq + 1) * 4, tb: tb + 512],
                    p._arena[:, bh * 512:(bh + 4) * 512].rearrange("p (b t) -> p b t", b=4),
                    ALU.add))
            for sl in slots:
                slot_tok[sl] = sP.n
      p.barrier()


def build_l1_ffn(parts="ABLN"):
    import os
    p = Prog()
    nc = p.nc
    x = nc.dram_tensor("x", [S, E], F32, kind="ExternalInput").ap()
    w_gate = nc.dram_tensor("w_gate", [E, HID], F32, kind="ExternalInput").ap()
    w_up = nc.dram_tensor("w_up", [E, HID], F32, kind="ExternalInput").ap()
    w_down = nc.dram_tensor("w_down", [HID, E], F32, kind="ExternalInput").ap()
    y = nc.dram_tensor("y", [S, E], F32, kind="ExternalOutput").ap()
    hkind = "ExternalOutput" if os.environ.get("DBG") else "Internal"
    hT_dram = nc.dram_tensor("hT_scratch", [64, 128, S], BF16, kind=hkind).ap()

    x_fm = p.sbuf("x_fm", [128, NCH * S], F32)
    p.setup_consts()
    emit_load_x(p, x, x_fm)
    emit_l1_ffn(p, x_fm, w_gate, w_up, w_down, hT_dram, parts=parts)
    if "LN" in parts:
        emit_ln(p, x_fm, "ln3")
    sST = emit_store_x(p, x_fm, y)
    sST.wait(p.sy)
    return p.finish()


def build_l0_attn(parts="ABLN"):
    p = Prog()
    nc = p.nc
    x = nc.dram_tensor("x", [S, E], F32, kind="ExternalInput").ap()
    Wq = nc.dram_tensor("Wq", [16, E, 128], F32, kind="ExternalInput").ap()
    bq = nc.dram_tensor("bq", [16, 128], F32, kind="ExternalInput").ap()
    Wk = nc.dram_tensor("Wk", [4, E, 128], F32, kind="ExternalInput").ap()
    bk = nc.dram_tensor("bk", [4, 128], F32, kind="ExternalInput").ap()
    Wv = nc.dram_tensor("Wv", [4, E, 128], F32, kind="ExternalInput").ap()
    bv = nc.dram_tensor("bv", [4, 128], F32, kind="ExternalInput").ap()
    y = nc.dram_tensor("y", [S, E], F32, kind="ExternalOutput").ap()

    x_fm = p.sbuf("x_fm", [128, NCH * S], F32)
    p.setup_consts()
    emit_load_x(p, x, x_fm)
    if "A" in parts:
        import os
        nh = int(os.environ.get("NHEADS", "16"))
        dbg = None
        if os.environ.get("DBG"):
            dbg = {
                "q": nc.dram_tensor("dbg_q", [128, 2 * S], BF16, kind="ExternalOutput").ap(),
                "k": nc.dram_tensor("dbg_k", [128, 4 * S], BF16, kind="ExternalOutput").ap(),
                "v": nc.dram_tensor("dbg_v", [128, 4 * S], BF16, kind="ExternalOutput").ap(),
                "e": nc.dram_tensor("dbg_e", [128, S], BF16, kind="ExternalOutput").ap(),
                "z": nc.dram_tensor("dbg_z", [128, 1], F32, kind="ExternalOutput").ap(),
                "rzbc": nc.dram_tensor("dbg_rzbc", [128, 128], BF16, kind="ExternalOutput").ap(),
            }
            if os.environ.get("DBG_QI") is not None:
                dbg["_h"] = int(os.environ.get("DBG_H", str(nh - 1)))
                dbg["_qi"] = int(os.environ["DBG_QI"])
                dbg["e0"] = nc.dram_tensor("dbg_e0", [128, S], BF16, kind="ExternalOutput").ap()
                dbg["z0"] = nc.dram_tensor("dbg_z0", [128, 1], F32, kind="ExternalOutput").ap()
                dbg["r0"] = nc.dram_tensor("dbg_r0", [128, 128], BF16, kind="ExternalOutput").ap()
                dbg["o0"] = nc.dram_tensor("dbg_o0", [128, 128], F32, kind="ExternalOutput").ap()
                dbg["_eb"] = p.sbuf("dbg_eb", [128, S], BF16)
                dbg["_zb"] = p.sbuf("dbg_zb", [128, 1], F32)
                dbg["_rb"] = p.sbuf("dbg_rb", [128, 128], BF16)
                dbg["_ob"] = p.sbuf("dbg_ob", [128, 128], F32)
                dbg["s0"] = nc.dram_tensor("dbg_s0", [128, 128], F32, kind="ExternalOutput").ap()
                dbg["q0"] = nc.dram_tensor("dbg_q0", [128, 128], BF16, kind="ExternalOutput").ap()
                dbg["_sb"] = p.sbuf("dbg_sb", [128, 128], F32)
                dbg["_qb"] = p.sbuf("dbg_qb", [128, 128], BF16)
        O_dram = nc.dram_tensor("O_scratch", [128, NCH * S], F32, kind="Internal").ap()
        emit_l0_attn(p, x_fm, Wq, bq, Wk, bk, Wv, bv, O_dram=O_dram, parts=parts, nheads=nh, dbg=dbg)
    if "LN" in parts:
        emit_ln(p, x_fm, "ln0")
    sST = emit_store_x(p, x_fm, y)
    sST.wait(p.sy)
    return p.finish()


def build_fused():
    p = Prog()
    nc = p.nc
    x = nc.dram_tensor("x", [S, E], F32, kind="ExternalInput").ap()
    Wq = nc.dram_tensor("Wq", [16, E, 128], F32, kind="ExternalInput").ap()
    bq = nc.dram_tensor("bq", [16, 128], F32, kind="ExternalInput").ap()
    Wk = nc.dram_tensor("Wk", [4, E, 128], F32, kind="ExternalInput").ap()
    bk = nc.dram_tensor("bk", [4, 128], F32, kind="ExternalInput").ap()
    Wv = nc.dram_tensor("Wv", [4, E, 128], F32, kind="ExternalInput").ap()
    bv = nc.dram_tensor("bv", [4, 128], F32, kind="ExternalInput").ap()
    w_gate0 = nc.dram_tensor("w_gate0", [HID, E], F32, kind="ExternalInput").ap()
    w_up0 = nc.dram_tensor("w_up0", [HID, E], F32, kind="ExternalInput").ap()
    w_down0 = nc.dram_tensor("w_down0", [E, HID], F32, kind="ExternalInput").ap()
    w_qkv = nc.dram_tensor("w_qkv", [3 * E, E], F32, kind="ExternalInput").ap()
    w_proj = nc.dram_tensor("w_proj", [E, E], F32, kind="ExternalInput").ap()
    b_proj = nc.dram_tensor("b_proj", [E], F32, kind="ExternalInput").ap()
    w_gate1 = nc.dram_tensor("w_gate1", [E, HID], F32, kind="ExternalInput").ap()
    w_up1 = nc.dram_tensor("w_up1", [E, HID], F32, kind="ExternalInput").ap()
    w_down1 = nc.dram_tensor("w_down1", [HID, E], F32, kind="ExternalInput").ap()
    y = nc.dram_tensor("y", [S, E], F32, kind="ExternalOutput").ap()
    O_dram = nc.dram_tensor("O_scratch", [128, NCH * S], F32, kind="Internal").ap()
    g_dram = nc.dram_tensor("g_scratch", [64, 128, S], BF16, kind="Internal").ap()
    hT_dram = nc.dram_tensor("hT_scratch", [64, 128, S], BF16, kind="Internal").ap()

    x_fm = p.sbuf("x_fm", [128, NCH * S], F32)
    p.setup_consts()
    with nc.named_scope("loadx"):
        emit_load_x(p, x, x_fm)
    with nc.named_scope("l0attn"):
        emit_l0_attn(p, x_fm, Wq, bq, Wk, bk, Wv, bv, O_dram=O_dram)
    with nc.named_scope("ln0"):
        emit_ln(p, x_fm, "ln0")
    with nc.named_scope("l0ffn"):
        emit_l0_ffn(p, x_fm, w_gate0, w_up0, w_down0, g_dram)
    with nc.named_scope("ln1"):
        emit_ln(p, x_fm, "ln1")
    with nc.named_scope("l1attn"):
        emit_l1_attn(p, x_fm, w_qkv, w_proj, b_proj)
    with nc.named_scope("ln2"):
        emit_ln(p, x_fm, "ln2")
    with nc.named_scope("l1ffn"):
        emit_l1_ffn(p, x_fm, w_gate1, w_up1, w_down1, hT_dram)
    with nc.named_scope("ln3"):
        emit_ln(p, x_fm, "ln3")
    with nc.named_scope("storex"):
        sST = emit_store_x(p, x_fm, y)
    sST.wait(p.sy)
    return p.finish()


def kernel(**inputs):
    g = lambda k: np.ascontiguousarray(inputs[k])
    x = g("L0_attn_x")
    n = 8
    nc = build_fused()
    maps = [{"x": np.ascontiguousarray(x[b]),
             "Wq": g("L0_attn_Wq"), "bq": g("L0_attn_bq"),
             "Wk": g("L0_attn_Wk"), "bk": g("L0_attn_bk"),
             "Wv": g("L0_attn_Wv"), "bv": g("L0_attn_bv"),
             "w_gate0": g("L0_ffn_w_gate"), "w_up0": g("L0_ffn_w_up"),
             "w_down0": g("L0_ffn_w_down"),
             "w_qkv": g("L1_attn_w_qkv"), "w_proj": g("L1_attn_w_proj"),
             "b_proj": g("L1_attn_b_proj"),
             "w_gate1": g("L1_ffn_w_gate"), "w_up1": g("L1_ffn_w_up"),
             "w_down1": g("L1_ffn_w_down")} for b in range(n)]
    r = run_bass_kernel_spmd(nc, maps, core_ids=list(range(n)))
    return np.stack([np.asarray(r.results[b]["y"]) for b in range(n)], axis=0)


def kernel_unfused(**inputs):
    g = lambda k: np.ascontiguousarray(inputs[k])
    x = g("L0_attn_x")
    n = 8
    cores = list(range(n))

    nc1 = build_l0_attn()
    maps1 = [{"x": np.ascontiguousarray(x[b]), "Wq": g("L0_attn_Wq"), "bq": g("L0_attn_bq"),
              "Wk": g("L0_attn_Wk"), "bk": g("L0_attn_bk"),
              "Wv": g("L0_attn_Wv"), "bv": g("L0_attn_bv")} for b in range(n)]
    r1 = run_bass_kernel_spmd(nc1, maps1, core_ids=cores)
    x1 = [r1.results[b]["y"] for b in range(n)]

    nc2 = build_l0_ffn()
    maps2 = [{"x": x1[b], "w_gate": g("L0_ffn_w_gate"), "w_up": g("L0_ffn_w_up"),
              "w_down": g("L0_ffn_w_down")} for b in range(n)]
    r2 = run_bass_kernel_spmd(nc2, maps2, core_ids=cores)
    x2 = [r2.results[b]["y"] for b in range(n)]

    nc3 = build_l1_attn()
    maps3 = [{"x": x2[b], "w_qkv": g("L1_attn_w_qkv"), "w_proj": g("L1_attn_w_proj"),
              "b_proj": g("L1_attn_b_proj")} for b in range(n)]
    r3 = run_bass_kernel_spmd(nc3, maps3, core_ids=cores)
    x3 = [r3.results[b]["y"] for b in range(n)]

    nc4 = build_l1_ffn()
    maps4 = [{"x": x3[b], "w_gate": g("L1_ffn_w_gate"), "w_up": g("L1_ffn_w_up"),
              "w_down": g("L1_ffn_w_down")} for b in range(n)]
    r4 = run_bass_kernel_spmd(nc4, maps4, core_ids=cores)
    return np.stack([np.asarray(r4.results[b]["y"]) for b in range(n)], axis=0)


# version 09
# speedup vs baseline: 1.4580x; 1.0035x over previous
import numpy as np
from contextlib import ExitStack

import concourse.bass as bass
import concourse.mybir as mybir
from concourse.bass_utils import run_bass_kernel_spmd

F32 = mybir.dt.float32
BF16 = mybir.dt.bfloat16
AF = mybir.ActivationFunctionType
ALU = mybir.AluOpType
AX = mybir.AxisListType

S = 2048
E = 2048
NCH = 16
HID = 8192
QSCALE = float(1.0 / np.sqrt(128.0))
LN_EPS = 1e-5


class Sem:
    __slots__ = ("h", "n")

    def __init__(self, nc, name=None):
        self.h = nc.alloc_semaphore(name=name)
        self.n = 0

    def inc(self, instr, k=1):
        instr.then_inc(self.h, k)
        self.n += k
        return self.n

    def wait(self, eng, tok=None):
        tok = self.n if tok is None else tok
        if tok > 0:
            eng.wait_ge(self.h, tok)


class Prog:
    def __init__(self):
        nc = bass.Bass("TRN2", target_bir_lowering=False)
        self.nc = nc
        self.t = nc.tensor
        self.v = nc.vector
        self.sc = nc.scalar
        self.g = nc.gpsimd
        self.sy = nc.sync
        self.ctx = ExitStack()
        self.sems = {}
        self.PS = self.sbuf_like = None
        self._arena = self.ctx.enter_context(nc.psum_tensor("arena", [128, 8 * 512], F32))

    def bank(self, i, cols=512):
        return self._arena[:, i * 512: i * 512 + cols]

    def bankb(self, i, cols=1024):
        return self._arena[:, i * 512:(i + 1) * 512].bitcast(BF16)[:, 0:cols]

    def bankrow(self, i, cols=512):
        return self._arena[0:1, i * 512: i * 512 + cols]

    def sem(self, name):
        if name not in self.sems:
            self.sems[name] = Sem(self.nc, name)
        return self.sems[name]

    def sbuf(self, name, shape, dtype, stack=None):
        return (stack or self.ctx).enter_context(self.nc.sbuf_tensor(name, shape, dtype))

    def barrier(self):
        self.nc.all_engine_barrier()

    def setup_consts(self):
        p = self
        self.identf = p.sbuf("identf", [128, 128], F32)
        self.identb = p.sbuf("identb", [128, 128], BF16)
        self.mask01 = p.sbuf("mask01", [128, 128], BF16)
        self.ones_col_b = p.sbuf("ones_col_b", [128, 1], BF16)
        self.ones_row_f = p.sbuf("ones_row_f", [1, 128], F32)
        sC = p.sem("sConst")
        g = p.g
        g.memset(self.identf[:], 0.0)
        g.affine_select(out=self.identf[:], in_=self.identf[:],
                        compare_op=ALU.not_equal, fill=1.0, base=0,
                        pattern=[[-1, 128]], channel_multiplier=1)
        g.memset(self.identb[:], 0.0)
        g.affine_select(out=self.identb[:], in_=self.identb[:],
                        compare_op=ALU.not_equal, fill=1.0, base=0,
                        pattern=[[-1, 128]], channel_multiplier=1)
        g.memset(self.mask01[:], 1.0)
        g.affine_select(out=self.mask01[:], in_=self.mask01[:],
                        compare_op=ALU.is_ge, fill=0.0, base=0,
                        pattern=[[-1, 128]], channel_multiplier=1)
        g.memset(self.ones_col_b[:], 1.0)
        sC.inc(g.memset(self.ones_row_f[:], 1.0))
        for eng in (p.t, p.v, p.sc):
            sC.wait(eng)

    def finish(self):
        self.ctx.close()
        return self.nc



def emit_load_x(p, x_dram, x_fm):
    st_ = ExitStack()
    stage = p.sbuf("ld_stage", [128, E], F32, stack=st_)
    sDMA = p.sem("sLdX")
    sPE = p.sem("sLdPE")
    sDVE = p.sem("sLdDVE")
    ctok = [0, 0]
    for tc in range(S // 128):
        tok = sDMA.inc(p.sy.dma_start(out=stage[:], in_=x_dram[tc * 128:(tc + 1) * 128, :]), 16)
        sDMA.wait(p.t, tok)
        for c in range(NCH):
            par = c % 2
            sDVE.wait(p.t, ctok[par])
            tp = sPE.inc(p.t.transpose(p.bank(par, 128), stage[:, c * 128:(c + 1) * 128], p.identf[:]))
            sPE.wait(p.v, tp)
            ctok[par] = sDVE.inc(p.v.tensor_copy(
                x_fm[:, c * S + tc * 128: c * S + tc * 128 + 128], p.bank(par, 128)))
        sPE.wait(p.sy)
    p.barrier()
    st_.close()


def emit_store_x(p, x_fm, y_dram):
    stage = p.sbuf("st_stage", [128, E], F32)
    sPE = p.sem("sStPE")
    sDVE = p.sem("sStDVE")
    sST = p.sem("sStDMA")
    ctok = [0, 0]
    for tc in range(S // 128):
        sST.wait(p.v)
        for c in range(NCH):
            par = c % 2
            sDVE.wait(p.t, ctok[par])
            tp = sPE.inc(p.t.transpose(p.bank(par, 128),
                                       x_fm[:, c * S + tc * 128: c * S + tc * 128 + 128],
                                       p.identf[:]))
            sPE.wait(p.v, tp)
            ctok[par] = sDVE.inc(p.v.tensor_copy(stage[:, c * 128:(c + 1) * 128], p.bank(par, 128)))
        sDVE.wait(p.sy)
        sST.inc(p.sy.dma_start(out=y_dram[tc * 128:(tc + 1) * 128, :], in_=stage[:]), 16)
    return sST


def emit_bias_cols(p, dram_2d, out_tile, rows, scale=None):
    st_ = ExitStack()
    raw = p.sbuf(f"braw_{out_tile.name}", [128, 128], F32, stack=st_)
    sB = p.sem("sBias")
    tokz = sB.inc(p.g.memset(raw[:], 0.0))
    sB.wait(p.sy, tokz)
    tok = sB.inc(p.sy.dma_start(out=raw[0:rows, :], in_=dram_2d), 16)
    sB.wait(p.t, tok)
    tp = sB.inc(p.t.transpose(p.bank(7, 128), raw[:], p.identf[:]))
    sB.wait(p.v, tp)
    if scale is None:
        sB.inc(p.v.tensor_copy(out_tile[:], p.bank(7, 128)[:, 0:rows]))
    else:
        sB.inc(p.v.tensor_scalar_mul(out_tile[:], p.bank(7, 128)[:, 0:rows], scale))
    sB.wait(p.t)
    sB.wait(p.v)
    sB.wait(p.g)
    sB.wait(p.sy)
    st_.close()


def emit_ln(p, x_fm, tag):
    st_ = ExitStack()
    rbf = [p.sbuf(f"{tag}_rbf{i}", [128, 512], BF16, stack=st_) for i in range(2)]
    rsq = [p.sbuf(f"{tag}_rsq{i}", [128, 512], BF16, stack=st_) for i in range(2)]
    rowA = p.sbuf(f"{tag}_rowA", [1, S], F32, stack=st_)
    rowB = p.sbuf(f"{tag}_rowB", [1, S], F32, stack=st_)
    rowC = p.sbuf(f"{tag}_rowC", [1, S], F32, stack=st_)
    sA = p.sem(f"{tag}A")
    sP = p.sem(f"{tag}P")
    sV = p.sem(f"{tag}V")
    ps0 = p.bankrow(0)
    ps1 = p.bankrow(1)
    casttok = [0, 0]
    for tt in range(4):
        for c in range(NCH):
            par = c % 2
            sP.wait(p.sc, casttok[par])
            p.sc.activation(out=rbf[par][:], in_=x_fm[:, c * S + tt * 512: c * S + tt * 512 + 512], func=AF.Copy)
            ta = sA.inc(p.sc.activation(out=rsq[par][:], in_=x_fm[:, c * S + tt * 512: c * S + tt * 512 + 512], func=AF.Square))
            sA.wait(p.t, ta)
            p.t.matmul(ps0, p.ones_col_b[:], rbf[par][:], start=(c == 0), stop=(c == NCH - 1))
            casttok[par] = sP.inc(p.t.matmul(ps1, p.ones_col_b[:], rsq[par][:], start=(c == 0), stop=(c == NCH - 1)))
        sP.wait(p.v)
        p.v.tensor_copy(rowA[0:1, tt * 512: tt * 512 + 512], ps0)
        tv = sV.inc(p.v.tensor_copy(rowB[0:1, tt * 512: tt * 512 + 512], ps1))
        sV.wait(p.t, tv)
    def step(instr):
        tok = sV.inc(instr)
        sV.wait(p.v, tok)
        return tok
    step(p.v.tensor_scalar_mul(rowA[:], rowA[:], 1.0 / E))
    step(p.v.tensor_mul(rowC[:], rowA[:], rowA[:]))
    tokA = sV.inc(p.v.tensor_scalar_mul(rowA[:], rowA[:], -1.0))
    sV.wait(p.v, tokA)
    tokB = sV.inc(p.v.scalar_tensor_tensor(rowB[:], rowB[:], 1.0 / E, rowC[:], ALU.mult, ALU.subtract))
    sV.wait(p.v, tokB)
    tokB = sV.inc(p.v.tensor_single_scalar(rowB[:], rowB[:], LN_EPS, ALU.add))
    sV.wait(p.sc, tokB)
    ta = sA.inc(p.sc.activation(out=rowB[:], in_=rowB[:], func=AF.Sqrt))
    sA.wait(p.v, ta)
    step(p.v.reciprocal(rowB[:], rowB[:]))
    sV.wait(p.t)
    normtok = [0, 0]
    for tt in range(4):
        bp = (tt % 2) * 2
        sV.wait(p.t, normtok[tt % 2])
        p.t.matmul(p.bank(bp), p.ones_row_f[:], rowA[0:1, tt * 512: tt * 512 + 512], start=True, stop=True)
        tp = sP.inc(p.t.matmul(p.bank(bp + 1), p.ones_row_f[:], rowB[0:1, tt * 512: tt * 512 + 512], start=True, stop=True))
        sP.wait(p.v, tp)
        for c in range(NCH):
            sl = x_fm[:, c * S + tt * 512: c * S + tt * 512 + 512]
            p.v.tensor_tensor(sl, sl, p.bank(bp), ALU.add)
            tv = sV.inc(p.v.tensor_tensor(sl, sl, p.bank(bp + 1), ALU.mult))
        normtok[tt % 2] = tv
    sV.wait(p.t)
    sV.wait(p.sc)
    sV.wait(p.g)
    p.barrier()
    st_.close()



def emit_l0_attn(p, x_fm, Wq, bq, Wk, bk, Wv, bv, O_dram=None, parts="AB", nheads=16, dbg=None):
    stb = ExitStack()
    Kt = p.sbuf("Kt", [128, 4 * S], BF16, stack=stb)
    Vt = p.sbuf("Vt", [128, 4 * S], BF16, stack=stb)
    xb = [p.sbuf(f"at_xb{i}", [128, 512], BF16, stack=stb) for i in range(2)]
    expb = p.sbuf("expb", [128, S], BF16, stack=stb)
    expT = [p.sbuf(f"expT{i}", [128, 512], BF16, stack=stb) for i in range(2)]
    pmax = p.sbuf("pmax", [128, 4], F32, stack=stb)
    negmax = p.sbuf("negmax", [128, 1], F32, stack=stb)
    zcol = p.sbuf("zcol", [128, 1], F32, stack=stb)
    rz = p.sbuf("rz", [128, 1], F32, stack=stb)
    zrow = p.sbuf("zrow", [1, 128], F32, stack=stb)
    rzbc = p.sbuf("rzbc", [128, 128], BF16, stack=stb)
    bqs = p.sbuf("bqs", [128, 16], F32, stack=stb)
    bks = p.sbuf("bks", [128, 4], F32, stack=stb)
    bvs = p.sbuf("bvs", [128, 4], F32, stack=stb)

    sW = p.sem("atW")
    sA = p.sem("atA")
    sP = p.sem("atP")
    sV = p.sem("atV")

    emit_bias_cols(p, bq, bqs, 16, scale=QSCALE)
    emit_bias_cols(p, bk, bks, 4)
    emit_bias_cols(p, bv, bvs, 4)
    p.barrier()

    with ExitStack() as st_:
        wkv = [p.sbuf(f"wkv{i}", [128, E], BF16, stack=st_) for i in range(4)]
        Vf = p.sbuf("Vf", [128, 2 * S], BF16, stack=st_)
        wkv_lastuse = 0
        for sweep in range(2):
            g0 = sweep * 2
            sP.wait(p.g, wkv_lastuse)
            for i, (W, g) in enumerate(((Wk, g0), (Wk, g0 + 1), (Wv, g0), (Wv, g0 + 1))):
                sW.inc(p.g.dma_start(out=wkv[i][:].rearrange("p (c d) -> p c d", c=NCH),
                                     in_=W[g].rearrange("(c p) d -> p c d", p=128)), 16)
            sW.wait(p.t)
            casttok = [0, 0]
            for tt in range(4):
                for c in range(NCH):
                    par = c % 2
                    sP.wait(p.sc, casttok[par])
                    ta = sA.inc(p.sc.activation(out=xb[par][:],
                                                in_=x_fm[:, c * S + tt * 512: c * S + tt * 512 + 512],
                                                func=AF.Copy))
                    sA.wait(p.t, ta)
                    for o in range(4):
                        mm = p.t.matmul(p.bank(o), wkv[o][:, c * 128:(c + 1) * 128], xb[par][:],
                                        start=(c == 0), stop=(c == NCH - 1))
                    casttok[par] = sP.inc(mm)
                wkv_lastuse = sP.n
                sP.wait(p.sc)
                for o in range(4):
                    g = g0 + (o % 2)
                    if o < 2:
                        dst = Kt[:, g * S + tt * 512: g * S + tt * 512 + 512]
                        bias = bks[:, g:g + 1]
                    else:
                        dst = Vf[:, (o - 2) * S + tt * 512: (o - 2) * S + tt * 512 + 512]
                        bias = bvs[:, g:g + 1]
                    ta = sA.inc(p.sc.activation(out=dst, in_=p.bank(o), func=AF.Identity, bias=bias))
                sA.wait(p.t, ta)
            ctokT = [0, 0]
            for gg in range(2):
                g = g0 + gg
                for kt in range(NCH):
                    par = kt % 2
                    sV.wait(p.t, ctokT[par])
                    tp = sP.inc(p.t.transpose(p.bankb(4 + par)[:, 0:128],
                                              Vf[:, gg * S + kt * 128: gg * S + kt * 128 + 128],
                                              p.identb[:]))
                    sP.wait(p.v, tp)
                    ctokT[par] = sV.inc(p.v.tensor_copy(
                        Vt[:, g * S + kt * 128: g * S + kt * 128 + 128], p.bankb(4 + par)[:, 0:128]))
            sV.wait(p.t)
    p.barrier()

    Qh = p.sbuf("Qh", [128, 2 * S], BF16, stack=stb)
    wq = [p.sbuf(f"wq{i}", [128, E], BF16, stack=stb) for i in range(2)]
    ostg = [p.sbuf(f"ostg{i}", [128, 128], F32, stack=stb) for i in range(2)]
    sO = p.sem("atO")
    otok = [0, 0]
    onum = 0
    rzt_tok = [0]
    if "B" not in parts:
        p.barrier()
        stb.close()
        return
    wq_lastuse = 0
    for hp in range(max(0, nheads) // 2):
        heads = (2 * hp, 2 * hp + 1)
        sV.wait(p.t)
        sP.wait(p.g, wq_lastuse)
        for i, h in enumerate(heads):
            sW.inc(p.g.dma_start(out=wq[i][:].rearrange("p (c d) -> p c d", c=NCH),
                                 in_=Wq[h].rearrange("(c p) d -> p c d", p=128)), 16)
        sW.wait(p.t)
        casttok = [0, 0]
        for tt in range(4):
            for c in range(NCH):
                par = c % 2
                sP.wait(p.sc, casttok[par])
                ta = sA.inc(p.sc.activation(out=xb[par][:],
                                            in_=x_fm[:, c * S + tt * 512: c * S + tt * 512 + 512],
                                            func=AF.Copy))
                sA.wait(p.t, ta)
                for i in range(2):
                    mm = p.t.matmul(p.bank(6 + i), wq[i][:, c * 128:(c + 1) * 128], xb[par][:],
                                    start=(c == 0), stop=(c == NCH - 1))
                casttok[par] = sP.inc(mm)
            wq_lastuse = sP.n
            sP.wait(p.sc)
            for i, h in enumerate(heads):
                ta = sA.inc(p.sc.activation(out=Qh[:, i * S + tt * 512: i * S + tt * 512 + 512],
                                            in_=p.bank(6 + i), func=AF.Identity,
                                            scale=QSCALE, bias=bqs[:, h:h + 1]))
            sA.wait(p.t, ta)
        for i, h in enumerate(heads):
            g = h // 4
            exptok = 0
            evactok = 0
            zread_tok = [0]
            for qi in range(16):
                nk = qi + 1
                ns = (nk + 3) // 4
                sA.wait(p.t)
                sV.wait(p.t, max(evactok, zread_tok[0]))
                sctok = []
                for s in range(ns):
                    cols = min(512, nk * 128 - s * 512)
                    mm = p.t.matmul(p.bank(s, cols), Qh[:, i * S + qi * 128: i * S + qi * 128 + 128],
                                    Kt[:, g * S + s * 512: g * S + s * 512 + cols],
                                    start=True, stop=True)
                    sctok.append(sP.inc(mm))
                for s in range(ns):
                    cols = min(512, nk * 128 - s * 512)
                    sP.wait(p.v, sctok[s])
                    mr = p.v.reduce_max(pmax[:, s:s + 1], p.bank(s, cols), axis=AX.X)
                tv = sV.inc(mr)
                sV.wait(p.v, tv)
                tv = sV.inc(p.v.reduce_max(negmax[:], pmax[:, 0:ns], axis=AX.X, negate=True))
                sV.wait(p.sc, tv)
                sP.wait(p.sc, exptok)
                for s in range(ns):
                    cols = min(512, nk * 128 - s * 512)
                    ta = sA.inc(p.sc.activation(out=expb[:, s * 512: s * 512 + cols],
                                                in_=p.bank(s, cols), func=AF.Exp,
                                                bias=negmax[:, 0:1]))
                sA.wait(p.v, ta)
                dsl = expb[:, qi * 128: qi * 128 + 128]
                tv = sV.inc(p.v.tensor_tensor(dsl, dsl, p.mask01[:], ALU.mult))
                sV.wait(p.v, tv)
                tv = sV.inc(p.v.reduce_sum(zcol[:], expb[:, 0:nk * 128], axis=AX.X))
                sV.wait(p.v, tv)
                sP.wait(p.v, rzt_tok[0])
                rztok = sV.inc(p.v.reciprocal(rz[:], zcol[:]))
                ngrp = ns
                Ttok = []
                Mtok = []
                mults_tok = []

                def emit_T(grp):
                    gpar = grp % 2
                    base = grp * 4
                    cnt = min(4, nk - base)
                    if grp >= 2:
                        sV.wait(p.t, mults_tok[grp - 2])
                    for j in range(cnt):
                        tpj = p.t.transpose(p.bankb(4 + gpar)[:, j * 128:(j + 1) * 128],
                                            expb[:, (base + j) * 128:(base + j + 1) * 128],
                                            p.identb[:])
                    Ttok.append(sP.inc(tpj))

                sV.wait(p.t, tv)
                emit_T(0)
                if ngrp > 1:
                    emit_T(1)
                sV.wait(p.t, rztok)
                tp = sP.inc(p.t.transpose(p.bankrow(0, 128), rz[:], p.identf[:]))
                rzt_tok[0] = tp
                sP.wait(p.v, tp)
                tv2 = sV.inc(p.v.tensor_copy(zrow[:], p.bankrow(0, 128)))
                sV.wait(p.t, tv2)
                tp = sP.inc(p.t.matmul(p.bank(1, 128), p.ones_row_f[:], zrow[:], start=True, stop=True))
                sP.wait(p.v, tp)
                tv2 = sV.inc(p.v.tensor_copy(rzbc[:], p.bank(1, 128)))
                zread_tok[0] = tv2
                for grp in range(ngrp):
                    gpar = grp % 2
                    base = grp * 4
                    cnt = min(4, nk - base)
                    sP.wait(p.v, Ttok[grp])
                    if grp == 0:
                        sV.wait(p.v, zread_tok[0])
                    for j in range(cnt):
                        mv = p.v.tensor_tensor(expT[gpar][:, j * 128:(j + 1) * 128],
                                               p.bankb(4 + gpar)[:, j * 128:(j + 1) * 128],
                                               rzbc[:], ALU.mult)
                    mults_tok.append(sV.inc(mv))
                    sV.wait(p.t, mults_tok[grp])
                    for j in range(cnt):
                        kt = base + j
                        mm = p.t.matmul(p.bank(6, 128), Vt[:, g * S + kt * 128: g * S + kt * 128 + 128],
                                        expT[gpar][:, j * 128:(j + 1) * 128],
                                        start=(kt == 0), stop=(kt == nk - 1))
                    Mtok.append(sP.inc(mm))
                    if grp + 2 < ngrp:
                        emit_T(grp + 2)
                exptok = sP.n
                if dbg is not None and "e0" in dbg and h == dbg["_h"] and qi == dbg["_qi"]:
                    p.v.tensor_copy(dbg["_eb"][:], expb[:])
                    p.v.tensor_copy(dbg["_zb"][:], zcol[:])
                    sV.inc(p.v.tensor_copy(dbg["_ob"][:], p.bank(6, 128)))
                par = onum % 2
                sP.wait(p.v, Mtok[-1])
                sO.wait(p.v, otok[par])
                evactok = sV.inc(p.v.tensor_copy(ostg[par][:], p.bank(6, 128)))
                sV.wait(p.sy, evactok)
                otok[par] = sO.inc(p.sy.dma_start(
                    out=O_dram[:, h * S + qi * 128: h * S + qi * 128 + 128], in_=ostg[par][:]), 16)
                onum += 1
    obig = [p.sbuf(f"obig{i}", [128, E], F32, stack=stb) for i in range(2)]
    sO.wait(p.sy)
    btok = [0, 0]
    for c in range(NCH):
        par = c % 2
        sV.wait(p.sy, btok[par])
        tokl = sO.inc(p.sy.dma_start(out=obig[par][:], in_=O_dram[:, c * S:(c + 1) * S]), 16)
        sO.wait(p.v, tokl)
        xsl = x_fm[:, c * S:(c + 1) * S]
        btok[par] = sV.inc(p.v.tensor_tensor(xsl, xsl, obig[par][:], ALU.add))
    sV.wait(p.t)
    p.barrier()
    if dbg is not None and "e0" in dbg:
        sD0 = p.sem("sDbg0")
        sV.wait(p.sy)
        sD0.inc(p.sy.dma_start(out=dbg["e0"], in_=dbg["_eb"][:]), 16)
        sD0.inc(p.sy.dma_start(out=dbg["z0"], in_=dbg["_zb"][:]), 16)
        sD0.inc(p.sy.dma_start(out=dbg["r0"], in_=dbg["_rb"][:]), 16)
        sD0.inc(p.sy.dma_start(out=dbg["o0"], in_=dbg["_ob"][:]), 16)
        if "s0" in dbg:
            sD0.inc(p.sy.dma_start(out=dbg["s0"], in_=dbg["_sb"][:]), 16)
            sD0.inc(p.sy.dma_start(out=dbg["q0"], in_=dbg["_qb"][:]), 16)
        sD0.wait(p.sy)
    if dbg is not None:
        sD = p.sem("sDbg")
        sD.inc(p.sy.dma_start(out=dbg["q"], in_=Qh[:]), 16)
        sD.inc(p.sy.dma_start(out=dbg["k"], in_=Kt[:]), 16)
        sD.inc(p.sy.dma_start(out=dbg["v"], in_=Vt[:]), 16)
        sD.inc(p.sy.dma_start(out=dbg["e"], in_=expb[:]), 16)
        sD.inc(p.sy.dma_start(out=dbg["z"], in_=zcol[:]), 16)
        sD.inc(p.sy.dma_start(out=dbg["rzbc"], in_=rzbc[:]), 16)
        sD.wait(p.sy)
        p.barrier()
    stb.close()



class WStg:

    def __init__(self, p, name, stack, cols=E):
        self.bufs = [p.sbuf(f"{name}_{i}", [128, cols], BF16, stack=stack) for i in range(2)]
        self.tok = [0, 0]
        self.i = 0


def emit_wT_prep(p, dram_rows_ap, wT, sW, sP, sV, stg, nch=NCH, out_off=0, out_stride=128):
    par_s = stg.i % 2
    buf = stg.bufs[par_s]
    sP.wait(p.g, stg.tok[par_s])
    tokw = sW.inc(p.g.dma_start(out=buf[:, 0:nch * 128].rearrange("p (c d) -> p c d", c=nch),
                                in_=dram_rows_ap), 16)
    sW.wait(p.t, tokw)
    for c0 in range(0, nch, 4):
        par = (c0 // 4) % 2
        cnt = min(4, nch - c0)
        sV.wait(p.t, max(0, sV.n - 1))
        for j in range(cnt):
            tp = p.t.transpose(p.bankb(4 + par)[:, j * 128:(j + 1) * 128],
                               buf[:, (c0 + j) * 128:(c0 + j + 1) * 128], p.identb[:])
        tptok = sP.inc(tp)
        sP.wait(p.v, tptok)
        if out_stride == 128:
            o = out_off + c0 * 128
            sV.inc(p.v.tensor_copy(wT[:, o: o + cnt * 128], p.bankb(4 + par)[:, 0:cnt * 128]))
        else:
            for j in range(cnt):
                o = out_off + (c0 + j) * out_stride
                sV.inc(p.v.tensor_copy(wT[:, o: o + 128], p.bankb(4 + par)[:, j * 128:(j + 1) * 128]))
    stg.tok[par_s] = sP.n
    stg.i += 1
    return sV.n


def emit_l0_ffn(p, x_fm, w_gate, w_up, w_down, g_dram, parts="AB"):
    sW = p.sem("f0W")
    sA = p.sem("f0A")
    sP = p.sem("f0P")
    sV = p.sem("f0V")
    sG = p.sem("f0G")

    with ExitStack() as st_:
        xbh = p.sbuf("f0_xbh", [128, NCH * 1024], BF16, stack=st_)
        wgT = [p.sbuf(f"f0_wgT{i}", [128, NCH * 128], BF16, stack=st_) for i in range(4)]
        wuT = [p.sbuf(f"f0_wuT{i}", [128, NCH * 128], BF16, stack=st_) for i in range(4)]
        stgA = WStg(p, "f0_stgA", st_)
        sil = [p.sbuf(f"f0_sil{i}", [128, 512], F32, stack=st_) for i in range(2)]
        gt = [p.sbuf(f"f0_gt{i}", [128, 512], BF16, stack=st_) for i in range(2)]
        siltok = [0, 0]
        gtok = [0, 0]
        en = 0
        for half in range(2):
            t0 = half * 1024
            sP.wait(p.sc)
            ta = 0
            for c in range(NCH):
                ta = sA.inc(p.sc.activation(out=xbh[:, c * 1024:(c + 1) * 1024],
                                            in_=x_fm[:, c * S + t0: c * S + t0 + 1024],
                                            func=AF.Copy))
            sA.wait(p.t, ta)
            for Hq in range(16):
                sA.wait(p.t)
                sV.wait(p.t)
                for i in range(4):
                    h = Hq * 4 + i
                    emit_wT_prep(p, w_gate[h * 128:(h + 1) * 128, :].rearrange("r (c pp) -> r c pp", pp=128),
                                 wgT[i], sW, sP, sV, stgA)
                    emit_wT_prep(p, w_up[h * 128:(h + 1) * 128, :].rearrange("r (c pp) -> r c pp", pp=128),
                                 wuT[i], sW, sP, sV, stgA)
                sV.wait(p.t)
                for tsl in range(2):
                    tb = tsl * 512
                    for c in range(NCH):
                        for i in range(4):
                            p.t.matmul(p.bank(i), wgT[i][:, c * 128:(c + 1) * 128],
                                       xbh[:, c * 1024 + tb: c * 1024 + tb + 512],
                                       start=(c == 0), stop=(c == NCH - 1))
                            mm = p.t.matmul(p.bank(4 + i), wuT[i][:, c * 128:(c + 1) * 128],
                                            xbh[:, c * 1024 + tb: c * 1024 + tb + 512],
                                            start=(c == 0), stop=(c == NCH - 1))
                    tp = sP.inc(mm)
                    sP.wait(p.sc, tp)
                    for i in range(4):
                        h = Hq * 4 + i
                        pa = en % 2
                        en += 1
                        sV.wait(p.sc, siltok[pa])
                        ta = sA.inc(p.sc.activation(out=sil[pa][:], in_=p.bank(i), func=AF.Silu))
                        sA.wait(p.v, ta)
                        sG.wait(p.v, gtok[pa])
                        tv = sV.inc(p.v.tensor_tensor(gt[pa][:], p.bank(4 + i), sil[pa][:], ALU.mult))
                        siltok[pa] = tv
                        sV.wait(p.g, tv)
                        gtok[pa] = sG.inc(p.g.dma_start(out=g_dram[h, :, t0 + tb: t0 + tb + 512],
                                                        in_=gt[pa][:]), 16)
                    sA.wait(p.t)
                    sV.wait(p.t)
    p.barrier()
    if "B" not in parts:
        return

    with ExitStack() as st_:
        wdT = [p.sbuf(f"f0_wdT{i}", [128, NCH * 128], BF16, stack=st_) for i in range(8)]
        gh = [p.sbuf(f"f0_gh{i}", [128, S], BF16, stack=st_) for i in range(8)]
        stgB = WStg(p, "f0_stgB", st_)
        slot_tok = [0] * 8
        half_drain = [0, 0]
        xv = x_fm[:].rearrange("p (ch t) -> p ch t", ch=NCH)
        for hg in range(16):
            hf = hg % 2
            slots = [hf * 4 + i for i in range(4)]
            sV.wait(p.t)
            for k, sl in enumerate(slots):
                h = hg * 4 + k
                sP.wait(p.g, slot_tok[sl])
                sG.inc(p.g.dma_start(out=gh[sl][:], in_=g_dram[h, :, :]), 16)
                emit_wT_prep(p, w_down[:, h * 128:(h + 1) * 128].rearrange("(c pp) h -> pp c h", pp=128),
                             wdT[sl], sW, sP, sV, stgB)
            sV.wait(p.t)
            sG.wait(p.t)
            for win in range(16):
                dq, tsl = win // 4, win % 4
                tb = tsl * 512
                bh = (win % 2) * 4
                sV.wait(p.t, half_drain[win % 2])
                for k, sl in enumerate(slots):
                    for dd in range(4):
                        D = dq * 4 + dd
                        mm = p.t.matmul(p.bank(bh + dd), wdT[sl][:, D * 128:(D + 1) * 128],
                                        gh[sl][:, tb: tb + 512],
                                        start=(k == 0), stop=(k == 3))
                tok_mm = sP.inc(mm)
                sP.wait(p.v, tok_mm)
                half_drain[win % 2] = sV.inc(p.v.tensor_tensor(
                    xv[:, dq * 4:(dq + 1) * 4, tb: tb + 512],
                    xv[:, dq * 4:(dq + 1) * 4, tb: tb + 512],
                    p._arena[:, bh * 512:(bh + 4) * 512].rearrange("p (b t) -> p b t", b=4),
                    ALU.add))
            for sl in slots:
                slot_tok[sl] = sP.n
    p.barrier()


def build_l0_ffn(parts="ABLN"):
    import os
    p = Prog()
    nc = p.nc
    x = nc.dram_tensor("x", [S, E], F32, kind="ExternalInput").ap()
    w_gate = nc.dram_tensor("w_gate", [HID, E], F32, kind="ExternalInput").ap()
    w_up = nc.dram_tensor("w_up", [HID, E], F32, kind="ExternalInput").ap()
    w_down = nc.dram_tensor("w_down", [E, HID], F32, kind="ExternalInput").ap()
    y = nc.dram_tensor("y", [S, E], F32, kind="ExternalOutput").ap()
    gkind = "ExternalOutput" if os.environ.get("DBG") else "Internal"
    g_dram = nc.dram_tensor("g_scratch", [64, 128, S], BF16, kind=gkind).ap()

    x_fm = p.sbuf("x_fm", [128, NCH * S], F32)
    p.setup_consts()
    emit_load_x(p, x, x_fm)
    if "A" in parts:
        emit_l0_ffn(p, x_fm, w_gate, w_up, w_down, g_dram, parts=parts)
    if "LN" in parts:
        emit_ln(p, x_fm, "ln1")
    sST = emit_store_x(p, x_fm, y)
    sST.wait(p.sy)
    return p.finish()



def emit_l1_attn(p, x_fm, w_qkv, w_proj, b_proj, parts="12"):
    sW = p.sem("a1W")
    sA = p.sem("a1A")
    sP = p.sem("a1P")
    sV = p.sem("a1V")

    blockdiag = p.sbuf("a1_bd", [128, 16 * 128], BF16)
    ksums = p.sbuf("a1_ks", [128, 16 * 4], BF16)
    sel4 = p.sbuf("a1_sel4", [4, 128], F32)
    bps = p.sbuf("a1_bps", [128, 16], F32)
    sC = p.sem("a1C")
    g = p.g
    g.memset(blockdiag[:], 0.0)
    g.memset(ksums[:], 0.0)
    g.memset(sel4[:], 1.0)
    g.affine_select(out=sel4[:], in_=sel4[:], compare_op=ALU.is_ge, fill=0.0,
                    base=0, pattern=[[1, 128]], channel_multiplier=-32)
    sC.inc(g.affine_select(out=sel4[:], in_=sel4[:], compare_op=ALU.is_ge, fill=0.0,
                           base=31, pattern=[[-1, 128]], channel_multiplier=32))
    for eng in (p.t, p.v, p.sc):
        sC.wait(eng)
    emit_bias_cols(p, b_proj.rearrange("(c pp) -> c pp", pp=128), bps, 16)
    p.barrier()

    if "1" in parts:
      with ExitStack() as st_:
        wkT = p.sbuf("a1_wkT", [128, NCH * 512], BF16, stack=st_)
        wvT = p.sbuf("a1_wvT", [128, NCH * 512], BF16, stack=st_)
        wstg = WStg(p, "a1_stg1", st_)
        xb = [p.sbuf(f"a1_xb{i}", [128, NCH * 128], BF16, stack=st_) for i in range(2)]
        k_tm = p.sbuf("a1_ktm", [128, 512], BF16, stack=st_)
        vext = p.sbuf("a1_vext", [128, 4 * 132], BF16, stack=st_)
        sC2 = p.sem("a1C2")
        sC2.inc(p.g.memset(vext[:], 1.0))
        sC2.wait(p.sc)
        xbtok1 = [0, 0]

        def casts1(t):
            pt = t % 2
            tb = (t % 16) * 128
            sP.wait(p.sc, xbtok1[pt])
            ta = 0
            for c in range(NCH):
                ta = sA.inc(p.sc.activation(out=xb[pt][:, c * 128:(c + 1) * 128],
                                            in_=x_fm[:, c * S + tb: c * S + tb + 128],
                                            func=AF.Copy))
            return ta

        ct1 = casts1(0)
        tg1 = 0
        for si in range(4):
            for rr in range(4):
                krow = 2048 + si * 512 + rr * 128
                vrow = 4096 + si * 512 + rr * 128
                emit_wT_prep(p, w_qkv[krow:krow + 128, :].rearrange("r (c pp) -> r c pp", pp=128),
                             wkT, sW, sP, sV, wstg,
                             out_off=rr * 128, out_stride=512)
                emit_wT_prep(p, w_qkv[vrow:vrow + 128, :].rearrange("r (c pp) -> r c pp", pp=128),
                             wvT, sW, sP, sV, wstg,
                             out_off=rr * 128, out_stride=512)
            sV.wait(p.t)
            for t128 in range(16):
                tb = t128 * 128
                t = tg1
                pt = t % 2
                sA.wait(p.t, ct1)
                for c in range(NCH):
                    p.t.matmul(p.bank(0), xb[pt][:, c * 128:(c + 1) * 128], wkT[:, c * 512:(c + 1) * 512],
                               start=(c == 0), stop=(c == NCH - 1))
                    mm = p.t.matmul(p.bank(1), xb[pt][:, c * 128:(c + 1) * 128], wvT[:, c * 512:(c + 1) * 512],
                                    start=(c == 0), stop=(c == NCH - 1))
                xbtok1[pt] = sP.n + 1
                if tg1 + 1 < 64:
                    ct1 = casts1(tg1 + 1)
                tg1 += 1
                tp = sP.inc(mm)
                sP.wait(p.sc, tp)
                ta = sA.inc(p.sc.activation(out=k_tm[:], in_=p.bank(0), func=AF.Relu))
                for q in range(4):
                    ta = sA.inc(p.sc.activation(out=vext[:, q * 132: q * 132 + 128],
                                                in_=p.bank(1, 512)[:, q * 128:(q + 1) * 128],
                                                func=AF.Copy))
                sA.wait(p.t, ta)
                for q in range(4):
                    mm = p.t.matmul(p.bank(2 + q, 129), k_tm[:, q * 128:(q + 1) * 128],
                                    vext[:, q * 132: q * 132 + 129],
                                    start=(t128 == 0), stop=(t128 == 15))
                sP.inc(mm)
            sP.wait(p.v)
            for q in range(4):
                Q = si * 4 + q
                for qp in range(4):
                    b0 = 32 * qp
                    p.v.tensor_copy(blockdiag[b0:b0 + 32, Q * 128 + b0: Q * 128 + b0 + 32],
                                    p.bank(2 + q, 129)[b0:b0 + 32, b0:b0 + 32])
                    tv = sV.inc(p.v.tensor_copy(ksums[b0:b0 + 32, Q * 4 + qp: Q * 4 + qp + 1],
                                                p.bank(2 + q, 129)[b0:b0 + 32, 128:129]))
            sV.wait(p.t, tv)
      p.barrier()

    if "2" in parts:
      with ExitStack() as st_:
        xb5 = p.sbuf("a1_xb5", [128, NCH * 512], BF16, stack=st_)
        wT = p.sbuf("a1_wT", [128, NCH * 128], BF16, stack=st_)
        wstg = WStg(p, "a1_stg2", st_)
        q4 = p.sbuf("a1_q4", [128, 512], BF16, stack=st_)
        den_r = p.sbuf("a1_denr", [4, 512], F32, stack=st_)
        denb_s = p.sbuf("a1_denb", [128, 512], F32, stack=st_)
        o_fm = p.sbuf("a1_ofm", [128, NCH * 512], BF16, stack=st_)
        for tt in range(4):
            tb = tt * 512
            sP.wait(p.sc)
            ta = 0
            for c in range(NCH):
                ta = sA.inc(p.sc.activation(out=xb5[:, c * 512:(c + 1) * 512],
                                            in_=x_fm[:, c * S + tb: c * S + tb + 512],
                                            func=AF.Copy))
            for Q in range(16):
                emit_wT_prep(p, w_qkv[Q * 128:(Q + 1) * 128, :].rearrange("r (c pp) -> r c pp", pp=128),
                             wT, sW, sP, sV, wstg)
                sV.wait(p.t)
                sA.wait(p.t, ta)
                for c in range(NCH):
                    mm = p.t.matmul(p.bank(0), wT[:, c * 128:(c + 1) * 128],
                                    xb5[:, c * 512:(c + 1) * 512],
                                    start=(c == 0), stop=(c == NCH - 1))
                tp = sP.inc(mm)
                sP.wait(p.sc, tp)
                tq = sA.inc(p.sc.activation(out=q4[:], in_=p.bank(0), func=AF.Relu))
                sA.wait(p.t, tq)
                p.t.matmul(p.bank(1), blockdiag[:, Q * 128:(Q + 1) * 128], q4[:],
                           start=True, stop=True)
                tp = sP.inc(p.t.matmul(p.bank(2, 512)[0:4, :], ksums[:, Q * 4:(Q + 1) * 4], q4[:],
                                       start=True, stop=True))
                sP.wait(p.v, tp)
                tv = sV.inc(p.v.tensor_single_scalar(den_r[:], p.bank(2, 512)[0:4, :], 1e-15, ALU.add))
                sV.wait(p.v, tv)
                tv = sV.inc(p.v.reciprocal(den_r[:], den_r[:]))
                sV.wait(p.t, tv)
                tp = sP.inc(p.t.matmul(p.bank(3), sel4[:], den_r[:], start=True, stop=True))
                sP.wait(p.v, tp)
                tv = sV.inc(p.v.tensor_copy(denb_s[:], p.bank(3)))
                sV.wait(p.v, tv)
                tv = sV.inc(p.v.tensor_tensor(o_fm[:, Q * 512:(Q + 1) * 512], p.bank(1), denb_s[:], ALU.mult))
                sV.wait(p.t, tv)
            for D in range(NCH):
                emit_wT_prep(p, w_proj[D * 128:(D + 1) * 128, :].rearrange("r (c pp) -> r c pp", pp=128),
                             wT, sW, sP, sV, wstg)
                sV.wait(p.t)
                for c in range(NCH):
                    mm = p.t.matmul(p.bank(0), wT[:, c * 128:(c + 1) * 128],
                                    o_fm[:, c * 512:(c + 1) * 512],
                                    start=(c == 0), stop=(c == NCH - 1))
                tp = sP.inc(mm)
                sP.wait(p.v, tp)
                sl = x_fm[:, D * S + tb: D * S + tb + 512]
                tv = sV.inc(p.v.scalar_tensor_tensor(sl, p.bank(0), bps[:, D:D + 1], sl,
                                                     ALU.add, ALU.add))
                sV.wait(p.t, tv)
      p.barrier()


def build_l1_attn(parts="12LN"):
    p = Prog()
    nc = p.nc
    x = nc.dram_tensor("x", [S, E], F32, kind="ExternalInput").ap()
    w_qkv = nc.dram_tensor("w_qkv", [3 * E, E], F32, kind="ExternalInput").ap()
    w_proj = nc.dram_tensor("w_proj", [E, E], F32, kind="ExternalInput").ap()
    b_proj = nc.dram_tensor("b_proj", [E], F32, kind="ExternalInput").ap()
    y = nc.dram_tensor("y", [S, E], F32, kind="ExternalOutput").ap()

    x_fm = p.sbuf("x_fm", [128, NCH * S], F32)
    p.setup_consts()
    emit_load_x(p, x, x_fm)
    emit_l1_attn(p, x_fm, w_qkv, w_proj, b_proj, parts=parts)
    if "LN" in parts:
        emit_ln(p, x_fm, "ln2")
    sST = emit_store_x(p, x_fm, y)
    sST.wait(p.sy)
    return p.finish()



def emit_l1_ffn(p, x_fm, w_gate, w_up, w_down, hT_dram, parts="AB"):
    sW = p.sem("f1W")
    sA = p.sem("f1A")
    sP = p.sem("f1P")
    sV = p.sem("f1V")
    sG = p.sem("f1G")

    if "A" in parts:
      with ExitStack() as st_:
        wg_sl = p.sbuf("f1_wg", [128, NCH * 512], BF16, stack=st_)
        wu_sl = p.sbuf("f1_wu", [128, NCH * 512], BF16, stack=st_)
        xb = [p.sbuf(f"f1_xb{i}", [128, NCH * 128], BF16, stack=st_) for i in range(2)]
        sil = [p.sbuf(f"f1_sil{i}", [128, 512], F32, stack=st_) for i in range(2)]
        h = [p.sbuf(f"f1_h{i}", [128, 512], F32, stack=st_) for i in range(2)]
        hm = [p.sbuf(f"f1_hm{i}", [128, 512], BF16, stack=st_) for i in range(2)]
        hTt = [p.sbuf(f"f1_hTt{i}", [128, 512], BF16, stack=st_) for i in range(2)]
        a4 = p.sbuf("f1_a4", [128, 512], F32, stack=st_)
        cmp6 = p.sbuf("f1_cmp", [128, 6 * 128], F32, stack=st_)
        sacc = p.sbuf("f1_sacc", [128, 128], F32, stack=st_)
        keep = p.sbuf("f1_keep", [128, 4 * 128], F32, stack=st_)

        xbtok = [0, 0]
        banktok = [0, 0]
        sparstok = [0, 0]
        dmatok = [0, 0]
        t3tok = [0, 0]
        s3evac = [0]
        wtok = 0

        def stage3(t):
            pt = t % 2
            tb = (t % 16) * 128
            hs = t // 16
            sV.wait(p.t, sparstok[pt])
            sV.wait(p.t, s3evac[0])
            for j in range(4):
                parb = j % 2
                if j >= 2:
                    sV.wait(p.t, sV.n - 1)
                tp = p.t.transpose(p.bankb(6 + parb)[:, 0:128],
                                   hm[pt][:, j * 128:(j + 1) * 128], p.identb[:])
                tpt = sP.inc(tp)
                sP.wait(p.v, tpt)
                if j == 0:
                    sG.wait(p.v, dmatok[pt])
                sV.inc(p.v.tensor_copy(hTt[pt][:, j * 128:(j + 1) * 128], p.bankb(6 + parb)[:, 0:128]))
            tv = sV.n
            s3evac[0] = tv
            sV.wait(p.g, tv)
            dmatok[pt] = sG.inc(p.g.dma_start(
                out=hT_dram[4 * hs: 4 * hs + 4, :, tb: tb + 128].rearrange("j p t -> p j t"),
                in_=hTt[pt][:].rearrange("p (j t) -> p j t", j=4)), 16)
            t3tok[pt] = sP.n

        def emit_casts(t):
            pt = t % 2
            hs_, t128 = t // 16, t % 16
            tb = t128 * 128
            sP.wait(p.sc, xbtok[pt])
            ta = 0
            for c in range(NCH):
                ta = sA.inc(p.sc.activation(out=xb[pt][:, c * 128:(c + 1) * 128],
                                            in_=x_fm[:, c * S + tb: c * S + tb + 128],
                                            func=AF.Copy))
            return ta

        casttok_a = emit_casts(0)
        tglob = 0
        for hs in range(16):
            sP.wait(p.g, wtok)
            sW.inc(p.g.dma_start(
                out=wg_sl[:].rearrange("p (c i) -> p c i", c=NCH),
                in_=w_gate.rearrange("(c p) i -> p c i", p=128)[:, :, hs * 512:(hs + 1) * 512]), 16)
            tokw = sW.inc(p.g.dma_start(
                out=wu_sl[:].rearrange("p (c i) -> p c i", c=NCH),
                in_=w_up.rearrange("(c p) i -> p c i", p=128)[:, :, hs * 512:(hs + 1) * 512]), 16)
            sW.wait(p.t, tokw)
            for t128 in range(16):
                t = tglob
                pt = t % 2
                sA.wait(p.t, casttok_a)
                bg, bu = pt * 2, pt * 2 + 1
                sV.wait(p.t, banktok[pt])
                for c in range(NCH):
                    p.t.matmul(p.bank(bg), xb[pt][:, c * 128:(c + 1) * 128], wg_sl[:, c * 512:(c + 1) * 512],
                               start=(c == 0), stop=(c == NCH - 1))
                    mm = p.t.matmul(p.bank(bu), xb[pt][:, c * 128:(c + 1) * 128], wu_sl[:, c * 512:(c + 1) * 512],
                                    start=(c == 0), stop=(c == NCH - 1))
                tp = sP.inc(mm)
                xbtok[pt] = tp
                if t128 == 15:
                    wtok = tp
                if t + 1 < 256:
                    casttok_a = emit_casts(t + 1)
                sP.wait(p.sc, tp)
                sV.wait(p.sc, sparstok[pt])
                ta = sA.inc(p.sc.activation(out=sil[pt][:], in_=p.bank(bg), func=AF.Silu))
                sA.wait(p.v, ta)
                tv = sV.inc(p.v.tensor_tensor(h[pt][:], p.bank(bu), sil[pt][:], ALU.mult))
                banktok[pt] = tv
                sV.wait(p.sc, tv)
                ta = sA.inc(p.sc.activation(out=a4[:], in_=h[pt][:], func=AF.Abs))
                sA.wait(p.v, ta)
                v = p.v
                def a_(i):
                    return a4[:].rearrange("p (g f) -> p g f", f=4)[:, :, i:i + 1].rearrange("p g o -> p (g o)")
                def hs_(i):
                    return h[pt][:].rearrange("p (g f) -> p g f", f=4)[:, :, i:i + 1].rearrange("p g o -> p (g o)")
                def hms_(i):
                    return hm[pt][:].rearrange("p (g f) -> p g f", f=4)[:, :, i:i + 1].rearrange("p g o -> p (g o)")
                pair = [(0, 1), (0, 2), (0, 3), (1, 2), (1, 3), (2, 3)]
                for n, (i, j) in enumerate(pair):
                    v.tensor_tensor(cmp6[:, n * 128:(n + 1) * 128], a_(i), a_(j), ALU.is_ge)
                c01, c02, c03, c12, c13, c23 = [cmp6[:, n * 128:(n + 1) * 128] for n in range(6)]
                v.tensor_tensor(sacc[:], c01, c02, ALU.add)
                v.tensor_tensor(sacc[:], sacc[:], c03, ALU.add)
                v.tensor_single_scalar(keep[:, 0:128], sacc[:], 1.5, ALU.is_ge)
                v.tensor_tensor(sacc[:], c12, c13, ALU.add)
                v.tensor_tensor(sacc[:], sacc[:], c01, ALU.subtract)
                v.tensor_single_scalar(keep[:, 128:256], sacc[:], 0.5, ALU.is_ge)
                v.tensor_tensor(sacc[:], c02, c12, ALU.add)
                v.tensor_tensor(sacc[:], c23, sacc[:], ALU.subtract)
                v.tensor_single_scalar(keep[:, 256:384], sacc[:], -0.5, ALU.is_ge)
                v.tensor_tensor(sacc[:], c03, c13, ALU.add)
                v.tensor_tensor(sacc[:], sacc[:], c23, ALU.add)
                v.tensor_single_scalar(sacc[:], sacc[:], 1.5, ALU.is_ge)
                v.tensor_scalar(keep[:, 384:512], sacc[:], -1.0, 1.0, ALU.mult, ALU.add)
                sP.wait(p.v, t3tok[pt])
                for i in range(4):
                    mmv = v.tensor_tensor(hms_(i), hs_(i), keep[:, i * 128:(i + 1) * 128], ALU.mult)
                sparstok[pt] = sV.inc(mmv)
                if t >= 1:
                    stage3(t - 1)
                tglob += 1
        stage3(tglob - 1)
      p.barrier()

    if "B" in parts:
      with ExitStack() as st_:
        wds = [p.sbuf(f"f1_wds{i}", [128, E], BF16, stack=st_) for i in range(8)]
        gh = [p.sbuf(f"f1_gh{i}", [128, S], BF16, stack=st_) for i in range(8)]
        slot_tok = [0] * 8
        half_drain = [0, 0]
        xv = x_fm[:].rearrange("p (ch t) -> p ch t", ch=NCH)
        for hg in range(16):
            hf = hg % 2
            slots = [hf * 4 + i for i in range(4)]
            for k, sl in enumerate(slots):
                hh = hg * 4 + k
                sP.wait(p.g, slot_tok[sl])
                sW.inc(p.g.dma_start(out=wds[sl][:].rearrange("p (c d) -> p c d", c=NCH),
                                     in_=w_down[hh * 128:(hh + 1) * 128, :].rearrange("r (c d) -> r c d", d=128)), 16)
                sG.inc(p.g.dma_start(out=gh[sl][:], in_=hT_dram[hh, :, :]), 16)
            sW.wait(p.t)
            sG.wait(p.t)
            for win in range(16):
                dq, tsl = win // 4, win % 4
                tb = tsl * 512
                bh = (win % 2) * 4
                sV.wait(p.t, half_drain[win % 2])
                for k, sl in enumerate(slots):
                    for dd in range(4):
                        D = dq * 4 + dd
                        mm = p.t.matmul(p.bank(bh + dd), wds[sl][:, D * 128:(D + 1) * 128],
                                        gh[sl][:, tb: tb + 512],
                                        start=(k == 0), stop=(k == 3))
                tok_mm = sP.inc(mm)
                sP.wait(p.v, tok_mm)
                half_drain[win % 2] = sV.inc(p.v.tensor_tensor(
                    xv[:, dq * 4:(dq + 1) * 4, tb: tb + 512],
                    xv[:, dq * 4:(dq + 1) * 4, tb: tb + 512],
                    p._arena[:, bh * 512:(bh + 4) * 512].rearrange("p (b t) -> p b t", b=4),
                    ALU.add))
            for sl in slots:
                slot_tok[sl] = sP.n
      p.barrier()


def build_l1_ffn(parts="ABLN"):
    import os
    p = Prog()
    nc = p.nc
    x = nc.dram_tensor("x", [S, E], F32, kind="ExternalInput").ap()
    w_gate = nc.dram_tensor("w_gate", [E, HID], F32, kind="ExternalInput").ap()
    w_up = nc.dram_tensor("w_up", [E, HID], F32, kind="ExternalInput").ap()
    w_down = nc.dram_tensor("w_down", [HID, E], F32, kind="ExternalInput").ap()
    y = nc.dram_tensor("y", [S, E], F32, kind="ExternalOutput").ap()
    hkind = "ExternalOutput" if os.environ.get("DBG") else "Internal"
    hT_dram = nc.dram_tensor("hT_scratch", [64, 128, S], BF16, kind=hkind).ap()

    x_fm = p.sbuf("x_fm", [128, NCH * S], F32)
    p.setup_consts()
    emit_load_x(p, x, x_fm)
    emit_l1_ffn(p, x_fm, w_gate, w_up, w_down, hT_dram, parts=parts)
    if "LN" in parts:
        emit_ln(p, x_fm, "ln3")
    sST = emit_store_x(p, x_fm, y)
    sST.wait(p.sy)
    return p.finish()


def build_l0_attn(parts="ABLN"):
    p = Prog()
    nc = p.nc
    x = nc.dram_tensor("x", [S, E], F32, kind="ExternalInput").ap()
    Wq = nc.dram_tensor("Wq", [16, E, 128], F32, kind="ExternalInput").ap()
    bq = nc.dram_tensor("bq", [16, 128], F32, kind="ExternalInput").ap()
    Wk = nc.dram_tensor("Wk", [4, E, 128], F32, kind="ExternalInput").ap()
    bk = nc.dram_tensor("bk", [4, 128], F32, kind="ExternalInput").ap()
    Wv = nc.dram_tensor("Wv", [4, E, 128], F32, kind="ExternalInput").ap()
    bv = nc.dram_tensor("bv", [4, 128], F32, kind="ExternalInput").ap()
    y = nc.dram_tensor("y", [S, E], F32, kind="ExternalOutput").ap()

    x_fm = p.sbuf("x_fm", [128, NCH * S], F32)
    p.setup_consts()
    emit_load_x(p, x, x_fm)
    if "A" in parts:
        import os
        nh = int(os.environ.get("NHEADS", "16"))
        dbg = None
        if os.environ.get("DBG"):
            dbg = {
                "q": nc.dram_tensor("dbg_q", [128, 2 * S], BF16, kind="ExternalOutput").ap(),
                "k": nc.dram_tensor("dbg_k", [128, 4 * S], BF16, kind="ExternalOutput").ap(),
                "v": nc.dram_tensor("dbg_v", [128, 4 * S], BF16, kind="ExternalOutput").ap(),
                "e": nc.dram_tensor("dbg_e", [128, S], BF16, kind="ExternalOutput").ap(),
                "z": nc.dram_tensor("dbg_z", [128, 1], F32, kind="ExternalOutput").ap(),
                "rzbc": nc.dram_tensor("dbg_rzbc", [128, 128], BF16, kind="ExternalOutput").ap(),
            }
            if os.environ.get("DBG_QI") is not None:
                dbg["_h"] = int(os.environ.get("DBG_H", str(nh - 1)))
                dbg["_qi"] = int(os.environ["DBG_QI"])
                dbg["e0"] = nc.dram_tensor("dbg_e0", [128, S], BF16, kind="ExternalOutput").ap()
                dbg["z0"] = nc.dram_tensor("dbg_z0", [128, 1], F32, kind="ExternalOutput").ap()
                dbg["r0"] = nc.dram_tensor("dbg_r0", [128, 128], BF16, kind="ExternalOutput").ap()
                dbg["o0"] = nc.dram_tensor("dbg_o0", [128, 128], F32, kind="ExternalOutput").ap()
                dbg["_eb"] = p.sbuf("dbg_eb", [128, S], BF16)
                dbg["_zb"] = p.sbuf("dbg_zb", [128, 1], F32)
                dbg["_rb"] = p.sbuf("dbg_rb", [128, 128], BF16)
                dbg["_ob"] = p.sbuf("dbg_ob", [128, 128], F32)
                dbg["s0"] = nc.dram_tensor("dbg_s0", [128, 128], F32, kind="ExternalOutput").ap()
                dbg["q0"] = nc.dram_tensor("dbg_q0", [128, 128], BF16, kind="ExternalOutput").ap()
                dbg["_sb"] = p.sbuf("dbg_sb", [128, 128], F32)
                dbg["_qb"] = p.sbuf("dbg_qb", [128, 128], BF16)
        O_dram = nc.dram_tensor("O_scratch", [128, NCH * S], F32, kind="Internal").ap()
        emit_l0_attn(p, x_fm, Wq, bq, Wk, bk, Wv, bv, O_dram=O_dram, parts=parts, nheads=nh, dbg=dbg)
    if "LN" in parts:
        emit_ln(p, x_fm, "ln0")
    sST = emit_store_x(p, x_fm, y)
    sST.wait(p.sy)
    return p.finish()


def build_fused():
    p = Prog()
    nc = p.nc
    x = nc.dram_tensor("x", [S, E], F32, kind="ExternalInput").ap()
    Wq = nc.dram_tensor("Wq", [16, E, 128], F32, kind="ExternalInput").ap()
    bq = nc.dram_tensor("bq", [16, 128], F32, kind="ExternalInput").ap()
    Wk = nc.dram_tensor("Wk", [4, E, 128], F32, kind="ExternalInput").ap()
    bk = nc.dram_tensor("bk", [4, 128], F32, kind="ExternalInput").ap()
    Wv = nc.dram_tensor("Wv", [4, E, 128], F32, kind="ExternalInput").ap()
    bv = nc.dram_tensor("bv", [4, 128], F32, kind="ExternalInput").ap()
    w_gate0 = nc.dram_tensor("w_gate0", [HID, E], F32, kind="ExternalInput").ap()
    w_up0 = nc.dram_tensor("w_up0", [HID, E], F32, kind="ExternalInput").ap()
    w_down0 = nc.dram_tensor("w_down0", [E, HID], F32, kind="ExternalInput").ap()
    w_qkv = nc.dram_tensor("w_qkv", [3 * E, E], F32, kind="ExternalInput").ap()
    w_proj = nc.dram_tensor("w_proj", [E, E], F32, kind="ExternalInput").ap()
    b_proj = nc.dram_tensor("b_proj", [E], F32, kind="ExternalInput").ap()
    w_gate1 = nc.dram_tensor("w_gate1", [E, HID], F32, kind="ExternalInput").ap()
    w_up1 = nc.dram_tensor("w_up1", [E, HID], F32, kind="ExternalInput").ap()
    w_down1 = nc.dram_tensor("w_down1", [HID, E], F32, kind="ExternalInput").ap()
    y = nc.dram_tensor("y", [S, E], F32, kind="ExternalOutput").ap()
    O_dram = nc.dram_tensor("O_scratch", [128, NCH * S], F32, kind="Internal").ap()
    g_dram = nc.dram_tensor("g_scratch", [64, 128, S], BF16, kind="Internal").ap()
    hT_dram = nc.dram_tensor("hT_scratch", [64, 128, S], BF16, kind="Internal").ap()

    x_fm = p.sbuf("x_fm", [128, NCH * S], F32)
    p.setup_consts()
    with nc.named_scope("loadx"):
        emit_load_x(p, x, x_fm)
    with nc.named_scope("l0attn"):
        emit_l0_attn(p, x_fm, Wq, bq, Wk, bk, Wv, bv, O_dram=O_dram)
    with nc.named_scope("ln0"):
        emit_ln(p, x_fm, "ln0")
    with nc.named_scope("l0ffn"):
        emit_l0_ffn(p, x_fm, w_gate0, w_up0, w_down0, g_dram)
    with nc.named_scope("ln1"):
        emit_ln(p, x_fm, "ln1")
    with nc.named_scope("l1attn"):
        emit_l1_attn(p, x_fm, w_qkv, w_proj, b_proj)
    with nc.named_scope("ln2"):
        emit_ln(p, x_fm, "ln2")
    with nc.named_scope("l1ffn"):
        emit_l1_ffn(p, x_fm, w_gate1, w_up1, w_down1, hT_dram)
    with nc.named_scope("ln3"):
        emit_ln(p, x_fm, "ln3")
    with nc.named_scope("storex"):
        sST = emit_store_x(p, x_fm, y)
    sST.wait(p.sy)
    return p.finish()


def kernel(**inputs):
    g = lambda k: np.ascontiguousarray(inputs[k])
    x = g("L0_attn_x")
    n = 8
    nc = build_fused()
    maps = [{"x": np.ascontiguousarray(x[b]),
             "Wq": g("L0_attn_Wq"), "bq": g("L0_attn_bq"),
             "Wk": g("L0_attn_Wk"), "bk": g("L0_attn_bk"),
             "Wv": g("L0_attn_Wv"), "bv": g("L0_attn_bv"),
             "w_gate0": g("L0_ffn_w_gate"), "w_up0": g("L0_ffn_w_up"),
             "w_down0": g("L0_ffn_w_down"),
             "w_qkv": g("L1_attn_w_qkv"), "w_proj": g("L1_attn_w_proj"),
             "b_proj": g("L1_attn_b_proj"),
             "w_gate1": g("L1_ffn_w_gate"), "w_up1": g("L1_ffn_w_up"),
             "w_down1": g("L1_ffn_w_down")} for b in range(n)]
    r = run_bass_kernel_spmd(nc, maps, core_ids=list(range(n)))
    return np.stack([np.asarray(r.results[b]["y"]) for b in range(n)], axis=0)


def kernel_unfused(**inputs):
    g = lambda k: np.ascontiguousarray(inputs[k])
    x = g("L0_attn_x")
    n = 8
    cores = list(range(n))

    nc1 = build_l0_attn()
    maps1 = [{"x": np.ascontiguousarray(x[b]), "Wq": g("L0_attn_Wq"), "bq": g("L0_attn_bq"),
              "Wk": g("L0_attn_Wk"), "bk": g("L0_attn_bk"),
              "Wv": g("L0_attn_Wv"), "bv": g("L0_attn_bv")} for b in range(n)]
    r1 = run_bass_kernel_spmd(nc1, maps1, core_ids=cores)
    x1 = [r1.results[b]["y"] for b in range(n)]

    nc2 = build_l0_ffn()
    maps2 = [{"x": x1[b], "w_gate": g("L0_ffn_w_gate"), "w_up": g("L0_ffn_w_up"),
              "w_down": g("L0_ffn_w_down")} for b in range(n)]
    r2 = run_bass_kernel_spmd(nc2, maps2, core_ids=cores)
    x2 = [r2.results[b]["y"] for b in range(n)]

    nc3 = build_l1_attn()
    maps3 = [{"x": x2[b], "w_qkv": g("L1_attn_w_qkv"), "w_proj": g("L1_attn_w_proj"),
              "b_proj": g("L1_attn_b_proj")} for b in range(n)]
    r3 = run_bass_kernel_spmd(nc3, maps3, core_ids=cores)
    x3 = [r3.results[b]["y"] for b in range(n)]

    nc4 = build_l1_ffn()
    maps4 = [{"x": x3[b], "w_gate": g("L1_ffn_w_gate"), "w_up": g("L1_ffn_w_up"),
              "w_down": g("L1_ffn_w_down")} for b in range(n)]
    r4 = run_bass_kernel_spmd(nc4, maps4, core_ids=cores)
    return np.stack([np.asarray(r4.results[b]["y"]) for b in range(n)], axis=0)
